# version 15
# baseline (speedup 1.0000x reference)
import math
import numpy as np
import ml_dtypes
from contextlib import ExitStack

import concourse.bass as bass
import concourse.mybir as mybir
from concourse.bass_utils import run_bass_kernel_spmd

F32 = mybir.dt.float32
BF16 = mybir.dt.bfloat16
AF = mybir.ActivationFunctionType
ALU = mybir.AluOpType
AX = mybir.AxisListType.X

ENGINES = ("tensor", "vector", "scalar", "gpsimd", "sync")


class _Sem:
    def __init__(self, nc, name):
        self._cm = nc.semaphore(name)
        self.h = self._cm.__enter__()
        self.n = 0
        self.name = name


class _Prog:

    def __init__(self, nc):
        self.nc = nc
        self.ops = {e: [] for e in ENGINES}

    def sem(self, name):
        return _Sem(self.nc, name)

    def emit(self, engine, fn, wait=(), inc=()):
        wv = []
        for w in wait:
            if w is None:
                continue
            if isinstance(w, _Sem):
                wv.append((w, w.n))
            else:
                s, v = w
                wv.append((s, s.n if v is None else v))
        iv = []
        for s, k in inc:
            s.n += k
            iv.append((s, k))
        self.ops[engine].append((wv, fn, iv))

    def fence(self, marks):
        for e in ENGINES:
            self.emit(e, None, wait=marks)

    def build(self, block):
        def mk(engine):
            def body(eng):
                for wv, fn, iv in self.ops[engine]:
                    for s, v in wv:
                        if v > 0:
                            eng.wait_ge(s.h, v)
                    if fn is not None:
                        instr = fn(eng)
                        for s, k in iv:
                            instr = instr.then_inc(s.h, k)
            return body

        for engine in ENGINES:
            if self.ops[engine]:
                getattr(block, engine)(mk(engine))


class _Cfg:
    def __init__(self, D=1024, H=8, E=512, NRE=16, NSE=2, TOPK=4, KVT=64,
                 NB=4, eps=1e-5, G=4):
        self.D, self.H, self.E = D, H, E
        self.NRE, self.NSE, self.TOPK = NRE, NSE, TOPK
        self.NE = NRE + NSE
        self.KVT, self.NB = KVT, NB
        self.eps = eps
        self.HD = 128
        assert D == H * self.HD
        self.DC = D // 128
        self.ECN = E // 128
        self.TKV = KVT * 128
        self.NS = KVT // 2
        self.NQ = self.NS * 128
        self.NG = self.NS // 4
        assert self.NS % 4 == 0
        self.TTOT = self.TKV + self.NQ
        assert self.TTOT % 512 == 0
        self.NCH = self.TTOT // 512
        self.KVCH = self.TKV // 512
        self.QCH = self.NQ // 512
        self.NTC3 = self.NQ // 512
        self.NTC4 = self.NQ // 256
        self.NCORES = 2 * NB
        assert H % 2 == 0
        self.NPAIR = H // 2
        ids = list(range(self.NE))
        self.passes = [ids[i:i + G] for i in range(0, self.NE, G)]
        self.scale = 1.0 / math.sqrt(self.HD)


def _dirs(n):
    t = np.array([[(i >> 2 & 1) * 2 - 1, (i >> 1 & 1) * 2 - 1, (i & 1) * 2 - 1]
                  for i in range(8)], dtype=np.float32)
    return (t / np.sqrt(3.0, dtype=np.float32))[:n]


def _householders(cfg):
    H, HD, s = cfg.H, cfg.HD, cfg.scale
    d = _dirs(H)
    hhq = np.zeros((H, HD, HD), dtype=np.float32)
    hhk = np.zeros((H, HD, HD), dtype=np.float32)
    for h in range(H):
        u = np.zeros(HD, dtype=np.float64)
        u[:3] = d[h].astype(np.float64)
        v = u.copy()
        v[0] -= 1.0
        Hm = np.eye(HD) - 2.0 * np.outer(v, v) / (v @ v)
        hhq[h] = Hm.astype(np.float32)
        Hk = Hm.copy()
        Hk[:, 1:] *= s
        hhk[h] = Hk.astype(np.float32)
    return hhq, hhk


def _qrows(cfg, p):
    return np.concatenate(
        [np.arange(128 * (2 * i + p), 128 * (2 * i + p) + 128)
         for i in range(cfg.NS)])


def _mask01(p):
    k = np.arange(128)[:, None]
    q = np.arange(128)[None, :]
    tri = (k <= q).astype(np.float32)
    if p == 1:
        a, b = np.ones((128, 128), np.float32), tri
    else:
        a, b = tri, np.zeros((128, 128), np.float32)
    return np.ascontiguousarray(np.concatenate([a, b], axis=1))


def _active(cfg, g, kt):
    act = 0
    for j in range(4):
        if kt < 8 * g + 2 * j + 2:
            act += 1
    return act


def _build(cfg):
    nc = bass.Bass("TRN2", target_bir_lowering=False)
    D, DC, H, HD, E, ECN = cfg.D, cfg.DC, cfg.H, cfg.HD, cfg.E, cfg.ECN

    def din(name, shape, dt=F32):
        return nc.dram_tensor(name, shape, dt, kind="ExternalInput").ap()

    xT = din("xT", [D, cfg.TTOT])
    wqkT = din("wqkT", [2 * D, D])
    wv_d = din("wv", [D, D])
    wout_d = din("wout", [D, D])
    hhq_d = din("hhq", [H * HD, HD])
    hhk_d = din("hhk", [H * HD, HD])
    hs_d = din("hs", [1, H])
    wmod_d = din("wmod", [D, 6 * D])
    cfm_d = din("cfm", [128, DC])
    wsu_d = din("wsu", [cfg.NSE * D, E])
    wsd_d = din("wsd", [cfg.NSE * E, D])
    wru_d = din("wru", [cfg.NRE * D, E])
    wrd_d = din("wrd", [cfg.NRE * E, D])
    wrt_d = din("wrt", [D, cfg.NRE])
    mask_d = din("mask01", [128, 256])
    iden_d = din("identin", [128, 128])

    outT = nc.dram_tensor("outT", [D, cfg.NQ], F32, kind="ExternalOutput").ap()
    h1T = nc.dram_tensor("h1T", [D, cfg.TTOT], BF16, kind="Internal").ap()
    h2T = nc.dram_tensor("h2T", [D, cfg.NQ], BF16, kind="Internal").ap()
    x1T = nc.dram_tensor("x1T", [D, cfg.NQ], F32, kind="Internal").ap()
    atT = nc.dram_tensor("atT", [D, cfg.NQ], BF16, kind="Internal").ap()
    gate_hbm = nc.dram_tensor("gate_hbm", [cfg.NRE, cfg.NQ], BF16,
                              kind="Internal").ap()
    accT = nc.dram_tensor("accT", [D, cfg.NQ], BF16, kind="Internal").ap()
    NEX = cfg.NRE + cfg.NSE
    wub_h = nc.dram_tensor("wub", [NEX * D, E], BF16, kind="Internal").ap()
    wdb_h = nc.dram_tensor("wdb", [NEX * E, D], BF16, kind="Internal").ap()

    def fmr(ap_):
        return ap_.rearrange("(c p) t -> p c t", p=128)

    ctx = ExitStack()

    def sb(name, shape, dt=F32):
        return ctx.enter_context(nc.sbuf_tensor(name, shape, dt))

    p = _Prog(nc)
    LDW_ = p.sem("ldw")
    LDC = p.sem("ldc")
    PE = p.sem("pe")
    VE = p.sem("ve")
    SE = p.sem("se")
    GP = p.sem("gp")
    STH1 = [p.sem(f"sth1_{i}") for i in range(2)]
    STA = [p.sem(f"sta_{i}") for i in range(2)]
    STH2 = [p.sem(f"sth2_{i}") for i in range(2)]
    STX1 = [p.sem(f"stx1_{i}") for i in range(2)]
    STO = [p.sem(f"sto_{i}") for i in range(2)]
    LX = [p.sem(f"lx_{i}") for i in range(2)]
    LXB = [p.sem(f"lxb_{i}") for i in range(2)]
    LH1 = [p.sem(f"lh1_{i}") for i in range(2)]
    LAT = [p.sem(f"lat_{i}") for i in range(2)]
    LH2 = [p.sem(f"lh2_{i}") for i in range(2)]
    LX1 = [p.sem(f"lx1_{i}") for i in range(2)]
    LGR = p.sem("lgr")
    LFW = [p.sem(f"lfw_{i}") for i in range(8)]
    LGA = [p.sem(f"lga_{i}") for i in range(2)]
    LAC = [p.sem(f"lac_{i}") for i in range(2)]
    SAC = [p.sem(f"sac_{i}") for i in range(2)]

    onescol = sb("onescol", [128, 1], BF16)
    ones128 = sb("ones128", [128, 128], BF16)
    epscol = sb("epscol", [128, 1], F32)
    neghalf = sb("neghalf", [128, 1], F32)
    negone = sb("negone", [128, 1], F32)
    onesrow = sb("onesrow", [1, 128], F32)
    onesrow_bf = sb("onesrow_bf", [1, 128], BF16)
    one1 = sb("one1", [1, 1], F32)
    ident = sb("ident", [128, 128], F32)
    maskb = sb("maskb", [128, 256], BF16)
    modfm = sb("modfm", [128, 6, DC], F32)
    hsc = sb("hsc", [128, H], F32)
    ctx_w3 = ExitStack()
    sbw3 = lambda n, s_, d=F32: ctx_w3.enter_context(
        nc.sbuf_tensor(n, s_, d, side="right"))
    woutb = sbw3("woutb", [128, DC, D], BF16)
    wrtb = sbw3("wrtb", [128, DC, cfg.NRE], BF16)
    ctx_w2 = ExitStack()
    sbw2 = lambda n, s_, d=F32: ctx_w2.enter_context(
        nc.sbuf_tensor(n, s_, d, side="right"))
    wqrot = sbw2("wqrot", [128, H, DC, 128], BF16)
    wkrot = sbw2("wkrot", [128, H, DC, 128], BF16)
    wvb = sbw2("wvb", [128, DC, D], BF16)

    def p0():
        p.emit("vector", lambda v: v.memset(onescol[:], 1.0), inc=[(VE, 1)])
        p.emit("vector", lambda v: v.memset(ones128[:], 1.0), inc=[(VE, 1)])
        p.emit("vector", lambda v: v.memset(epscol[:], float(cfg.eps)),
               inc=[(VE, 1)])
        p.emit("vector", lambda v: v.memset(neghalf[:], -0.5), inc=[(VE, 1)])
        p.emit("vector", lambda v: v.memset(negone[:], -1.0), inc=[(VE, 1)])
        p.emit("vector", lambda v: v.memset(onesrow[:], 1.0), inc=[(VE, 1)])
        p.emit("vector", lambda v: v.memset(one1[:], 1.0), inc=[(VE, 1)])
        p.emit("vector", lambda v: v.memset(onesrow_bf[:], 1.0), inc=[(VE, 1)])
        const_ve = VE.n
        p.emit("sync", lambda g: g.dma_start(out=ident[:], in_=iden_d[:]),
               inc=[(LDW_, 16)])
        p.emit("gpsimd", lambda g: g.dma_start(out=maskb[:], in_=mask_d[:]),
               inc=[(LDC, 16)])
        p.emit("gpsimd", lambda g: g.dma_start(
            out=wrtb[:], in_=wrt_d.rearrange("(c p) r -> p c r", p=128)),
            inc=[(LDC, 16)])
        p.emit("gpsimd", lambda g: g.dma_start(out=wvb[:], in_=fmr(wv_d)),
               inc=[(LDC, 16)])
        p.emit("gpsimd", lambda g: g.dma_start(out=woutb[:], in_=fmr(wout_d)),
               inc=[(LDC, 16)])

        es = ExitStack()
        sbl = lambda n, s_, d=F32: es.enter_context(nc.sbuf_tensor(n, s_, d))
        psl = lambda n, s_, d=F32: es.enter_context(nc.psum_tensor(n, s_, d))
        cfm = sbl("p0_cfm", [128, DC])
        mod_row = sbl("p0_mod_row", [1, 6 * D])
        csg = sbl("p0_csg", [128, DC])
        scb = sbl("p0_scb", [128, DC], BF16)
        hs_sb = sbl("p0_hs", [1, H])
        nmodw = 4 if (6 * D // 512) % 4 == 0 else 1
        wmod_sb = sbl("p0_wmod", [128, DC, (6 * D) // nmodw], BF16)
        hhq_sb = sbl("p0_hhq", [128, H, HD])
        hhk_sb = sbl("p0_hhk", [128, H, HD])
        hhkb_sb = sbl("p0_hhkb", [128, H, HD], BF16)
        hhqb_sb = sbl("p0_hhqb", [128, H, HD], BF16)
        wqkT_sb = sbl("p0_wqkT", [128, 2 * H, D], BF16)
        gsc = sbl("p0_gsc", [128, DC])
        ps_mod = [psl(f"p0_ps_mod{i}", [1, 512]) for i in range(2)]
        ps_fm = psl("p0_ps_fm", [128, 64])
        ps_hs = psl("p0_ps_hs", [128, H])
        ps_rot = [psl(f"p0_ps_rot{i}", [128, 512]) for i in range(2)]

        p.emit("sync", lambda g: g.dma_start(out=cfm[:], in_=cfm_d[:]),
               inc=[(LDW_, 16)])
        p.emit("sync", lambda g: g.dma_start(out=hs_sb[:], in_=hs_d[:]),
               inc=[(LDW_, 16)])
        p.emit("sync", lambda g: g.dma_start(
            out=hhq_sb[:], in_=hhq_d.rearrange("(h p) c -> p h c", p=128)),
            inc=[(LDW_, 16)])
        p.emit("sync", lambda g: g.dma_start(
            out=hhk_sb[:], in_=hhk_d.rearrange("(h p) c -> p h c", p=128)),
            inc=[(LDW_, 16)])
        p.emit("gpsimd", lambda g: g.dma_start(
            out=wqkT_sb[:], in_=wqkT.rearrange("(c p) d -> p c d", p=128)),
            inc=[(LDC, 16)])
        cfm_ld = hs_ld = hh_ld = LDW_.n
        mask_ld = wrt_ld = wv_ld = wout_ld = wqk_ld = LDC.n
        WMD = p.sem("wmd")

        p.emit("scalar", lambda s: s.activation(csg[:], cfm[:], AF.Sigmoid),
               wait=[(LDW_, cfm_ld)], inc=[(SE, 1)])
        p.emit("vector", lambda v: v.tensor_mul(csg[:], csg[:], cfm[:]),
               wait=[SE, (VE, None)], inc=[(VE, 1)])
        p.emit("vector", lambda v: v.tensor_copy(scb[:], csg[:]),
               wait=[(VE, None)], inc=[(VE, 1)])
        scb_ve = VE.n

        p.emit("tensor", lambda t: t.matmul(ps_hs[:], onesrow[:], hs_sb[:],
                                            start=True, stop=True),
               wait=[(LDW_, hs_ld), (VE, const_ve)], inc=[(PE, 1)])
        p.emit("vector", lambda v: v.tensor_scalar(
            hsc[:], ps_hs[:], cfg.scale, None, ALU.add),
            wait=[PE, (VE, None)], inc=[(VE, 1)])
        hsc_ve = VE.n
        for h in range(H):
            p.emit("vector", lambda v, h=h: v.tensor_tensor(
                hhk_sb[:, h, 0:1], hhk_sb[:, h, 0:1], hsc[:, h:h + 1],
                ALU.mult),
                wait=[(VE, hsc_ve), (LDW_, hh_ld), (VE, None)], inc=[(VE, 1)])
        p.emit("vector", lambda v: v.tensor_copy(hhkb_sb[:], hhk_sb[:]),
               wait=[(VE, None)], inc=[(VE, 1)])
        p.emit("vector", lambda v: v.tensor_copy(hhqb_sb[:], hhq_sb[:]),
               wait=[(LDW_, hh_ld), (VE, None)], inc=[(VE, 1)])
        hhk_ve = VE.n

        nmodch = 6 * D // 512
        wavech = nmodch // nmodw
        evac_ve = {}
        wmodf = wmod_d.rearrange("(c p) f -> p c f", p=128)
        for wv_ in range(nmodw):
            wlo = wv_ * (6 * D // nmodw)
            p.emit("gpsimd", lambda g, wlo=wlo: g.dma_start(
                out=wmod_sb[:], in_=wmodf[:, :, wlo:wlo + (6 * D) // nmodw]),
                wait=([(PE, None)] if wv_ > 0 else []),
                inc=[(WMD, 16)])
            wmod_ld = WMD.n
            for fi in range(wavech):
                f = wv_ * wavech + fi
                par = f % 2
                w0 = [(VE, scb_ve), (WMD, wmod_ld)]
                if f >= 2:
                    w0.append((VE, evac_ve[f - 2]))
                for dc in range(DC):
                    p.emit("tensor", lambda t, dc=dc, fi=fi, par=par: t.matmul(
                        ps_mod[par][:], scb[:, dc:dc + 1],
                        wmod_sb[:, dc, fi * 512:(fi + 1) * 512],
                        start=(dc == 0), stop=(dc == DC - 1)),
                        wait=(w0 if dc == 0 else []),
                        inc=([(PE, 1)] if dc == DC - 1 else []))
                p.emit("vector", lambda v, f=f, par=par: v.tensor_copy(
                    mod_row[:, f * 512:(f + 1) * 512], ps_mod[par][:]),
                    wait=[PE, (VE, None)], inc=[(VE, 1)])
                evac_ve[f] = VE.n
        modrow_ve = VE.n

        n6 = 6 * DC
        for j in range(n6):
            p.emit("tensor", lambda t, j=j: t.matmul(
                ps_fm[:, j:j + 1], mod_row[:, j * 128:(j + 1) * 128], one1[:],
                start=True, stop=True),
                wait=([(VE, modrow_ve), (VE, const_ve)] if j == 0 else []),
                inc=[(PE, 1)])
        p.emit("scalar", lambda s: s.copy(
            modfm[:].rearrange("p a b -> p (a b)"), ps_fm[:, 0:n6]),
            wait=[PE, (SE, None)], inc=[(SE, 1)])
        raw_se = SE.n

        for vi in (1, 4):
            p.emit("vector", lambda v, vi=vi: v.tensor_scalar_add(
                modfm[:, vi, :], modfm[:, vi, :], 1.0),
                wait=[(SE, raw_se), (VE, None)], inc=[(VE, 1)])
        for vi in (2, 5):
            p.emit("vector", lambda v, vi=vi: v.tensor_scalar(
                gsc[:], modfm[:, vi, :], 0.0, None, ALU.is_ge),
                wait=[(SE, raw_se), (VE, None)], inc=[(VE, 1)])
            p.emit("vector", lambda v: v.tensor_scalar(
                gsc[:], gsc[:], 2.0, -1.0, ALU.mult, ALU.add),
                wait=[(VE, None)], inc=[(VE, 1)])
            p.emit("vector", lambda v, vi=vi: v.tensor_tensor(
                modfm[:, vi, :], modfm[:, vi, :], gsc[:], ALU.mult),
                wait=[(VE, None)], inc=[(VE, 1)])
            p.emit("vector", lambda v, vi=vi: v.tensor_scalar_max(
                modfm[:, vi, :], modfm[:, vi, :], 0.25),
                wait=[(VE, None)], inc=[(VE, 1)])
            p.emit("vector", lambda v, vi=vi: v.tensor_tensor(
                modfm[:, vi, :], modfm[:, vi, :], gsc[:], ALU.mult),
                wait=[(VE, None)], inc=[(VE, 1)])
        mod_ve = VE.n

        evac_se = {}
        ri = 0
        for side in range(2):
            hh_sb_ = hhqb_sb if side == 0 else hhkb_sb
            dst = wqrot if side == 0 else wkrot
            for h in range(H):
                for d0 in range(0, DC, 4):
                    dcs = list(range(d0, min(d0 + 4, DC)))
                    par = ri % 2
                    w0 = [(VE, hhk_ve), (LDC, wqk_ld)]
                    if ri >= 2:
                        w0.append((SE, evac_se[ri - 2]))
                    crow = h if side == 0 else H + h
                    for i, dc in enumerate(dcs):
                        p.emit("tensor",
                               lambda t, i=i, dc=dc, crow=crow, hh_sb_=hh_sb_,
                               h=h, par=par:
                               t.matmul(ps_rot[par][:, i * 128:(i + 1) * 128],
                                        wqkT_sb[:, crow,
                                                dc * 128:(dc + 1) * 128],
                                        hh_sb_[:, h, :],
                                        start=True, stop=True),
                               wait=(w0 if i == 0 else []),
                               inc=([(PE, 1)] if i == len(dcs) - 1 else []))
                    p.emit("scalar",
                           lambda s, dst=dst, h=h, d0=d0, n=len(dcs), par=par:
                           s.copy(dst[:, h, d0:d0 + n, :],
                                  ps_rot[par][:, 0:n * 128]),
                           wait=[PE, (SE, None)], inc=[(SE, 1)])
                    evac_se[ri] = SE.n
                    ri += 1
        rot_se = SE.n
        es.close()
        return dict(mod_ve=mod_ve, rot_se=rot_se, mask_ld=mask_ld,
                    wrt_ld=wrt_ld, wv_ld=wv_ld, wout_ld=wout_ld,
                    const_ve=const_ve)

    P0 = p0()
    p.fence([(PE, None), (VE, None), (SE, None), (GP, None), LDW_, LDC])

    def ln_phase(prefix, src_of, n_ch, dst_of, ca_idx, sa_idx, st_sems,
                 src_wait_of=None):
        es = ExitStack()
        sbl = lambda n, s_, d=F32: es.enter_context(
            nc.sbuf_tensor(prefix + n, s_, d))
        psl = lambda n, s_, d=F32: es.enter_context(
            nc.psum_tensor(prefix + n, s_, d))
        xb = [sbl(f"xb{i}", [128, DC, 512], BF16) for i in range(2)]
        xq = [sbl(f"xq{i}", [128, DC, 512], BF16) for i in range(2)]
        h_c = [sbl(f"h{i}", [128, DC, 512], BF16) for i in range(2)]
        rf = [sbl(f"rf{i}", [128, 3, 512]) for i in range(2)]
        rb = [sbl(f"rb{i}", [128, 2, 512], BF16) for i in range(2)]
        t1b = sbl("t1b", [128, DC, 512], BF16)
        ps_s = [psl(f"ps_s{i}", [128, 512]) for i in range(2)]
        ps_q = [psl(f"ps_q{i}", [128, 512]) for i in range(2)]

        dve_done, st_done, aff_done = {}, {}, {}
        ld_marks = {}

        def emit_loads(ci):
            pr = ci % 2
            w_free = ([(VE, dve_done[ci - 2])]
                      if ci >= 2 else [])
            w_src = src_wait_of(ci) if src_wait_of else []
            p.emit("gpsimd", lambda g, ci=ci, pr=pr: g.dma_start(
                out=xb[pr][:], in_=src_of(ci)),
                wait=w_free + w_src, inc=[(LXB[pr], 16)])
            ld_marks[ci] = (LXB[pr], LXB[pr].n)

        emit_loads(0)
        if n_ch > 1:
            emit_loads(1)
        for ci in range(n_ch):
            pr = ci % 2
            ldc_w = ld_marks[ci]
            p.emit("scalar", lambda s, pr=pr: s.activation(
                xq[pr][:], xb[pr][:], AF.Square),
                wait=[ldc_w, (SE, None)], inc=[(SE, 1)])
            sq_se = SE.n
            pe_w = [ldc_w, (VE, P0["const_ve"])] + \
                ([(VE, dve_done[ci - 2])] if ci >= 2 else [])
            for dc in range(DC):
                p.emit("tensor", lambda t, dc=dc, pr=pr: t.matmul(
                    ps_s[pr][:], ones128[:], xb[pr][:, dc, :],
                    start=(dc == 0), stop=(dc == DC - 1)),
                    wait=(pe_w if dc == 0 else []))
            for dc in range(DC):
                p.emit("tensor", lambda t, dc=dc, pr=pr: t.matmul(
                    ps_q[pr][:], ones128[:], xq[pr][:, dc, :],
                    start=(dc == 0), stop=(dc == DC - 1)),
                    wait=([(SE, sq_se)] if dc == 0 else []),
                    inc=([(PE, 1)] if dc == DC - 1 else []))
            pe_stats = PE.n
            r, rb_ = rf[pr], rb[pr]
            rw = [(VE, dve_done[ci - 2])] if ci >= 2 else []
            p.emit("vector", lambda v, r=r, pr=pr: v.tensor_scalar(
                r[:, 0, :], ps_s[pr][:], 1.0 / D, None, ALU.mult),
                wait=[(PE, pe_stats), (VE, None)] + rw, inc=[(VE, 1)])
            p.emit("vector", lambda v, r=r, pr=pr: v.tensor_scalar(
                r[:, 1, :], ps_q[pr][:], 1.0 / D, None, ALU.mult),
                wait=[(VE, None)], inc=[(VE, 1)])
            p.emit("vector", lambda v, r=r: v.tensor_tensor(
                r[:, 2, :], r[:, 0, :], r[:, 0, :], ALU.mult),
                wait=[(VE, None)], inc=[(VE, 1)])
            p.emit("vector", lambda v, r=r: v.tensor_tensor(
                r[:, 2, :], r[:, 1, :], r[:, 2, :], ALU.subtract),
                wait=[(VE, None)], inc=[(VE, 1)])
            p.emit("scalar", lambda s, r=r: s.activation(
                r[:, 2, :], r[:, 2, :], AF.Ln, bias=epscol[:]),
                wait=[VE, (SE, None)], inc=[(SE, 1)])
            p.emit("scalar", lambda s, r=r, rb_=rb_: s.activation(
                rb_[:, 0, :], r[:, 2, :], AF.Exp, scale=neghalf[:]),
                wait=[(SE, None)], inc=[(SE, 1)])
            rstd_se = SE.n
            p.emit("vector", lambda v, r=r, rb_=rb_: v.scalar_tensor_tensor(
                rb_[:, 1, :], r[:, 0, :], -1.0, rb_[:, 0, :],
                ALU.mult, ALU.mult),
                wait=[(SE, rstd_se), (VE, None)], inc=[(VE, 1)])
            rowm = VE.n
            p.emit("vector", lambda v, pr=pr, rb_=rb_: v.tensor_tensor(
                t1b[:], xb[pr][:],
                rb_[:, 0, :].unsqueeze(1).to_broadcast([128, DC, 512]),
                ALU.mult),
                wait=[(VE, None), ldc_w, (VE, rowm), (VE, P0["mod_ve"])],
                inc=[(VE, 1)])
            p.emit("vector", lambda v, rb_=rb_: v.tensor_tensor(
                t1b[:], t1b[:],
                rb_[:, 1, :].unsqueeze(1).to_broadcast([128, DC, 512]),
                ALU.add),
                wait=[(VE, None)], inc=[(VE, 1)])
            for dc in range(DC):
                hw = [st_done[ci - 2]] if (dc == 0 and ci >= 2) else []
                p.emit("vector", lambda v, dc=dc, pr=pr:
                       v.tensor_scalar(
                           h_c[pr][:, dc, :], t1b[:, dc, :],
                           modfm[:, ca_idx, dc:dc + 1],
                           modfm[:, sa_idx, dc:dc + 1],
                           ALU.mult, ALU.add),
                       wait=[(VE, None)] + hw, inc=[(VE, 1)])
            dve_done[ci] = VE.n
            aff_done[ci] = VE.n
            if ci + 2 < n_ch:
                emit_loads(ci + 2)
            p.emit("sync", lambda g, ci=ci, pr=pr: g.dma_start(
                out=dst_of(ci), in_=h_c[pr][:]),
                wait=[(VE, aff_done[ci])] +
                     ([st_done[ci - 2]] if ci >= 2 else []),
                inc=[(st_sems[pr], 16)])
            st_done[ci] = (st_sems[pr], st_sems[pr].n)
        es.close()
        return st_done

    p1_st = ln_phase(
        "p1_",
        src_of=lambda ci: fmr(xT)[:, :, ci * 512:(ci + 1) * 512],
        n_ch=cfg.NCH,
        dst_of=lambda ci: fmr(h1T)[:, :, ci * 512:(ci + 1) * 512],
        ca_idx=1, sa_idx=0, st_sems=STH1)
    p.fence([(PE, None), (VE, None), (SE, None), (GP, None),
             p1_st[cfg.NCH - 1], p1_st[cfg.NCH - 2],
             LX[0], LX[1], LXB[0], LXB[1]])

    WCPA = p.sem("wcpa")
    WCPB = p.sem("wcpb")

    def emit_precast(first_wait):
        na = min(len(cfg.passes[0]) + (len(cfg.passes[1])
                                       if len(cfg.passes) > 1 else 0), NEX)
        for r in range(NEX):
            sem = WCPA if r < na else WCPB
            usrc = wru_d if r < cfg.NRE else wsu_d
            dsrc = wrd_d if r < cfg.NRE else wsd_d
            ui = r if r < cfg.NRE else r - cfg.NRE
            p.emit("gpsimd", lambda g, r=r, usrc=usrc, ui=ui: g.dma_start(
                out=wub_h[r * D:(r + 1) * D, :],
                in_=usrc[ui * D:(ui + 1) * D, :]),
                wait=(first_wait if r == 0 else []), inc=[(sem, 16)])
            p.emit("gpsimd", lambda g, r=r, dsrc=dsrc, ui=ui: g.dma_start(
                out=wdb_h[r * E:(r + 1) * E, :],
                in_=dsrc[ui * E:(ui + 1) * E, :]), inc=[(sem, 16)])

    def p2():
        es = ExitStack()
        sbl = lambda n, s_, d=F32: es.enter_context(nc.sbuf_tensor(n, s_, d))
        psl = lambda n, s_, d=F32: es.enter_context(nc.psum_tensor(n, s_, d))
        h1c = [sbl(f"a_h1c{i}", [128, DC, 512], BF16) for i in range(2)]
        Ksb = sbl("a_K", [128, 2, cfg.TKV], BF16)
        Vsb = sbl("a_V", [128, cfg.KVT, 2, 128], BF16)
        Qsb = sbl("a_Q", [128, 2, cfg.NQ], BF16)
        expb = [sbl(f"a_exp{i}", [128, 1024], BF16) for i in range(2)]
        recb = [sbl(f"a_recb{i}", [128, 512]) for i in range(2)]
        aoutg = [sbl(f"a_aout{i}", [128, 512], BF16) for i in range(2)]
        ps_p = [psl(f"a_ps_p{i}", [128, 512]) for i in range(2)]
        ps_t = [psl(f"a_ps_t{i}", [128, 1024]) for i in range(2)]
        ps_l = [psl(f"a_ps_l{i}", [128, 512]) for i in range(2)]
        h1f = fmr(h1T)

        proj_pe = {}
        aout_st = {}
        ag_st = {}
        gctr = 0
        ld2_marks = {}

        def emit_h1load(pair, ci, w_free):
            pr = ci % 2
            p.emit("sync", lambda g, ci=ci, pr=pr: g.dma_start(
                out=h1c[pr][:], in_=h1f[:, :, ci * 512:(ci + 1) * 512]),
                wait=w_free + [p1_st[ci]], inc=[(LH1[pr], 16)])
            ld2_marks[(pair, ci)] = (LH1[pr], LH1[pr].n)
        p.emit("vector", lambda v: v.memset(ps_t[0][:], 0.0),
               wait=[(VE, None)], inc=[(VE, 1)])
        p.emit("vector", lambda v: v.memset(ps_t[1][:], 0.0),
               wait=[(VE, None)], inc=[(VE, 1)])
        groupend_ve = VE.n
        pt_init_ve = VE.n
        prev_exp = {}
        ln_se = {}
        aout_ve = {}
        last_pair = None
        for pair in getattr(cfg, "pair_order", range(cfg.NPAIR)):
            h0 = 2 * pair
            evac_se = {}
            pi = 0
            for ci in range(cfg.KVCH + cfg.QCH):
                pr = ci % 2
                w_free = []
                if ci >= 2:
                    w_free = [(PE, proj_pe[(pair, ci - 2)])]
                elif last_pair is not None:
                    w_free = [(PE, proj_pe[(last_pair,
                                            cfg.KVCH + cfg.QCH - 2 + ci)])]
                if (pair, ci) not in ld2_marks:
                    emit_h1load(pair, ci, w_free)
                ldh_w = ld2_marks[(pair, ci)]
                is_kv = ci < cfg.KVCH

                def proj(lhs_of, rhs_of, out_of, n_acc, extra_w, out_w):
                    nonlocal pi
                    par = pi % 2
                    w0 = list(extra_w)
                    if pi >= 2:
                        w0.append((SE, evac_se[pi - 2]))
                    else:
                        w0.append((VE, max(aout_ve.values(), default=0)))
                    for a in range(n_acc):
                        p.emit("tensor",
                               lambda t, a=a, par=par, lhs_of=lhs_of,
                               rhs_of=rhs_of, out_w=out_w, n_acc=n_acc:
                               t.matmul(ps_p[par][:, 0:out_w],
                                        lhs_of(a), rhs_of(a),
                                        start=(a == 0), stop=(a == n_acc - 1)),
                               wait=(w0 if a == 0 else []),
                               inc=([(PE, 1)] if a == n_acc - 1 else []))
                    p.emit("scalar",
                           lambda s, par=par, out_of=out_of, out_w=out_w:
                           s.copy(out_of(), ps_p[par][:, 0:out_w]),
                           wait=[PE, (SE, None)], inc=[(SE, 1)])
                    evac_se[pi] = SE.n
                    pi += 1

                if is_kv:
                    for hh in range(2):
                        proj(lambda a, hh=hh, h0=h0: wkrot[:, h0 + hh, a, :],
                             lambda a, pr=pr: h1c[pr][:, a, :],
                             lambda hh=hh, ci=ci:
                             Ksb[:, hh, ci * 512:(ci + 1) * 512],
                             DC, [ldh_w, (SE, P0["rot_se"])], 512)
                    for ts_ in range(4):
                        proj(lambda a, pr=pr, ts_=ts_:
                             h1c[pr][:, a, ts_ * 128:(ts_ + 1) * 128],
                             lambda a, h0=h0:
                             wvb[:, a, h0 * 128:(h0 + 2) * 128],
                             lambda ci=ci, ts_=ts_:
                             Vsb[:, ci * 4 + ts_, :, :].rearrange(
                                 "p a b -> p (a b)"),
                             DC, [ldh_w, (LDC, P0["wv_ld"])], 256)
                else:
                    qi = ci - cfg.KVCH
                    for hh in range(2):
                        proj(lambda a, hh=hh, h0=h0: wqrot[:, h0 + hh, a, :],
                             lambda a, pr=pr: h1c[pr][:, a, :],
                             lambda hh=hh, qi=qi:
                             Qsb[:, hh, qi * 512:(qi + 1) * 512],
                             DC, [ldh_w, (SE, P0["rot_se"])], 512)
                proj_pe[(pair, ci)] = PE.n
            tailA = SE.n
            if pair == 0:
                emit_precast([(SE, tailA)])
            nxt = pair + 1
            if nxt < cfg.NPAIR:
                for cj in range(2):
                    emit_h1load(nxt, cj,
                                [(PE, proj_pe[(pair,
                                               cfg.KVCH + cfg.QCH - 2 + cj)])])

            for hh in range(2):
                h = h0 + hh
                for g in range(cfg.NG):
                    ktmax = 8 * g + 8
                    qlo = 4 * g * 128
                    gpar = gctr % 2
                    ps_o_g = ps_p[gpar]
                    ps_l_g = ps_l[gpar]
                    duos = [(k, min(k + 2, ktmax)) for k in range(0, ktmax, 2)]
                    nd = len(duos)
                    qk_pe, exp_se, msk_ve, av_pe = {}, {}, {}, {}
                    pe0 = prev_exp.copy()

                    def emit_qk(di):
                        k0, k1 = duos[di]
                        par = di % 2
                        for kt in range(k0, k1):
                            act = _active(cfg, g, kt)
                            j0 = 4 - act
                            col = (kt - k0) * 512
                            w = []
                            if kt == k0:
                                w = [(SE, tailA)]
                                if di >= 2:
                                    w.append((SE, exp_se[di - 2]))
                                elif (di % 2) in pe0:
                                    w.append((SE, pe0[di % 2]))
                                else:
                                    w.append((VE, pt_init_ve))
                            p.emit("tensor",
                                   lambda t, kt=kt, j0=j0, col=col, par=par,
                                   hh=hh, qlo=qlo, g=g:
                                   t.matmul(
                                       ps_t[par][:, col + j0 * 128:col + 512],
                                       Ksb[:, hh, kt * 128:(kt + 1) * 128],
                                       Qsb[:, hh, qlo + j0 * 128:qlo + 512],
                                       start=True, stop=True),
                                   wait=w,
                                   inc=([(PE, 1)] if kt == k1 - 1 else []))
                        qk_pe[di] = PE.n

                    def emit_exp(di):
                        k0, k1 = duos[di]
                        par = di % 2
                        wid = (k1 - k0) * 512
                        w = [(PE, qk_pe[di]), (SE, None)]
                        if di >= 2:
                            w.append((PE, av_pe[di - 2]))
                            if (di - 2) in msk_ve:
                                w.append((VE, msk_ve[di - 2]))
                        p.emit("scalar", lambda s, par=par, wid=wid:
                               s.activation(expb[par][:, 0:wid],
                                            ps_t[par][:, 0:wid], AF.Exp),
                               wait=w, inc=[(SE, 1)])
                        exp_se[di] = SE.n
                        mm = False
                        for kt in range(k0, k1):
                            if kt >= 8 * g:
                                jstar = (kt - 8 * g) // 2
                                mA = (kt - 8 * g) % 2 == 0
                                col = (kt - k0) * 512 + jstar * 128
                                mo = 0 if mA else 128
                                p.emit("vector",
                                       lambda v, par=par, col=col, mo=mo:
                                       v.tensor_tensor(
                                           expb[par][:, col:col + 128],
                                           expb[par][:, col:col + 128],
                                           maskb[:, mo:mo + 128], ALU.mult),
                                       wait=[(SE, exp_se[di]),
                                             (LDC, P0["mask_ld"]),
                                             (VE, None)],
                                       inc=[(VE, 1)])
                                mm = True
                        if mm:
                            msk_ve[di] = VE.n

                    def emit_av(di):
                        k0, k1 = duos[di]
                        par = di % 2
                        first = True
                        for kt in range(k0, k1):
                            act = _active(cfg, g, kt)
                            j0 = 4 - act
                            col = (kt - k0) * 512
                            w = []
                            if first:
                                w = [(SE, exp_se[di])]
                                if di in msk_ve:
                                    w.append((VE, msk_ve[di]))
                                if kt == 0:
                                    w.append((SE, ln_se.get(gpar, 0)))
                                    w.append((VE, aout_ve.get(gpar, 0)))
                            p.emit("tensor",
                                   lambda t, kt=kt, j0=j0, col=col, par=par,
                                   hh=hh, ktmax=ktmax, ps_o_g=ps_o_g:
                                   t.matmul(
                                       ps_o_g[:, j0 * 128:512],
                                       Vsb[:, kt, hh, :],
                                       expb[par][:, col + j0 * 128:col + 512],
                                       start=(kt == 0),
                                       stop=(kt == ktmax - 1)),
                                   wait=w)
                            p.emit("tensor",
                                   lambda t, kt=kt, j0=j0, col=col, par=par,
                                   ktmax=ktmax, ps_l_g=ps_l_g:
                                   t.matmul(
                                       ps_l_g[:, j0 * 128:512],
                                       ones128[:],
                                       expb[par][:, col + j0 * 128:col + 512],
                                       start=(kt == 0),
                                       stop=(kt == ktmax - 1)),
                                   inc=([(PE, 1)] if kt == k1 - 1 else []))
                            first = False
                        av_pe[di] = PE.n

                    for di in range(nd):
                        emit_qk(di)
                        emit_exp(di)
                        if di >= 1:
                            emit_av(di - 1)
                    emit_av(nd - 1)

                    p.emit("scalar", lambda s, ps_l_g=ps_l_g, gpar=gpar:
                           s.activation(recb[gpar][:], ps_l_g[:], AF.Ln),
                           wait=[(PE, av_pe[nd - 1]), (SE, None),
                                 (VE, aout_ve.get(gpar, 0))],
                           inc=[(SE, 1)])
                    ln_se[gpar] = SE.n
                    p.emit("scalar", lambda s, gpar=gpar:
                           s.activation(recb[gpar][:], recb[gpar][:],
                                        AF.Exp, scale=negone[:]),
                           wait=[(SE, None)], inc=[(SE, 1)])
                    rec_se = SE.n
                    gp = gctr % 2
                    aw = [ag_st[gctr - 2]] if gctr >= 2 else []
                    p.emit("vector", lambda v, gp=gp, gpar=gpar,
                           ps_o_g=ps_o_g:
                           v.tensor_tensor(
                               aoutg[gp][:], ps_o_g[:], recb[gpar][:],
                               ALU.mult),
                           wait=[(SE, rec_se), (VE, None)] + aw,
                           inc=[(VE, 1)])
                    aout_ve[gpar] = VE.n
                    groupend_ve = VE.n
                    prev_exp = {}
                    for dd in (nd - 2, nd - 1):
                        if dd >= 0:
                            prev_exp[dd % 2] = exp_se[dd]
                    p.emit("sync", lambda g_, h=h, g=g, gp=gp: g_.dma_start(
                        out=fmr(atT)[:, h, g * 512:(g + 1) * 512],
                        in_=aoutg[gp][:]),
                        wait=[(VE, groupend_ve)] + aw,
                        inc=[(STA[gp], 16)])
                    ag_st[gctr] = (STA[gp], STA[gp].n)
                    gctr += 1
            last_pair = pair
        es.close()
        return aout_st

    aout_st = p2()
    ctx_w2.close()
    p.fence([(PE, None), (VE, None), (SE, None), (GP, None),
             LH1[0], LH1[1], STA[0], STA[1], LGR])

    def p3():
        es = ExitStack()
        sbl = lambda n, s_, d=F32: es.enter_context(nc.sbuf_tensor(n, s_, d))
        psl = lambda n, s_, d=F32: es.enter_context(nc.psum_tensor(n, s_, d))
        xq = [sbl(f"o_xq{i}", [128, DC, 512]) for i in range(2)]
        ats = [sbl(f"o_at{i}", [128, H, 512], BF16) for i in range(2)]
        x1c = [sbl(f"o_x1{i}", [128, DC, 512]) for i in range(2)]
        x1b = sbl("o_x1b", [128, DC, 512], BF16)
        x1q = sbl("o_x1q", [128, DC, 512], BF16)
        h2c = [sbl(f"o_h2{i}", [128, DC, 512], BF16) for i in range(2)]
        rf3 = sbl("o_rf", [128, 3, 512])
        rb3 = sbl("o_rb", [128, 2, 512], BF16)
        t1b3 = sbl("o_t1b", [128, DC, 512], BF16)
        ps_o = [psl(f"o_ps_o{i}", [128, 512]) for i in range(2)]
        ps_s = [psl(f"o_ps_s{i}", [128, 512]) for i in range(2)]
        ps_q = [psl(f"o_ps_q{i}", [128, 512]) for i in range(2)]
        ps_lg = psl("o_ps_lg", [cfg.NRE, 512])

        atf = fmr(atT)
        xf = fmr(xT)
        dve_done, sth2_done, stx1_done, lg_se = {}, {}, {}, {}
        x1_ve = {}
        gps3_done = {}
        aff3_done = {}
        ld3_marks = {}
        pe_stats_prev = 0

        def emit_loads3(ci):
            pr = ci % 2
            sl = slice(ci * 512, (ci + 1) * 512)
            w_free = [(VE, dve_done[ci - 2]), (GP, gps3_done[ci - 2])] \
                if ci >= 2 else []
            p.emit("sync", lambda g, ci=ci, pr=pr: g.dma_start(
                out=xq[pr][:], in_=xf[:, :, cfg.TKV + ci * 512:
                                      cfg.TKV + (ci + 1) * 512]),
                wait=w_free, inc=[(LX[pr], 16)])
            p.emit("sync", lambda g, ci=ci, pr=pr, sl=sl: g.dma_start(
                out=ats[pr][:], in_=atf[:, :, sl]),
                wait=w_free + [STA[0], STA[1]],
                inc=[(LAT[pr], 16)])
            ld3_marks[ci] = ((LX[pr], LX[pr].n), (LAT[pr], LAT[pr].n))

        emit_loads3(0)
        if cfg.NTC3 > 1:
            emit_loads3(1)
        for ci in range(cfg.NTC3):
            pr = ci % 2
            sl = slice(ci * 512, (ci + 1) * 512)
            ldx_w, ldh_w = ld3_marks[ci]
            oproj_ve = {}
            for fc in range(DC):
                par2 = fc % 2
                w0 = [ldh_w, (LDC, P0["wout_ld"])]
                if fc >= 2:
                    w0.append((VE, oproj_ve[fc - 2]))
                elif ci >= 1:
                    w0.append((VE, x1_ve[ci - 1]))
                if fc == 0 and ci >= 2:
                    w0.append(stx1_done[ci - 2])
                for h in range(H):
                    p.emit("tensor", lambda t, h=h, fc=fc, pr=pr, par2=par2:
                           t.matmul(ps_o[par2][:],
                                    woutb[:, h, fc * 128:(fc + 1) * 128],
                                    ats[pr][:, h, :],
                                    start=(h == 0), stop=(h == H - 1)),
                           wait=(w0 if h == 0 else []),
                           inc=([(PE, 1)] if h == H - 1 else []))
                pe_o = PE.n
                p.emit("vector", lambda v, fc=fc, pr=pr, par2=par2:
                       v.scalar_tensor_tensor(
                           x1c[pr][:, fc, :], ps_o[par2][:],
                           modfm[:, 2, fc:fc + 1], xq[pr][:, fc, :],
                           ALU.mult, ALU.add),
                       wait=[(PE, pe_o), ldx_w, (VE, None),
                             (VE, P0["mod_ve"])],
                       inc=[(VE, 1)])
                oproj_ve[fc] = VE.n
            x1_ve[ci] = VE.n
            xw = [(PE, pe_stats_prev)] if ci >= 1 else []
            if ci >= 1:
                xw.append((VE, dve_done[ci - 1]))
            p.emit("scalar", lambda s, pr=pr: s.copy(x1b[:], x1c[pr][:]),
                   wait=[(VE, x1_ve[ci]), (SE, None)] + xw, inc=[(SE, 1)])
            p.emit("scalar", lambda s: s.activation(
                x1q[:], x1b[:], AF.Square), wait=[(SE, None)], inc=[(SE, 1)])
            sq_se = SE.n
            p.emit("sync", lambda g, pr=pr, sl=sl: g.dma_start(
                out=fmr(x1T)[:, :, sl], in_=x1c[pr][:]),
                wait=[(VE, x1_ve[ci])] +
                     ([stx1_done[ci - 2]] if ci >= 2 else []),
                inc=[(STX1[pr], 16)])
            stx1_done[ci] = (STX1[pr], STX1[pr].n)
            w_st = [(VE, dve_done[ci - 2]), (GP, gps3_done[ci - 2])] \
                if ci >= 2 else []
            for dc in range(DC):
                p.emit("tensor", lambda t, dc=dc, pr=pr: t.matmul(
                    ps_s[pr][:], ones128[:], x1b[:, dc, :],
                    start=(dc == 0), stop=(dc == DC - 1)),
                    wait=([(SE, sq_se)] + w_st if dc == 0 else []))
            for dc in range(DC):
                p.emit("tensor", lambda t, dc=dc, pr=pr: t.matmul(
                    ps_q[pr][:], ones128[:], x1q[:, dc, :],
                    start=(dc == 0), stop=(dc == DC - 1)),
                    inc=([(PE, 1)] if dc == DC - 1 else []))
            pe_stats = PE.n
            pe_stats_prev = pe_stats
            r, rb_ = rf3, rb3
            p.emit("vector", lambda v, r=r, pr=pr: v.tensor_scalar(
                r[:, 0, :], ps_s[pr][:], 1.0 / D, None, ALU.mult),
                wait=[(PE, pe_stats), (VE, None)], inc=[(VE, 1)])
            p.emit("vector", lambda v, r=r, pr=pr: v.tensor_scalar(
                r[:, 1, :], ps_q[pr][:], 1.0 / D, None, ALU.mult),
                wait=[(VE, None)], inc=[(VE, 1)])
            p.emit("vector", lambda v, r=r: v.tensor_tensor(
                r[:, 2, :], r[:, 0, :], r[:, 0, :], ALU.mult),
                wait=[(VE, None)], inc=[(VE, 1)])
            p.emit("vector", lambda v, r=r: v.tensor_tensor(
                r[:, 2, :], r[:, 1, :], r[:, 2, :], ALU.subtract),
                wait=[(VE, None)], inc=[(VE, 1)])
            p.emit("scalar", lambda s, r=r: s.activation(
                r[:, 2, :], r[:, 2, :], AF.Ln, bias=epscol[:]),
                wait=[VE, (SE, None)], inc=[(SE, 1)])
            p.emit("scalar", lambda s, r=r, rb_=rb_: s.activation(
                rb_[:, 0, :], r[:, 2, :], AF.Exp, scale=neghalf[:]),
                wait=[(SE, None)], inc=[(SE, 1)])
            rstd_se = SE.n
            p.emit("vector", lambda v, r=r, rb_=rb_: v.scalar_tensor_tensor(
                rb_[:, 1, :], r[:, 0, :], -1.0, rb_[:, 0, :],
                ALU.mult, ALU.mult),
                wait=[(SE, rstd_se), (VE, None)], inc=[(VE, 1)])
            rowm = VE.n
            hw = [sth2_done[ci - 2]] if ci >= 2 else []
            p.emit("vector", lambda v, rb_=rb_: v.tensor_tensor(
                t1b3[:], x1b[:],
                rb_[:, 0, :].unsqueeze(1).to_broadcast([128, DC, 512]),
                ALU.mult),
                wait=[(VE, rowm), (VE, None)], inc=[(VE, 1)])
            p.emit("vector", lambda v, rb_=rb_: v.tensor_tensor(
                t1b3[:], t1b3[:],
                rb_[:, 1, :].unsqueeze(1).to_broadcast([128, DC, 512]),
                ALU.add),
                wait=[(VE, None)], inc=[(VE, 1)])
            for dc in range(DC):
                p.emit("vector", lambda v, dc=dc, pr=pr:
                       v.tensor_scalar(
                           h2c[pr][:, dc, :], t1b3[:, dc, :],
                           modfm[:, 4, dc:dc + 1],
                           modfm[:, 3, dc:dc + 1],
                           ALU.mult, ALU.add),
                       wait=[(VE, None)] + (hw if dc == 0 else []),
                       inc=[(VE, 1)])
            dve_done[ci] = VE.n
            gps3_done[ci] = GP.n
            aff3_done[ci] = VE.n
            if ci + 2 < cfg.NTC3:
                emit_loads3(ci + 2)
            p.emit("sync", lambda g, pr=pr, sl=sl: g.dma_start(
                out=fmr(h2T)[:, :, sl], in_=h2c[pr][:]),
                wait=[(VE, aff3_done[ci])] +
                     ([sth2_done[ci - 2]] if ci >= 2 else []),
                inc=[(STH2[pr], 16)])
            sth2_done[ci] = (STH2[pr], STH2[pr].n)
            w_lg = [(SE, lg_se[ci - 1])] if ci >= 1 else []
            for dc in range(DC):
                p.emit("tensor", lambda t, dc=dc, pr=pr: t.matmul(
                    ps_lg[:], wrtb[:, dc, :], h2c[pr][:, dc, :],
                    start=(dc == 0), stop=(dc == DC - 1)),
                    wait=([(VE, aff3_done[ci]),
                           (LDC, P0["wrt_ld"])] + w_lg
                          if dc == 0 else []),
                    inc=([(PE, 1)] if dc == DC - 1 else []))
            p.emit("scalar", lambda s, sl=sl: s.copy(logT[:, sl], ps_lg[:]),
                   wait=[PE, (SE, None)], inc=[(SE, 1)])
            lg_se[ci] = SE.n
        es.close()
        return sth2_done, stx1_done, lg_se[cfg.NTC3 - 1]

    ctx_r = ExitStack()
    logT = ctx_r.enter_context(nc.sbuf_tensor("logT", [cfg.NRE, cfg.NQ], F32))
    sth2_done, stx1_done, lg_last = p3()
    ctx_w3.close()
    p.fence([(PE, None), (VE, None), (SE, None), (GP, None),
             sth2_done[cfg.NTC3 - 1], stx1_done[cfg.NTC3 - 1],
             LX[0], LX[1], LAT[0], LAT[1]] +
            ([sth2_done[cfg.NTC3 - 2], stx1_done[cfg.NTC3 - 2]]
             if cfg.NTC3 >= 2 else []))

    def router():
        es = ExitStack()
        sbl = lambda n, s_, d=F32: es.enter_context(nc.sbuf_tensor(n, s_, d))
        psl = lambda n, s_, d=F32: es.enter_context(nc.psum_tensor(n, s_, d))
        NT = cfg.NQ // 128
        NR = cfg.NRE
        gate_fm = sbl("r_gate_fm", [NR, cfg.NQ], BF16)
        lt = sbl("r_lt", [128, NT, NR])
        pex = sbl("r_pe", [128, NT, NR])
        red = sbl("r_red", [128, NT])
        m8 = sbl("r_m8", [128, 8])
        zap = sbl("r_zap", [128, NR])
        gtm = sbl("r_gtm", [128, NT, NR])
        ps_a = psl("r_ps_a", [128, 512])

        tr_se = []
        per_bank_lg = 512 // NR
        for t0 in range(0, NT, per_bank_lg):
            tiles = list(range(t0, min(t0 + per_bank_lg, NT)))
            w0 = [(SE, lg_last), (LDW_, None)]
            if tr_se:
                w0.append((SE, tr_se[-1]))
            for i, tt in enumerate(tiles):
                p.emit("tensor", lambda t, tt=tt, i=i: t.transpose(
                    ps_a[:, i * NR:(i + 1) * NR],
                    logT[:, tt * 128:(tt + 1) * 128],
                    ident[0:NR, 0:NR]),
                    wait=(w0 if i == 0 else []),
                    inc=([(PE, 1)] if i == len(tiles) - 1 else []))
            p.emit("scalar", lambda s, t0=t0, n=len(tiles): s.copy(
                lt[:, t0:t0 + n, :].rearrange("p a b -> p (a b)"),
                ps_a[:, 0:n * NR]), wait=[PE, (SE, None)], inc=[(SE, 1)])
            tr_se.append(SE.n)
        p.emit("scalar", lambda s: s.activation(
            pex[:].rearrange("p a b -> p (a b)"),
            lt[:].rearrange("p a b -> p (a b)"), AF.Exp),
            wait=[(SE, tr_se[-1])], inc=[(SE, 1)])
        exp_se = SE.n
        p.emit("vector", lambda v: v.reduce_sum(red[:], pex[:], axis=AX),
               wait=[(SE, exp_se), (VE, None)], inc=[(VE, 1)])
        p.emit("vector", lambda v: v.reciprocal(red[:], red[:]),
               wait=[(VE, None)], inc=[(VE, 1)])
        p.emit("vector", lambda v: v.tensor_tensor(
            pex[:], pex[:],
            red[:].unsqueeze(2).to_broadcast([128, NT, NR]), ALU.mult),
            wait=[(VE, None)], inc=[(VE, 1)])
        for tt in range(NT):
            p.emit("vector", lambda v, tt=tt: v.max(
                out=m8[:], in_=pex[:, tt, :]),
                wait=[(VE, None)], inc=[(VE, 1)])
            if cfg.TOPK < 8:
                p.emit("vector", lambda v: v.memset(
                    m8[:, cfg.TOPK:8], -1e30),
                    wait=[(VE, None)], inc=[(VE, 1)])
            p.emit("vector", lambda v, tt=tt: v.match_replace(
                out=zap[:], in_to_replace=m8[:],
                in_values=pex[:, tt, :], imm_value=-1.0),
                wait=[(VE, None)], inc=[(VE, 1)])
            p.emit("vector", lambda v, tt=tt: v.tensor_tensor(
                zap[:], zap[:], pex[:, tt, :], ALU.not_equal),
                wait=[(VE, None)], inc=[(VE, 1)])
            p.emit("vector", lambda v, tt=tt: v.tensor_tensor(
                gtm[:, tt, :], pex[:, tt, :], zap[:], ALU.mult),
                wait=[(VE, None)], inc=[(VE, 1)])
        gtm_ve = VE.n
        gt_se = []
        for t0 in range(0, NT, 4):
            tiles = list(range(t0, min(t0 + 4, NT)))
            w0 = [(VE, gtm_ve), (SE, tr_se[-1])]
            if gt_se:
                w0.append((SE, gt_se[-1]))
            for i, tt in enumerate(tiles):
                p.emit("tensor", lambda t, tt=tt, i=i: t.transpose(
                    ps_a[0:NR, i * 128:(i + 1) * 128],
                    gtm[:, tt, :], ident[:]),
                    wait=(w0 if i == 0 else []),
                    inc=([(PE, 1)] if i == len(tiles) - 1 else []))
            p.emit("scalar", lambda s, t0=t0, n=len(tiles): s.copy(
                gate_fm[:, t0 * 128:(t0 + n) * 128],
                ps_a[0:NR, 0:n * 128]), wait=[PE, (SE, None)], inc=[(SE, 1)])
            gt_se.append(SE.n)
        p.emit("sync", lambda g: g.dma_start(
            out=gate_hbm[:], in_=gate_fm[:]),
            wait=[(SE, gt_se[-1])], inc=[(LGR, 16)])
        gr_ld = (LGR, LGR.n)
        es.close()
        return gr_ld

    gate_ld = router()
    ctx_r.close()
    p.fence([(PE, None), (VE, None), (SE, None), (GP, None), LGR])

    def p4():
        es = ExitStack()
        sbl = lambda n, s_, d=F32: es.enter_context(nc.sbuf_tensor(n, s_, d))
        psl = lambda n, s_, d=F32: es.enter_context(nc.psum_tensor(n, s_, d))
        NP = len(cfg.passes)
        G = max(len(q) for q in cfg.passes)
        NSLOT = 2 * G - 1
        NTCH = cfg.NQ // 512
        DC2 = DC // 2
        wu = [sbl(f"f_wu{i}", [128, DC, E], BF16) for i in range(NSLOT)]
        wd = [sbl(f"f_wd{i}", [128, ECN, D], BF16) for i in range(NSLOT)]
        h2cb = [sbl(f"f_h2{i}", [128, DC, 512], BF16) for i in range(2)]
        x1cb = [sbl(f"f_x1{i}", [128, DC2, 512]) for i in range(2)]
        gbs = sbl("f_gb", [128, G, 512], BF16)
        grow = sbl("f_grow", [1, G * 512], BF16)
        actb = [sbl(f"f_act{i}", [128, ECN, 512], BF16) for i in range(G)]
        accin = [sbl(f"f_ai{i}", [128, DC2, 512], BF16) for i in range(2)]
        accout = [sbl(f"f_ao{i}", [128, DC2, 512], BF16) for i in range(2)]
        outc = [sbl("f_out0", [128, DC2, 512])]
        ps_dn = psl("f_ps_dn", [128, DC2 * 512])
        ps_up = psl("f_ps_up", [128, ECN * 512])

        wruf = wub_h.rearrange("(r c p) e -> r p c e", p=128, c=DC)
        wrdf = wdb_h.rearrange("(r c p) d -> r p c d", p=128, c=ECN)
        h2f = fmr(h2T)
        x1f = fmr(x1T)
        af_ = fmr(accT)
        of = fmr(outT)

        wload, grow_ld, ac_st, out_st = {}, {}, {}, {}
        gb_pe = {}
        chunk_pe, pass_pe = {}, {}
        evac_mark = {}
        comb_done = {}
        dnh1_pe = {}

        slot_of, prev_of, slot_hist = {}, {}, {}
        slot_ctr = [0]
        deferred = {}

        def assign_slots(pi):
            for i in range(len(cfg.passes[pi])):
                s = slot_ctr[0]
                slot_ctr[0] = (s + 1) % NSLOT
                slot_of[(pi, i)] = s
                prev_of[(pi, i)] = slot_hist.get(s)
                slot_hist[s] = pi

        def emit_one_wload(pi, i):
            sl_ = slot_of[(pi, i)]
            pp = prev_of[(pi, i)]
            w_free = [(WCPA, None)] if pi < 2 else [(WCPB, None)]
            if pp is not None:
                w_free.append((PE, pass_pe[pp]))
            eid = cfg.passes[pi][i]
            p.emit("gpsimd", lambda g, sl_=sl_, s_=wruf[eid]: g.dma_start(
                out=wu[sl_][:], in_=s_), wait=w_free, inc=[(LFW[sl_], 16)])
            p.emit("gpsimd", lambda g, sl_=sl_, s_=wrdf[eid]: g.dma_start(
                out=wd[sl_][:], in_=s_), wait=[], inc=[(LFW[sl_], 16)])
            wload[(pi, i)] = (LFW[sl_], LFW[sl_].n)

        def emit_pass_loads(pi):
            for i in range(len(cfg.passes[pi])):
                pp = prev_of[(pi, i)]
                if pp is not None and pp not in pass_pe:
                    deferred.setdefault(pi, []).append(i)
                else:
                    emit_one_wload(pi, i)

        assign_slots(0)
        emit_pass_loads(0)
        if NP > 1:
            assign_slots(1)
            emit_pass_loads(1)

        for pi, plist in enumerate(cfg.passes):
            npl = len(plist)
            for i in deferred.pop(pi, []):
                emit_one_wload(pi, i)
            if pi >= 1 and pi + 1 <= NP - 1:
                assign_slots(pi + 1)
                emit_pass_loads(pi + 1)
            routed = [e for e in plist if e < cfg.NRE]
            nro = len(routed)
            r0 = routed[0] if nro else 0
            pending = None
            for ci in range(NTCH):
                pr = ci % 2
                sl = slice(ci * 512, (ci + 1) * 512)
                w_free = []
                if (pi, ci - 2) in chunk_pe:
                    w_free = [(PE, chunk_pe[(pi, ci - 2)])]
                elif pi > 0 and ci < 2:
                    w_free = [(PE, pass_pe[pi - 1])]
                p.emit("sync", lambda g, sl=sl, pr=pr: g.dma_start(
                    out=h2cb[pr][:], in_=h2f[:, :, sl]),
                    wait=w_free + [sth2_done[ci]], inc=[(LH2[pr], 16)])
                ldh_w = (LH2[pr], LH2[pr].n)
                ldx_w, ac_w = {}, {}

                def emit_half_loads(hf, ci=ci, sl=sl, ldx_w=ldx_w,
                                    ac_w=ac_w):
                    hidx = 2 * ci + hf
                    hb = hidx % 2
                    if pi == NP - 1:
                        xw = [stx1_done[ci]]
                        if (hidx - 2) in comb_done:
                            xw.append((VE, comb_done[hidx - 2]))
                        p.emit("sync", lambda g, sl=sl, hb=hb, hf=hf:
                               g.dma_start(
                                   out=x1cb[hb][:],
                                   in_=x1f[:, hf * DC2:(hf + 1) * DC2, sl]),
                               wait=xw, inc=[(LX1[hb], 16)])
                        ldx_w[hf] = (LX1[hb], LX1[hb].n)
                    if pi > 0:
                        aw_ = [ac_st[(pi - 1, hidx)]]
                        em = evac_mark.get((pi, hidx - 2),
                                           evac_mark.get(
                                               (pi - 1, 2 * NTCH - 2 + hidx)))
                        if em is not None:
                            aw_.append((SE if em[0] == "se" else VE, em[1]))
                        if pi == NP - 1 and (hidx - 2) in comb_done:
                            aw_.append((VE, comb_done[hidx - 2]))
                        p.emit("sync", lambda g, sl=sl, hb=hb, hf=hf:
                               g.dma_start(
                                   out=accin[hb][:],
                                   in_=af_[:, hf * DC2:(hf + 1) * DC2, sl]),
                               wait=aw_, inc=[(LAC[hb], 16)])
                        ac_w[hf] = (LAC[hb], LAC[hb].n)

                emit_half_loads(0)
                gb_se = None
                if nro:
                    gw = [gate_ld]
                    gl = grow_ld.get((pi, ci - 1),
                                     grow_ld.get((pi - 1, NTCH - 1)))
                    if gl is not None:
                        gw.append(gl)
                    gm = gb_pe.get((pi, ci - 1),
                                   gb_pe.get((pi - 1, NTCH - 1)))
                    if gm is not None:
                        gw.append((PE, gm))
                    p.emit("sync", lambda g, sl=sl, r0=r0, nro=nro:
                           g.dma_start(
                               out=grow[0:1, 0:nro * 512],
                               in_=gate_hbm[r0:r0 + nro, sl]),
                           wait=gw, inc=[(LGA[ci % 2], 16)])
                    grow_ld[(pi, ci)] = (LGA[ci % 2], LGA[ci % 2].n)
                    first = True
                    for w0i in range(0, nro, ECN):
                        wave = list(range(w0i, min(w0i + ECN, nro)))
                        for k, j in enumerate(wave):
                            w_ = []
                            if first:
                                w_ = [grow_ld[(pi, ci)], (SE, None),
                                      (VE, None)]
                                first = False
                            elif k == 0:
                                w_ = [(SE, None)]
                            p.emit("tensor",
                                   lambda t, k=k, r0=r0,
                                   eid=routed[j]:
                                   t.matmul(
                                       ps_up[:, k * 512:(k + 1) * 512],
                                       onesrow_bf[:],
                                       grow[0:1, (eid - r0) * 512:
                                            (eid - r0 + 1) * 512],
                                       start=True, stop=True),
                                   wait=w_,
                                   inc=([(PE, 1)] if k == len(wave) - 1
                                        else [])),
                        p.emit("scalar", lambda s, w0i=w0i, n=len(wave):
                               s.copy(gbs[:, w0i:w0i + n, :].rearrange(
                                   "p a b -> p (a b)"),
                                   ps_up[:, 0:n * 512]),
                               wait=[PE, (SE, None)], inc=[(SE, 1)])
                    gb_pe[(pi, ci)] = PE.n
                    gb_se = SE.n

                up_pe, gelu_se, act_w, dnp = {}, {}, {}, {}

                def emit_up(i, ldh_w=ldh_w, pr=pr, gb_se=gb_se):
                    w0 = [wload[(pi, i)], ldh_w]
                    if i == 0:
                        w0.append((SE, gb_se) if gb_se is not None
                                  else (SE, None))
                    else:
                        w0.append((SE, gelu_se[i - 1]))
                    for ec in range(ECN):
                        for dc in range(DC):
                            p.emit("tensor",
                                   lambda t, ec=ec, dc=dc, i=i, pr=pr,
                                   sl_=slot_of[(pi, i)]:
                                   t.matmul(
                                       ps_up[:, ec * 512:(ec + 1) * 512],
                                       wu[sl_][:, dc,
                                               ec * 128:(ec + 1) * 128],
                                       h2cb[pr][:, dc, :],
                                       start=(dc == 0), stop=(dc == DC - 1)),
                                   wait=(w0 if ec == 0 and dc == 0 else []),
                                   inc=([(PE, 1)]
                                        if ec == ECN - 1 and dc == DC - 1
                                        else []))
                    up_pe[i] = PE.n

                def emit_act(i, eid, ci=ci):
                    w = [(PE, up_pe[i]), (SE, None)]
                    prev = dnh1_pe.get((pi, ci - 1, i),
                                       dnh1_pe.get((pi - 1, NTCH - 1, i)))
                    if prev is not None:
                        w.append((PE, prev))
                    p.emit("scalar", lambda s, i=i: s.activation(
                        actb[i][:].rearrange("p a b -> p (a b)"),
                        ps_up[:], AF.Gelu),
                        wait=w, inc=[(SE, 1)])
                    gelu_se[i] = SE.n
                    if eid < cfg.NRE:
                        p.emit("vector", lambda v, i=i: v.tensor_tensor(
                            actb[i][:], actb[i][:],
                            gbs[:, i, :].unsqueeze(1).to_broadcast(
                                [128, ECN, 512]), ALU.mult),
                            wait=[(SE, gelu_se[i]), (VE, None)],
                            inc=[(VE, 1)])
                        act_w[i] = [(VE, VE.n)]
                    else:
                        act_w[i] = [(SE, gelu_se[i])]

                def emit_dn_half(hf, i, act_w=act_w, dnp=dnp, ci=ci,
                                 npl=npl):
                    for hcl in range(DC2):
                        hc = hf * DC2 + hcl
                        for ec in range(ECN):
                            w = []
                            if hcl == 0 and ec == 0:
                                w = list(act_w[i])
                                if i == 0:
                                    ph = 2 * ci + hf - 1
                                    em = evac_mark.get(
                                        (pi, ph),
                                        evac_mark.get((pi - 1,
                                                       2 * NTCH - 1)))
                                    if em is not None:
                                        w.append((SE if em[0] == "se"
                                                  else VE, em[1]))
                            mark = (hcl == DC2 - 1 and ec == ECN - 1
                                    and (hf == 1 or i == npl - 1))
                            p.emit("tensor",
                                   lambda t, hcl=hcl, hc=hc, ec=ec, i=i,
                                   npl=npl, sl_=slot_of[(pi, i)]:
                                   t.matmul(
                                       ps_dn[:, hcl * 512:(hcl + 1) * 512],
                                       wd[sl_][:, ec,
                                               hc * 128:(hc + 1) * 128],
                                       actb[i][:, ec, :],
                                       start=(i == 0 and ec == 0),
                                       stop=(i == npl - 1
                                             and ec == ECN - 1)),
                                   wait=w,
                                   inc=([(PE, 1)] if mark else []))
                            if mark:
                                if i == npl - 1:
                                    dnp[hf] = PE.n
                                if hf == 1:
                                    dnh1_pe[(pi, ci, i)] = PE.n

                def emit_evac(hf, dnp=dnp, ci=ci, ldx_w=ldx_w, ac_w=ac_w,
                              sl=sl):
                    hidx = 2 * ci + hf
                    hb = hidx % 2
                    stw = ac_st.get((pi, hidx - 2),
                                    ac_st.get((pi - 1, 2 * NTCH - 2 + hidx)))
                    if pi == 0:
                        w0 = [(PE, dnp[hf]), (SE, None)]
                        if stw is not None:
                            w0.append(stw)
                        p.emit("scalar", lambda s, hb=hb: s.copy(
                            accout[hb][:],
                            ps_dn[:].rearrange("p (a b) -> p a b", b=512)),
                            wait=w0, inc=[(SE, 1)])
                        evac_mark[(pi, hidx)] = ("se", SE.n)
                    elif pi < NP - 1:
                        w0 = [(PE, dnp[hf]), (VE, None), ac_w[hf]]
                        if stw is not None:
                            w0.append(stw)
                        p.emit("vector", lambda v, hb=hb: v.tensor_tensor(
                            accout[hb][:], accin[hb][:],
                            ps_dn[:].rearrange("p (a b) -> p a b", b=512),
                            ALU.add),
                            wait=w0, inc=[(VE, 1)])
                        evac_mark[(pi, hidx)] = ("ve", VE.n)
                    if pi < NP - 1:
                        ev = evac_mark[(pi, hidx)]
                        p.emit("scalar", lambda s, sl=sl, hb=hb, hf=hf:
                               s.dma_start(
                                   out=af_[:, hf * DC2:(hf + 1) * DC2, sl],
                                   in_=accout[hb][:]),
                               wait=[(SE if ev[0] == "se" else VE, ev[1])] +
                                    ([stw] if stw is not None else []),
                               inc=[(SAC[hb], 16)])
                        ac_st[(pi, hidx)] = (SAC[hb], SAC[hb].n)
                    else:
                        ow = out_st.get(hidx - 1)
                        for hcl in range(DC2):
                            hc = hf * DC2 + hcl
                            w0 = [(PE, dnp[hf]), (VE, None)]
                            if hcl == 0:
                                w0 += [ldx_w[hf], ac_w[hf],
                                       (VE, P0["mod_ve"])]
                                if ow is not None:
                                    w0.append(ow)
                            p.emit("vector",
                                   lambda v, hcl=hcl, hb=hb: v.tensor_tensor(
                                       outc[0][:, hcl, :],
                                       accin[hb][:, hcl, :],
                                       ps_dn[:, hcl * 512:(hcl + 1) * 512],
                                       ALU.add),
                                   wait=w0, inc=[(VE, 1)])
                            p.emit("vector",
                                   lambda v, hcl=hcl, hc=hc, hb=hb:
                                   v.scalar_tensor_tensor(
                                       outc[0][:, hcl, :],
                                       outc[0][:, hcl, :],
                                       modfm[:, 5, hc:hc + 1],
                                       x1cb[hb][:, hcl, :],
                                       ALU.mult, ALU.add),
                                   wait=[(VE, None)], inc=[(VE, 1)])
                        comb_done[hidx] = VE.n
                        evac_mark[(pi, hidx)] = ("ve", VE.n)
                        p.emit("scalar", lambda s, sl=sl, hb=hb, hf=hf:
                               s.dma_start(
                                   out=of[:, hf * DC2:(hf + 1) * DC2, sl],
                                   in_=outc[0][:]),
                               wait=[(VE, None)] +
                                    ([ow] if ow is not None else []),
                               inc=[(STO[hb], 16)])
                        out_st[hidx] = (STO[hb], STO[hb].n)

                emit_up(0)
                if pending is not None:
                    pending()
                    pending = None
                emit_half_loads(1)
                emit_act(0, plist[0])
                for i in range(1, npl):
                    emit_up(i)
                    emit_act(i, plist[i])
                    emit_dn_half(0, i - 1)
                emit_dn_half(0, npl - 1)
                emit_evac(0)

                def mk_pending(emit_dn_half=emit_dn_half,
                               emit_evac=emit_evac, npl=npl):
                    def run():
                        for i in range(npl):
                            emit_dn_half(1, i)
                        emit_evac(1)
                    return run

                pending = mk_pending()
                chunk_pe[(pi, ci)] = PE.n
            if pending is not None:
                pending()
                pending = None
            pass_pe[pi] = PE.n
        p.emit("sync", None, wait=[STO[0], STO[1]])
        es.close()

    p4()

    with ctx, nc.Block() as block:
        p.build(block)
    return nc


def _prep_consts(cfg, inputs):
    wqkv = np.asarray(inputs["L0_attn_w_qkv"], dtype=np.float32)
    D = cfg.D
    hhq, hhk = _householders(cfg)
    return {
        "wqkT": np.ascontiguousarray(wqkv[:, :2 * D].T),
        "wv": np.ascontiguousarray(wqkv[:, 2 * D:3 * D]),
        "wout": np.ascontiguousarray(
            np.asarray(inputs["L0_attn_w_out"], dtype=np.float32)),
        "hhq": np.ascontiguousarray(hhq.reshape(cfg.H * cfg.HD, cfg.HD)),
        "hhk": np.ascontiguousarray(hhk.reshape(cfg.H * cfg.HD, cfg.HD)),
        "hs": np.ascontiguousarray(
            np.asarray(inputs["L0_attn_head_scales"],
                       dtype=np.float32).reshape(1, cfg.H)),
        "wmod": np.ascontiguousarray(
            np.asarray(inputs["L0_wmod"], dtype=np.float32)),
        "wsu": np.ascontiguousarray(
            np.asarray(inputs["L0_ffn_shared_up"], dtype=np.float32)
            .transpose(0, 2, 1).reshape(cfg.NSE * D, cfg.E)),
        "wsd": np.ascontiguousarray(
            np.asarray(inputs["L0_ffn_shared_down"], dtype=np.float32)
            .transpose(0, 2, 1).reshape(cfg.NSE * cfg.E, D)),
        "wru": np.ascontiguousarray(
            np.asarray(inputs["L0_ffn_routed_up"], dtype=np.float32)
            .transpose(0, 2, 1).reshape(cfg.NRE * D, cfg.E)),
        "wrd": np.ascontiguousarray(
            np.asarray(inputs["L0_ffn_routed_down"], dtype=np.float32)
            .transpose(0, 2, 1).reshape(cfg.NRE * cfg.E, D)),
        "wrt": np.ascontiguousarray(
            np.asarray(inputs["L0_ffn_router_w"], dtype=np.float32).T),
        "identin": np.eye(128, dtype=np.float32),
    }


def _prep_core(cfg, inputs, core, consts):
    b, par = core // 2, core % 2
    x = np.asarray(inputs["L0_attn_x"], dtype=np.float32)[b]
    qr = _qrows(cfg, par)
    m = {
        "xT": np.ascontiguousarray(
            np.concatenate([x.T, x[qr].T], axis=1)),
        "cfm": np.ascontiguousarray(
            np.asarray(inputs["L0_c"], dtype=np.float32)[b]
            .reshape(cfg.DC, 128).T),
        "mask01": _mask01(par),
    }
    m.update(consts)
    return m


def kernel(**inputs):
    cfg = _Cfg()
    consts = _prep_consts(cfg, inputs)
    in_maps = [_prep_core(cfg, inputs, c, consts) for c in range(cfg.NCORES)]
    nc = _build(cfg)
    res = run_bass_kernel_spmd(nc, in_maps, core_ids=list(range(cfg.NCORES)))
    out = np.empty((cfg.NB, cfg.TKV, cfg.D), dtype=np.float32)
    for c in range(cfg.NCORES):
        b, par = c // 2, c % 2
        out[b, _qrows(cfg, par)] = np.asarray(res.results[c]["outT"]).T
    return out


# version 16
# speedup vs baseline: 1.0317x; 1.0317x over previous
import math
import numpy as np
import ml_dtypes
from contextlib import ExitStack

import concourse.bass as bass
import concourse.mybir as mybir
from concourse.bass_utils import run_bass_kernel_spmd

F32 = mybir.dt.float32
BF16 = mybir.dt.bfloat16
AF = mybir.ActivationFunctionType
ALU = mybir.AluOpType
AX = mybir.AxisListType.X

ENGINES = ("tensor", "vector", "scalar", "gpsimd", "sync")


class _Sem:
    def __init__(self, nc, name):
        self._cm = nc.semaphore(name)
        self.h = self._cm.__enter__()
        self.n = 0
        self.name = name


class _Prog:

    def __init__(self, nc):
        self.nc = nc
        self.ops = {e: [] for e in ENGINES}

    def sem(self, name):
        return _Sem(self.nc, name)

    def emit(self, engine, fn, wait=(), inc=()):
        wv = []
        for w in wait:
            if w is None:
                continue
            if isinstance(w, _Sem):
                wv.append((w, w.n))
            else:
                s, v = w
                wv.append((s, s.n if v is None else v))
        iv = []
        for s, k in inc:
            s.n += k
            iv.append((s, k))
        self.ops[engine].append((wv, fn, iv))

    def fence(self, marks):
        for e in ENGINES:
            self.emit(e, None, wait=marks)

    def build(self, block):
        def mk(engine):
            def body(eng):
                for wv, fn, iv in self.ops[engine]:
                    for s, v in wv:
                        if v > 0:
                            eng.wait_ge(s.h, v)
                    if fn is not None:
                        instr = fn(eng)
                        for s, k in iv:
                            instr = instr.then_inc(s.h, k)
            return body

        for engine in ENGINES:
            if self.ops[engine]:
                getattr(block, engine)(mk(engine))


class _Cfg:
    def __init__(self, D=1024, H=8, E=512, NRE=16, NSE=2, TOPK=4, KVT=64,
                 NB=4, eps=1e-5, G=4):
        self.D, self.H, self.E = D, H, E
        self.NRE, self.NSE, self.TOPK = NRE, NSE, TOPK
        self.NE = NRE + NSE
        self.KVT, self.NB = KVT, NB
        self.eps = eps
        self.HD = 128
        assert D == H * self.HD
        self.DC = D // 128
        self.ECN = E // 128
        self.TKV = KVT * 128
        self.NS = KVT // 2
        self.NQ = self.NS * 128
        self.NG = self.NS // 4
        assert self.NS % 4 == 0
        self.TTOT = self.TKV + self.NQ
        assert self.TTOT % 512 == 0
        self.NCH = self.TTOT // 512
        self.KVCH = self.TKV // 512
        self.QCH = self.NQ // 512
        self.NTC3 = self.NQ // 512
        self.NTC4 = self.NQ // 256
        self.NCORES = 2 * NB
        assert H % 2 == 0
        self.NPAIR = H // 2
        ids = list(range(self.NE))
        self.passes = [ids[i:i + G] for i in range(0, self.NE, G)]
        self.scale = 1.0 / math.sqrt(self.HD)


def _dirs(n):
    t = np.array([[(i >> 2 & 1) * 2 - 1, (i >> 1 & 1) * 2 - 1, (i & 1) * 2 - 1]
                  for i in range(8)], dtype=np.float32)
    return (t / np.sqrt(3.0, dtype=np.float32))[:n]


def _householders(cfg):
    H, HD, s = cfg.H, cfg.HD, cfg.scale
    d = _dirs(H)
    hhq = np.zeros((H, HD, HD), dtype=np.float32)
    hhk = np.zeros((H, HD, HD), dtype=np.float32)
    for h in range(H):
        u = np.zeros(HD, dtype=np.float64)
        u[:3] = d[h].astype(np.float64)
        v = u.copy()
        v[0] -= 1.0
        Hm = np.eye(HD) - 2.0 * np.outer(v, v) / (v @ v)
        hhq[h] = Hm.astype(np.float32)
        Hk = Hm.copy()
        Hk[:, 1:] *= s
        hhk[h] = Hk.astype(np.float32)
    return hhq, hhk


def _qrows(cfg, p):
    return np.concatenate(
        [np.arange(128 * (2 * i + p), 128 * (2 * i + p) + 128)
         for i in range(cfg.NS)])


def _mask01(p):
    k = np.arange(128)[:, None]
    q = np.arange(128)[None, :]
    tri = (k <= q).astype(np.float32)
    if p == 1:
        a, b = np.ones((128, 128), np.float32), tri
    else:
        a, b = tri, np.zeros((128, 128), np.float32)
    return np.ascontiguousarray(np.concatenate([a, b], axis=1))


def _active(cfg, g, kt):
    act = 0
    for j in range(4):
        if kt < 8 * g + 2 * j + 2:
            act += 1
    return act


def _build(cfg):
    nc = bass.Bass("TRN2", target_bir_lowering=False)
    D, DC, H, HD, E, ECN = cfg.D, cfg.DC, cfg.H, cfg.HD, cfg.E, cfg.ECN

    def din(name, shape, dt=F32):
        return nc.dram_tensor(name, shape, dt, kind="ExternalInput").ap()

    xT = din("xT", [D, cfg.TTOT])
    wqkT = din("wqkT", [2 * D, D])
    wv_d = din("wv", [D, D])
    wout_d = din("wout", [D, D])
    hhq_d = din("hhq", [H * HD, HD])
    hhk_d = din("hhk", [H * HD, HD])
    hs_d = din("hs", [1, H])
    wmod_d = din("wmod", [D, 6 * D])
    cfm_d = din("cfm", [128, DC])
    wsu_d = din("wsu", [cfg.NSE * D, E])
    wsd_d = din("wsd", [cfg.NSE * E, D])
    wru_d = din("wru", [cfg.NRE * D, E])
    wrd_d = din("wrd", [cfg.NRE * E, D])
    wrt_d = din("wrt", [D, cfg.NRE])
    mask_d = din("mask01", [128, 256])
    iden_d = din("identin", [128, 128])

    outT = nc.dram_tensor("outT", [D, cfg.NQ], F32, kind="ExternalOutput").ap()
    h1T = nc.dram_tensor("h1T", [D, cfg.TTOT], BF16, kind="Internal").ap()
    h2T = nc.dram_tensor("h2T", [D, cfg.NQ], BF16, kind="Internal").ap()
    x1T = nc.dram_tensor("x1T", [D, cfg.NQ], F32, kind="Internal").ap()
    atT = nc.dram_tensor("atT", [D, cfg.NQ], BF16, kind="Internal").ap()
    gate_hbm = nc.dram_tensor("gate_hbm", [cfg.NRE, cfg.NQ], BF16,
                              kind="Internal").ap()
    accT = nc.dram_tensor("accT", [D, cfg.NQ], BF16, kind="Internal").ap()
    NEX = cfg.NRE + cfg.NSE
    wub_h = nc.dram_tensor("wub", [NEX * D, E], BF16, kind="Internal").ap()
    wdb_h = nc.dram_tensor("wdb", [NEX * E, D], BF16, kind="Internal").ap()

    def fmr(ap_):
        return ap_.rearrange("(c p) t -> p c t", p=128)

    ctx = ExitStack()

    def sb(name, shape, dt=F32):
        return ctx.enter_context(nc.sbuf_tensor(name, shape, dt))

    p = _Prog(nc)
    LDW_ = p.sem("ldw")
    LDC = p.sem("ldc")
    PE = p.sem("pe")
    VE = p.sem("ve")
    SE = p.sem("se")
    GP = p.sem("gp")
    STH1 = [p.sem(f"sth1_{i}") for i in range(2)]
    STA = [p.sem(f"sta_{i}") for i in range(2)]
    STH2 = [p.sem(f"sth2_{i}") for i in range(2)]
    STX1 = [p.sem(f"stx1_{i}") for i in range(2)]
    STO = [p.sem(f"sto_{i}") for i in range(2)]
    LX = [p.sem(f"lx_{i}") for i in range(2)]
    LXB = [p.sem(f"lxb_{i}") for i in range(2)]
    LH1 = [p.sem(f"lh1_{i}") for i in range(2)]
    LAT = [p.sem(f"lat_{i}") for i in range(2)]
    LH2 = [p.sem(f"lh2_{i}") for i in range(2)]
    LX1 = [p.sem(f"lx1_{i}") for i in range(2)]
    LGR = p.sem("lgr")
    LFW = [p.sem(f"lfw_{i}") for i in range(8)]
    LGA = [p.sem(f"lga_{i}") for i in range(2)]
    LAC = [p.sem(f"lac_{i}") for i in range(2)]
    SAC = [p.sem(f"sac_{i}") for i in range(2)]

    onescol = sb("onescol", [128, 1], BF16)
    ones128 = sb("ones128", [128, 128], BF16)
    epscol = sb("epscol", [128, 1], F32)
    neghalf = sb("neghalf", [128, 1], F32)
    negone = sb("negone", [128, 1], F32)
    onesrow = sb("onesrow", [1, 128], F32)
    onesrow_bf = sb("onesrow_bf", [1, 128], BF16)
    one1 = sb("one1", [1, 1], F32)
    ident = sb("ident", [128, 128], F32)
    maskb = sb("maskb", [128, 256], BF16)
    modfm = sb("modfm", [128, 6, DC], F32)
    hsc = sb("hsc", [128, H], F32)
    ctx_w3 = ExitStack()
    sbw3 = lambda n, s_, d=F32: ctx_w3.enter_context(
        nc.sbuf_tensor(n, s_, d, side="right"))
    woutb = sbw3("woutb", [128, DC, D], BF16)
    wrtb = sbw3("wrtb", [128, DC, cfg.NRE], BF16)
    ctx_w2 = ExitStack()
    sbw2 = lambda n, s_, d=F32: ctx_w2.enter_context(
        nc.sbuf_tensor(n, s_, d, side="right"))
    wqrot = sbw2("wqrot", [128, H, DC, 128], BF16)
    wkrot = sbw2("wkrot", [128, H, DC, 128], BF16)
    wvb = sbw2("wvb", [128, DC, D], BF16)

    def p0():
        p.emit("vector", lambda v: v.memset(onescol[:], 1.0), inc=[(VE, 1)])
        p.emit("vector", lambda v: v.memset(ones128[:], 1.0), inc=[(VE, 1)])
        p.emit("vector", lambda v: v.memset(epscol[:], float(cfg.eps)),
               inc=[(VE, 1)])
        p.emit("vector", lambda v: v.memset(neghalf[:], -0.5), inc=[(VE, 1)])
        p.emit("vector", lambda v: v.memset(negone[:], -1.0), inc=[(VE, 1)])
        p.emit("vector", lambda v: v.memset(onesrow[:], 1.0), inc=[(VE, 1)])
        p.emit("vector", lambda v: v.memset(one1[:], 1.0), inc=[(VE, 1)])
        p.emit("vector", lambda v: v.memset(onesrow_bf[:], 1.0), inc=[(VE, 1)])
        const_ve = VE.n
        p.emit("sync", lambda g: g.dma_start(out=ident[:], in_=iden_d[:]),
               inc=[(LDW_, 16)])
        p.emit("gpsimd", lambda g: g.dma_start(out=maskb[:], in_=mask_d[:]),
               inc=[(LDC, 16)])
        p.emit("gpsimd", lambda g: g.dma_start(
            out=wrtb[:], in_=wrt_d.rearrange("(c p) r -> p c r", p=128)),
            inc=[(LDC, 16)])
        p.emit("gpsimd", lambda g: g.dma_start(out=wvb[:], in_=fmr(wv_d)),
               inc=[(LDC, 16)])
        p.emit("gpsimd", lambda g: g.dma_start(out=woutb[:], in_=fmr(wout_d)),
               inc=[(LDC, 16)])

        es = ExitStack()
        sbl = lambda n, s_, d=F32: es.enter_context(nc.sbuf_tensor(n, s_, d))
        psl = lambda n, s_, d=F32: es.enter_context(nc.psum_tensor(n, s_, d))
        cfm = sbl("p0_cfm", [128, DC])
        mod_row = sbl("p0_mod_row", [1, 6 * D])
        csg = sbl("p0_csg", [128, DC])
        scb = sbl("p0_scb", [128, DC], BF16)
        hs_sb = sbl("p0_hs", [1, H])
        nmodw = 4 if (6 * D // 512) % 4 == 0 else 1
        wmod_sb = sbl("p0_wmod", [128, DC, (6 * D) // nmodw], BF16)
        hhq_sb = sbl("p0_hhq", [128, H, HD])
        hhk_sb = sbl("p0_hhk", [128, H, HD])
        hhkb_sb = sbl("p0_hhkb", [128, H, HD], BF16)
        hhqb_sb = sbl("p0_hhqb", [128, H, HD], BF16)
        wqkT_sb = sbl("p0_wqkT", [128, 2 * H, D], BF16)
        gsc = sbl("p0_gsc", [128, DC])
        ps_mod = [psl(f"p0_ps_mod{i}", [1, 512]) for i in range(2)]
        ps_fm = psl("p0_ps_fm", [128, 64])
        ps_hs = psl("p0_ps_hs", [128, H])
        ps_rot = [psl(f"p0_ps_rot{i}", [128, 512]) for i in range(2)]

        p.emit("sync", lambda g: g.dma_start(out=cfm[:], in_=cfm_d[:]),
               inc=[(LDW_, 16)])
        p.emit("sync", lambda g: g.dma_start(out=hs_sb[:], in_=hs_d[:]),
               inc=[(LDW_, 16)])
        p.emit("sync", lambda g: g.dma_start(
            out=hhq_sb[:], in_=hhq_d.rearrange("(h p) c -> p h c", p=128)),
            inc=[(LDW_, 16)])
        p.emit("sync", lambda g: g.dma_start(
            out=hhk_sb[:], in_=hhk_d.rearrange("(h p) c -> p h c", p=128)),
            inc=[(LDW_, 16)])
        p.emit("gpsimd", lambda g: g.dma_start(
            out=wqkT_sb[:], in_=wqkT.rearrange("(c p) d -> p c d", p=128)),
            inc=[(LDC, 16)])
        cfm_ld = hs_ld = hh_ld = LDW_.n
        mask_ld = wrt_ld = wv_ld = wout_ld = wqk_ld = LDC.n
        WMD = p.sem("wmd")

        p.emit("scalar", lambda s: s.activation(csg[:], cfm[:], AF.Sigmoid),
               wait=[(LDW_, cfm_ld)], inc=[(SE, 1)])
        p.emit("vector", lambda v: v.tensor_mul(csg[:], csg[:], cfm[:]),
               wait=[SE, (VE, None)], inc=[(VE, 1)])
        p.emit("vector", lambda v: v.tensor_copy(scb[:], csg[:]),
               wait=[(VE, None)], inc=[(VE, 1)])
        scb_ve = VE.n

        p.emit("tensor", lambda t: t.matmul(ps_hs[:], onesrow[:], hs_sb[:],
                                            start=True, stop=True),
               wait=[(LDW_, hs_ld), (VE, const_ve)], inc=[(PE, 1)])
        p.emit("vector", lambda v: v.tensor_scalar(
            hsc[:], ps_hs[:], cfg.scale, None, ALU.add),
            wait=[PE, (VE, None)], inc=[(VE, 1)])
        hsc_ve = VE.n
        for h in range(H):
            p.emit("vector", lambda v, h=h: v.tensor_tensor(
                hhk_sb[:, h, 0:1], hhk_sb[:, h, 0:1], hsc[:, h:h + 1],
                ALU.mult),
                wait=[(VE, hsc_ve), (LDW_, hh_ld), (VE, None)], inc=[(VE, 1)])
        p.emit("vector", lambda v: v.tensor_copy(hhkb_sb[:], hhk_sb[:]),
               wait=[(VE, None)], inc=[(VE, 1)])
        p.emit("vector", lambda v: v.tensor_copy(hhqb_sb[:], hhq_sb[:]),
               wait=[(LDW_, hh_ld), (VE, None)], inc=[(VE, 1)])
        hhk_ve = VE.n

        nmodch = 6 * D // 512
        wavech = nmodch // nmodw
        evac_ve = {}
        wmodf = wmod_d.rearrange("(c p) f -> p c f", p=128)
        for wv_ in range(nmodw):
            wlo = wv_ * (6 * D // nmodw)
            p.emit("gpsimd", lambda g, wlo=wlo: g.dma_start(
                out=wmod_sb[:], in_=wmodf[:, :, wlo:wlo + (6 * D) // nmodw]),
                wait=([(PE, None)] if wv_ > 0 else []),
                inc=[(WMD, 16)])
            wmod_ld = WMD.n
            for fi in range(wavech):
                f = wv_ * wavech + fi
                par = f % 2
                w0 = [(VE, scb_ve), (WMD, wmod_ld)]
                if f >= 2:
                    w0.append((VE, evac_ve[f - 2]))
                for dc in range(DC):
                    p.emit("tensor", lambda t, dc=dc, fi=fi, par=par: t.matmul(
                        ps_mod[par][:], scb[:, dc:dc + 1],
                        wmod_sb[:, dc, fi * 512:(fi + 1) * 512],
                        start=(dc == 0), stop=(dc == DC - 1)),
                        wait=(w0 if dc == 0 else []),
                        inc=([(PE, 1)] if dc == DC - 1 else []))
                p.emit("vector", lambda v, f=f, par=par: v.tensor_copy(
                    mod_row[:, f * 512:(f + 1) * 512], ps_mod[par][:]),
                    wait=[PE, (VE, None)], inc=[(VE, 1)])
                evac_ve[f] = VE.n
        modrow_ve = VE.n

        n6 = 6 * DC
        for j in range(n6):
            p.emit("tensor", lambda t, j=j: t.matmul(
                ps_fm[:, j:j + 1], mod_row[:, j * 128:(j + 1) * 128], one1[:],
                start=True, stop=True),
                wait=([(VE, modrow_ve), (VE, const_ve)] if j == 0 else []),
                inc=[(PE, 1)])
        p.emit("scalar", lambda s: s.copy(
            modfm[:].rearrange("p a b -> p (a b)"), ps_fm[:, 0:n6]),
            wait=[PE, (SE, None)], inc=[(SE, 1)])
        raw_se = SE.n

        for vi in (1, 4):
            p.emit("vector", lambda v, vi=vi: v.tensor_scalar_add(
                modfm[:, vi, :], modfm[:, vi, :], 1.0),
                wait=[(SE, raw_se), (VE, None)], inc=[(VE, 1)])
        for vi in (2, 5):
            p.emit("vector", lambda v, vi=vi: v.tensor_scalar(
                gsc[:], modfm[:, vi, :], 0.0, None, ALU.is_ge),
                wait=[(SE, raw_se), (VE, None)], inc=[(VE, 1)])
            p.emit("vector", lambda v: v.tensor_scalar(
                gsc[:], gsc[:], 2.0, -1.0, ALU.mult, ALU.add),
                wait=[(VE, None)], inc=[(VE, 1)])
            p.emit("vector", lambda v, vi=vi: v.tensor_tensor(
                modfm[:, vi, :], modfm[:, vi, :], gsc[:], ALU.mult),
                wait=[(VE, None)], inc=[(VE, 1)])
            p.emit("vector", lambda v, vi=vi: v.tensor_scalar_max(
                modfm[:, vi, :], modfm[:, vi, :], 0.25),
                wait=[(VE, None)], inc=[(VE, 1)])
            p.emit("vector", lambda v, vi=vi: v.tensor_tensor(
                modfm[:, vi, :], modfm[:, vi, :], gsc[:], ALU.mult),
                wait=[(VE, None)], inc=[(VE, 1)])
        mod_ve = VE.n

        evac_se = {}
        ri = 0
        for side in range(2):
            hh_sb_ = hhqb_sb if side == 0 else hhkb_sb
            dst = wqrot if side == 0 else wkrot
            for h in range(H):
                for d0 in range(0, DC, 4):
                    dcs = list(range(d0, min(d0 + 4, DC)))
                    par = ri % 2
                    w0 = [(VE, hhk_ve), (LDC, wqk_ld)]
                    if ri >= 2:
                        w0.append((SE, evac_se[ri - 2]))
                    crow = h if side == 0 else H + h
                    for i, dc in enumerate(dcs):
                        p.emit("tensor",
                               lambda t, i=i, dc=dc, crow=crow, hh_sb_=hh_sb_,
                               h=h, par=par:
                               t.matmul(ps_rot[par][:, i * 128:(i + 1) * 128],
                                        wqkT_sb[:, crow,
                                                dc * 128:(dc + 1) * 128],
                                        hh_sb_[:, h, :],
                                        start=True, stop=True),
                               wait=(w0 if i == 0 else []),
                               inc=([(PE, 1)] if i == len(dcs) - 1 else []))
                    p.emit("scalar",
                           lambda s, dst=dst, h=h, d0=d0, n=len(dcs), par=par:
                           s.copy(dst[:, h, d0:d0 + n, :],
                                  ps_rot[par][:, 0:n * 128]),
                           wait=[PE, (SE, None)], inc=[(SE, 1)])
                    evac_se[ri] = SE.n
                    ri += 1
        rot_se = SE.n
        es.close()
        return dict(mod_ve=mod_ve, rot_se=rot_se, mask_ld=mask_ld,
                    wrt_ld=wrt_ld, wv_ld=wv_ld, wout_ld=wout_ld,
                    const_ve=const_ve)

    P0 = p0()
    p.fence([(PE, None), (VE, None), (SE, None), (GP, None), LDW_, LDC])

    def ln_phase(prefix, src_of, n_ch, dst_of, ca_idx, sa_idx, st_sems,
                 src_wait_of=None):
        es = ExitStack()
        sbl = lambda n, s_, d=F32: es.enter_context(
            nc.sbuf_tensor(prefix + n, s_, d))
        psl = lambda n, s_, d=F32: es.enter_context(
            nc.psum_tensor(prefix + n, s_, d))
        xb = [sbl(f"xb{i}", [128, DC, 512], BF16) for i in range(2)]
        xq = [sbl(f"xq{i}", [128, DC, 512], BF16) for i in range(2)]
        h_c = [sbl(f"h{i}", [128, DC, 512], BF16) for i in range(2)]
        rf = [sbl(f"rf{i}", [128, 3, 512]) for i in range(2)]
        rb = [sbl(f"rb{i}", [128, 2, 512], BF16) for i in range(2)]
        t1b = sbl("t1b", [128, DC, 512], BF16)
        ps_s = [psl(f"ps_s{i}", [128, 512]) for i in range(2)]
        ps_q = [psl(f"ps_q{i}", [128, 512]) for i in range(2)]

        dve_done, st_done, aff_done = {}, {}, {}
        ld_marks = {}

        def emit_loads(ci):
            pr = ci % 2
            w_free = ([(VE, dve_done[ci - 2])]
                      if ci >= 2 else [])
            w_src = src_wait_of(ci) if src_wait_of else []
            p.emit("gpsimd", lambda g, ci=ci, pr=pr: g.dma_start(
                out=xb[pr][:], in_=src_of(ci)),
                wait=w_free + w_src, inc=[(LXB[pr], 16)])
            ld_marks[ci] = (LXB[pr], LXB[pr].n)

        emit_loads(0)
        if n_ch > 1:
            emit_loads(1)
        for ci in range(n_ch):
            pr = ci % 2
            ldc_w = ld_marks[ci]
            p.emit("scalar", lambda s, pr=pr: s.activation(
                xq[pr][:], xb[pr][:], AF.Square),
                wait=[ldc_w, (SE, None)], inc=[(SE, 1)])
            sq_se = SE.n
            pe_w = [ldc_w, (VE, P0["const_ve"])] + \
                ([(VE, dve_done[ci - 2])] if ci >= 2 else [])
            for dc in range(DC):
                p.emit("tensor", lambda t, dc=dc, pr=pr: t.matmul(
                    ps_s[pr][:], ones128[:], xb[pr][:, dc, :],
                    start=(dc == 0), stop=(dc == DC - 1)),
                    wait=(pe_w if dc == 0 else []))
            for dc in range(DC):
                p.emit("tensor", lambda t, dc=dc, pr=pr: t.matmul(
                    ps_q[pr][:], ones128[:], xq[pr][:, dc, :],
                    start=(dc == 0), stop=(dc == DC - 1)),
                    wait=([(SE, sq_se)] if dc == 0 else []),
                    inc=([(PE, 1)] if dc == DC - 1 else []))
            pe_stats = PE.n
            r, rb_ = rf[pr], rb[pr]
            rw = [(VE, dve_done[ci - 2])] if ci >= 2 else []
            p.emit("vector", lambda v, r=r, pr=pr: v.tensor_scalar(
                r[:, 0, :], ps_s[pr][:], 1.0 / D, None, ALU.mult),
                wait=[(PE, pe_stats), (VE, None)] + rw, inc=[(VE, 1)])
            p.emit("vector", lambda v, r=r, pr=pr: v.tensor_scalar(
                r[:, 1, :], ps_q[pr][:], 1.0 / D, None, ALU.mult),
                wait=[(VE, None)], inc=[(VE, 1)])
            p.emit("vector", lambda v, r=r: v.tensor_tensor(
                r[:, 2, :], r[:, 0, :], r[:, 0, :], ALU.mult),
                wait=[(VE, None)], inc=[(VE, 1)])
            p.emit("vector", lambda v, r=r: v.tensor_tensor(
                r[:, 2, :], r[:, 1, :], r[:, 2, :], ALU.subtract),
                wait=[(VE, None)], inc=[(VE, 1)])
            p.emit("scalar", lambda s, r=r: s.activation(
                r[:, 2, :], r[:, 2, :], AF.Ln, bias=epscol[:]),
                wait=[VE, (SE, None)], inc=[(SE, 1)])
            p.emit("scalar", lambda s, r=r, rb_=rb_: s.activation(
                rb_[:, 0, :], r[:, 2, :], AF.Exp, scale=neghalf[:]),
                wait=[(SE, None)], inc=[(SE, 1)])
            rstd_se = SE.n
            p.emit("vector", lambda v, r=r, rb_=rb_: v.scalar_tensor_tensor(
                rb_[:, 1, :], r[:, 0, :], -1.0, rb_[:, 0, :],
                ALU.mult, ALU.mult),
                wait=[(SE, rstd_se), (VE, None)], inc=[(VE, 1)])
            rowm = VE.n
            p.emit("vector", lambda v, pr=pr, rb_=rb_: v.tensor_tensor(
                t1b[:], xb[pr][:],
                rb_[:, 0, :].unsqueeze(1).to_broadcast([128, DC, 512]),
                ALU.mult),
                wait=[(VE, None), ldc_w, (VE, rowm), (VE, P0["mod_ve"])],
                inc=[(VE, 1)])
            p.emit("vector", lambda v, rb_=rb_: v.tensor_tensor(
                t1b[:], t1b[:],
                rb_[:, 1, :].unsqueeze(1).to_broadcast([128, DC, 512]),
                ALU.add),
                wait=[(VE, None)], inc=[(VE, 1)])
            for dc in range(DC):
                hw = [st_done[ci - 2]] if (dc == 0 and ci >= 2) else []
                p.emit("vector", lambda v, dc=dc, pr=pr:
                       v.tensor_scalar(
                           h_c[pr][:, dc, :], t1b[:, dc, :],
                           modfm[:, ca_idx, dc:dc + 1],
                           modfm[:, sa_idx, dc:dc + 1],
                           ALU.mult, ALU.add),
                       wait=[(VE, None)] + hw, inc=[(VE, 1)])
            dve_done[ci] = VE.n
            aff_done[ci] = VE.n
            if ci + 2 < n_ch:
                emit_loads(ci + 2)
            p.emit("sync", lambda g, ci=ci, pr=pr: g.dma_start(
                out=dst_of(ci), in_=h_c[pr][:]),
                wait=[(VE, aff_done[ci])] +
                     ([st_done[ci - 2]] if ci >= 2 else []),
                inc=[(st_sems[pr], 16)])
            st_done[ci] = (st_sems[pr], st_sems[pr].n)
        es.close()
        return st_done

    p1_st = ln_phase(
        "p1_",
        src_of=lambda ci: fmr(xT)[:, :, ci * 512:(ci + 1) * 512],
        n_ch=cfg.NCH,
        dst_of=lambda ci: fmr(h1T)[:, :, ci * 512:(ci + 1) * 512],
        ca_idx=1, sa_idx=0, st_sems=STH1)
    p.fence([(PE, None), (VE, None), (SE, None), (GP, None),
             p1_st[cfg.NCH - 1], p1_st[cfg.NCH - 2],
             LX[0], LX[1], LXB[0], LXB[1]])

    WCPA = p.sem("wcpa")
    WCPB = p.sem("wcpb")

    def emit_precast(first_wait):
        na = min(len(cfg.passes[0]) + (len(cfg.passes[1])
                                       if len(cfg.passes) > 1 else 0), NEX)
        for r in range(NEX):
            sem = WCPA if r < na else WCPB
            usrc = wru_d if r < cfg.NRE else wsu_d
            dsrc = wrd_d if r < cfg.NRE else wsd_d
            ui = r if r < cfg.NRE else r - cfg.NRE
            p.emit("gpsimd", lambda g, r=r, usrc=usrc, ui=ui: g.dma_start(
                out=wub_h[r * D:(r + 1) * D, :],
                in_=usrc[ui * D:(ui + 1) * D, :]),
                wait=(first_wait if r == 0 else []), inc=[(sem, 16)])
            p.emit("gpsimd", lambda g, r=r, dsrc=dsrc, ui=ui: g.dma_start(
                out=wdb_h[r * E:(r + 1) * E, :],
                in_=dsrc[ui * E:(ui + 1) * E, :]), inc=[(sem, 16)])

    def p2():
        es = ExitStack()
        sbl = lambda n, s_, d=F32: es.enter_context(nc.sbuf_tensor(n, s_, d))
        psl = lambda n, s_, d=F32: es.enter_context(nc.psum_tensor(n, s_, d))
        h1c = [sbl(f"a_h1c{i}", [128, DC, 512], BF16) for i in range(2)]
        Ksb = sbl("a_K", [128, 2, cfg.TKV], BF16)
        Vsb = sbl("a_V", [128, cfg.KVT, 2, 128], BF16)
        Qsb = sbl("a_Q", [128, 2, cfg.NQ], BF16)
        expb = [sbl(f"a_exp{i}", [128, 1024], BF16) for i in range(2)]
        recb = [sbl(f"a_recb{i}", [128, 512]) for i in range(2)]
        aoutg = [sbl(f"a_aout{i}", [128, 512], BF16) for i in range(2)]
        ps_p = [psl(f"a_ps_p{i}", [128, 512]) for i in range(2)]
        ps_t = [psl(f"a_ps_t{i}", [128, 1024]) for i in range(2)]
        ps_l = [psl(f"a_ps_l{i}", [128, 512]) for i in range(2)]
        h1f = fmr(h1T)

        proj_pe = {}
        aout_st = {}
        ag_st = {}
        gctr = 0
        ld2_marks = {}

        def emit_h1load(pair, ci, w_free):
            pr = ci % 2
            p.emit("sync", lambda g, ci=ci, pr=pr: g.dma_start(
                out=h1c[pr][:], in_=h1f[:, :, ci * 512:(ci + 1) * 512]),
                wait=w_free + [p1_st[ci]], inc=[(LH1[pr], 16)])
            ld2_marks[(pair, ci)] = (LH1[pr], LH1[pr].n)
        p.emit("vector", lambda v: v.memset(ps_t[0][:], 0.0),
               wait=[(VE, None)], inc=[(VE, 1)])
        p.emit("vector", lambda v: v.memset(ps_t[1][:], 0.0),
               wait=[(VE, None)], inc=[(VE, 1)])
        groupend_ve = VE.n
        pt_init_ve = VE.n
        prev_exp = {}
        ln_se = {}
        aout_ve = {}
        last_pair = None
        for pair in getattr(cfg, "pair_order", range(cfg.NPAIR)):
            h0 = 2 * pair
            evac_se = {}
            pi = 0
            for ci in range(cfg.KVCH + cfg.QCH):
                pr = ci % 2
                w_free = []
                if ci >= 2:
                    w_free = [(PE, proj_pe[(pair, ci - 2)])]
                elif last_pair is not None:
                    w_free = [(PE, proj_pe[(last_pair,
                                            cfg.KVCH + cfg.QCH - 2 + ci)])]
                if (pair, ci) not in ld2_marks:
                    emit_h1load(pair, ci, w_free)
                ldh_w = ld2_marks[(pair, ci)]
                is_kv = ci < cfg.KVCH

                def proj(lhs_of, rhs_of, out_of, n_acc, extra_w, out_w):
                    nonlocal pi
                    par = pi % 2
                    w0 = list(extra_w)
                    if pi >= 2:
                        w0.append((SE, evac_se[pi - 2]))
                    else:
                        w0.append((VE, max(aout_ve.values(), default=0)))
                    for a in range(n_acc):
                        p.emit("tensor",
                               lambda t, a=a, par=par, lhs_of=lhs_of,
                               rhs_of=rhs_of, out_w=out_w, n_acc=n_acc:
                               t.matmul(ps_p[par][:, 0:out_w],
                                        lhs_of(a), rhs_of(a),
                                        start=(a == 0), stop=(a == n_acc - 1)),
                               wait=(w0 if a == 0 else []),
                               inc=([(PE, 1)] if a == n_acc - 1 else []))
                    p.emit("scalar",
                           lambda s, par=par, out_of=out_of, out_w=out_w:
                           s.copy(out_of(), ps_p[par][:, 0:out_w]),
                           wait=[PE, (SE, None)], inc=[(SE, 1)])
                    evac_se[pi] = SE.n
                    pi += 1

                if is_kv:
                    for hh in range(2):
                        proj(lambda a, hh=hh, h0=h0: wkrot[:, h0 + hh, a, :],
                             lambda a, pr=pr: h1c[pr][:, a, :],
                             lambda hh=hh, ci=ci:
                             Ksb[:, hh, ci * 512:(ci + 1) * 512],
                             DC, [ldh_w, (SE, P0["rot_se"])], 512)
                    for ts_ in range(4):
                        proj(lambda a, pr=pr, ts_=ts_:
                             h1c[pr][:, a, ts_ * 128:(ts_ + 1) * 128],
                             lambda a, h0=h0:
                             wvb[:, a, h0 * 128:(h0 + 2) * 128],
                             lambda ci=ci, ts_=ts_:
                             Vsb[:, ci * 4 + ts_, :, :].rearrange(
                                 "p a b -> p (a b)"),
                             DC, [ldh_w, (LDC, P0["wv_ld"])], 256)
                else:
                    qi = ci - cfg.KVCH
                    for hh in range(2):
                        proj(lambda a, hh=hh, h0=h0: wqrot[:, h0 + hh, a, :],
                             lambda a, pr=pr: h1c[pr][:, a, :],
                             lambda hh=hh, qi=qi:
                             Qsb[:, hh, qi * 512:(qi + 1) * 512],
                             DC, [ldh_w, (SE, P0["rot_se"])], 512)
                proj_pe[(pair, ci)] = PE.n
            tailA = SE.n
            if pair == 0:
                emit_precast([(SE, tailA)])
            nxt = pair + 1
            if nxt < cfg.NPAIR:
                for cj in range(2):
                    emit_h1load(nxt, cj,
                                [(PE, proj_pe[(pair,
                                               cfg.KVCH + cfg.QCH - 2 + cj)])])

            for hh in range(2):
                h = h0 + hh
                for g in range(cfg.NG):
                    ktmax = 8 * g + 8
                    qlo = 4 * g * 128
                    gpar = gctr % 2
                    ps_o_g = ps_p[gpar]
                    ps_l_g = ps_l[gpar]
                    duos = [(k, min(k + 2, ktmax)) for k in range(0, ktmax, 2)]
                    nd = len(duos)
                    qk_pe, exp_se, msk_ve, av_pe = {}, {}, {}, {}
                    pe0 = prev_exp.copy()

                    def emit_qk(di):
                        k0, k1 = duos[di]
                        par = di % 2
                        for kt in range(k0, k1):
                            act = _active(cfg, g, kt)
                            j0 = 4 - act
                            col = (kt - k0) * 512
                            w = []
                            if kt == k0:
                                w = [(SE, tailA)]
                                if di >= 2:
                                    w.append((SE, exp_se[di - 2]))
                                elif (di % 2) in pe0:
                                    w.append((SE, pe0[di % 2]))
                                else:
                                    w.append((VE, pt_init_ve))
                            p.emit("tensor",
                                   lambda t, kt=kt, j0=j0, col=col, par=par,
                                   hh=hh, qlo=qlo, g=g:
                                   t.matmul(
                                       ps_t[par][:, col + j0 * 128:col + 512],
                                       Ksb[:, hh, kt * 128:(kt + 1) * 128],
                                       Qsb[:, hh, qlo + j0 * 128:qlo + 512],
                                       start=True, stop=True),
                                   wait=w,
                                   inc=([(PE, 1)] if kt == k1 - 1 else []))
                        qk_pe[di] = PE.n

                    def emit_exp(di):
                        k0, k1 = duos[di]
                        par = di % 2
                        wid = (k1 - k0) * 512
                        w = [(PE, qk_pe[di]), (SE, None)]
                        if di >= 2:
                            w.append((PE, av_pe[di - 2]))
                            if (di - 2) in msk_ve:
                                w.append((VE, msk_ve[di - 2]))
                        p.emit("scalar", lambda s, par=par, wid=wid:
                               s.activation(expb[par][:, 0:wid],
                                            ps_t[par][:, 0:wid], AF.Exp),
                               wait=w, inc=[(SE, 1)])
                        exp_se[di] = SE.n
                        mm = False
                        for kt in range(k0, k1):
                            if kt >= 8 * g:
                                jstar = (kt - 8 * g) // 2
                                mA = (kt - 8 * g) % 2 == 0
                                col = (kt - k0) * 512 + jstar * 128
                                mo = 0 if mA else 128
                                p.emit("vector",
                                       lambda v, par=par, col=col, mo=mo:
                                       v.tensor_tensor(
                                           expb[par][:, col:col + 128],
                                           expb[par][:, col:col + 128],
                                           maskb[:, mo:mo + 128], ALU.mult),
                                       wait=[(SE, exp_se[di]),
                                             (LDC, P0["mask_ld"]),
                                             (VE, None)],
                                       inc=[(VE, 1)])
                                mm = True
                        if mm:
                            msk_ve[di] = VE.n

                    def emit_av(di):
                        k0, k1 = duos[di]
                        par = di % 2
                        first = True
                        for kt in range(k0, k1):
                            act = _active(cfg, g, kt)
                            j0 = 4 - act
                            col = (kt - k0) * 512
                            w = []
                            if first:
                                w = [(SE, exp_se[di])]
                                if di in msk_ve:
                                    w.append((VE, msk_ve[di]))
                                if kt == 0:
                                    w.append((SE, ln_se.get(gpar, 0)))
                                    w.append((VE, aout_ve.get(gpar, 0)))
                            p.emit("tensor",
                                   lambda t, kt=kt, j0=j0, col=col, par=par,
                                   hh=hh, ktmax=ktmax, ps_o_g=ps_o_g:
                                   t.matmul(
                                       ps_o_g[:, j0 * 128:512],
                                       Vsb[:, kt, hh, :],
                                       expb[par][:, col + j0 * 128:col + 512],
                                       start=(kt == 0),
                                       stop=(kt == ktmax - 1)),
                                   wait=w)
                            p.emit("tensor",
                                   lambda t, kt=kt, j0=j0, col=col, par=par,
                                   ktmax=ktmax, ps_l_g=ps_l_g:
                                   t.matmul(
                                       ps_l_g[:, j0 * 128:512],
                                       ones128[:],
                                       expb[par][:, col + j0 * 128:col + 512],
                                       start=(kt == 0),
                                       stop=(kt == ktmax - 1)),
                                   inc=([(PE, 1)] if kt == k1 - 1 else []))
                            first = False
                        av_pe[di] = PE.n

                    for di in range(nd):
                        emit_qk(di)
                        emit_exp(di)
                        if di >= 1:
                            emit_av(di - 1)
                    emit_av(nd - 1)

                    p.emit("scalar", lambda s, ps_l_g=ps_l_g, gpar=gpar:
                           s.activation(recb[gpar][:], ps_l_g[:], AF.Ln),
                           wait=[(PE, av_pe[nd - 1]), (SE, None),
                                 (VE, aout_ve.get(gpar, 0))],
                           inc=[(SE, 1)])
                    ln_se[gpar] = SE.n
                    p.emit("scalar", lambda s, gpar=gpar:
                           s.activation(recb[gpar][:], recb[gpar][:],
                                        AF.Exp, scale=negone[:]),
                           wait=[(SE, None)], inc=[(SE, 1)])
                    rec_se = SE.n
                    gp = gctr % 2
                    aw = [ag_st[gctr - 2]] if gctr >= 2 else []
                    p.emit("vector", lambda v, gp=gp, gpar=gpar,
                           ps_o_g=ps_o_g:
                           v.tensor_tensor(
                               aoutg[gp][:], ps_o_g[:], recb[gpar][:],
                               ALU.mult),
                           wait=[(SE, rec_se), (VE, None)] + aw,
                           inc=[(VE, 1)])
                    aout_ve[gpar] = VE.n
                    groupend_ve = VE.n
                    prev_exp = {}
                    for dd in (nd - 2, nd - 1):
                        if dd >= 0:
                            prev_exp[dd % 2] = exp_se[dd]
                    p.emit("sync", lambda g_, h=h, g=g, gp=gp: g_.dma_start(
                        out=fmr(atT)[:, h, g * 512:(g + 1) * 512],
                        in_=aoutg[gp][:]),
                        wait=[(VE, groupend_ve)] + aw,
                        inc=[(STA[gp], 16)])
                    ag_st[gctr] = (STA[gp], STA[gp].n)
                    gctr += 1
            last_pair = pair
        es.close()
        return aout_st

    aout_st = p2()
    ctx_w2.close()
    p.fence([(PE, None), (VE, None), (SE, None), (GP, None),
             LH1[0], LH1[1], STA[0], STA[1], LGR])

    def p3():
        es = ExitStack()
        sbl = lambda n, s_, d=F32: es.enter_context(nc.sbuf_tensor(n, s_, d))
        psl = lambda n, s_, d=F32: es.enter_context(nc.psum_tensor(n, s_, d))
        xq = [sbl(f"o_xq{i}", [128, DC, 512]) for i in range(2)]
        ats = [sbl(f"o_at{i}", [128, H, 512], BF16) for i in range(2)]
        x1c = [sbl(f"o_x1{i}", [128, DC, 512]) for i in range(2)]
        x1b = sbl("o_x1b", [128, DC, 512], BF16)
        x1q = sbl("o_x1q", [128, DC, 512], BF16)
        h2c = [sbl(f"o_h2{i}", [128, DC, 512], BF16) for i in range(2)]
        rf3 = sbl("o_rf", [128, 3, 512])
        rb3 = sbl("o_rb", [128, 2, 512], BF16)
        t1b3 = sbl("o_t1b", [128, DC, 512], BF16)
        ps_o = [psl(f"o_ps_o{i}", [128, 512]) for i in range(2)]
        ps_s = [psl(f"o_ps_s{i}", [128, 512]) for i in range(2)]
        ps_q = [psl(f"o_ps_q{i}", [128, 512]) for i in range(2)]
        ps_lg = psl("o_ps_lg", [cfg.NRE, 512])

        atf = fmr(atT)
        xf = fmr(xT)
        dve_done, sth2_done, stx1_done, lg_se = {}, {}, {}, {}
        x1_ve = {}
        gps3_done = {}
        aff3_done = {}
        ld3_marks = {}
        pe_stats_prev = 0

        def emit_loads3(ci):
            pr = ci % 2
            sl = slice(ci * 512, (ci + 1) * 512)
            w_free = [(VE, dve_done[ci - 2]), (GP, gps3_done[ci - 2])] \
                if ci >= 2 else []
            p.emit("sync", lambda g, ci=ci, pr=pr: g.dma_start(
                out=xq[pr][:], in_=xf[:, :, cfg.TKV + ci * 512:
                                      cfg.TKV + (ci + 1) * 512]),
                wait=w_free, inc=[(LX[pr], 16)])
            p.emit("sync", lambda g, ci=ci, pr=pr, sl=sl: g.dma_start(
                out=ats[pr][:], in_=atf[:, :, sl]),
                wait=w_free + [STA[0], STA[1]],
                inc=[(LAT[pr], 16)])
            ld3_marks[ci] = ((LX[pr], LX[pr].n), (LAT[pr], LAT[pr].n))

        emit_loads3(0)
        if cfg.NTC3 > 1:
            emit_loads3(1)
        for ci in range(cfg.NTC3):
            pr = ci % 2
            sl = slice(ci * 512, (ci + 1) * 512)
            ldx_w, ldh_w = ld3_marks[ci]
            oproj_ve = {}
            for fc in range(DC):
                par2 = fc % 2
                w0 = [ldh_w, (LDC, P0["wout_ld"])]
                if fc >= 2:
                    w0.append((VE, oproj_ve[fc - 2]))
                elif ci >= 1:
                    w0.append((VE, x1_ve[ci - 1]))
                if fc == 0 and ci >= 2:
                    w0.append(stx1_done[ci - 2])
                for h in range(H):
                    p.emit("tensor", lambda t, h=h, fc=fc, pr=pr, par2=par2:
                           t.matmul(ps_o[par2][:],
                                    woutb[:, h, fc * 128:(fc + 1) * 128],
                                    ats[pr][:, h, :],
                                    start=(h == 0), stop=(h == H - 1)),
                           wait=(w0 if h == 0 else []),
                           inc=([(PE, 1)] if h == H - 1 else []))
                pe_o = PE.n
                p.emit("vector", lambda v, fc=fc, pr=pr, par2=par2:
                       v.scalar_tensor_tensor(
                           x1c[pr][:, fc, :], ps_o[par2][:],
                           modfm[:, 2, fc:fc + 1], xq[pr][:, fc, :],
                           ALU.mult, ALU.add),
                       wait=[(PE, pe_o), ldx_w, (VE, None),
                             (VE, P0["mod_ve"])],
                       inc=[(VE, 1)])
                oproj_ve[fc] = VE.n
            x1_ve[ci] = VE.n
            xw = [(PE, pe_stats_prev)] if ci >= 1 else []
            if ci >= 1:
                xw.append((VE, dve_done[ci - 1]))
            p.emit("scalar", lambda s, pr=pr: s.copy(x1b[:], x1c[pr][:]),
                   wait=[(VE, x1_ve[ci]), (SE, None)] + xw, inc=[(SE, 1)])
            p.emit("scalar", lambda s: s.activation(
                x1q[:], x1b[:], AF.Square), wait=[(SE, None)], inc=[(SE, 1)])
            sq_se = SE.n
            p.emit("sync", lambda g, pr=pr, sl=sl: g.dma_start(
                out=fmr(x1T)[:, :, sl], in_=x1c[pr][:]),
                wait=[(VE, x1_ve[ci])] +
                     ([stx1_done[ci - 2]] if ci >= 2 else []),
                inc=[(STX1[pr], 16)])
            stx1_done[ci] = (STX1[pr], STX1[pr].n)
            w_st = [(VE, dve_done[ci - 2]), (GP, gps3_done[ci - 2])] \
                if ci >= 2 else []
            for dc in range(DC):
                p.emit("tensor", lambda t, dc=dc, pr=pr: t.matmul(
                    ps_s[pr][:], ones128[:], x1b[:, dc, :],
                    start=(dc == 0), stop=(dc == DC - 1)),
                    wait=([(SE, sq_se)] + w_st if dc == 0 else []))
            for dc in range(DC):
                p.emit("tensor", lambda t, dc=dc, pr=pr: t.matmul(
                    ps_q[pr][:], ones128[:], x1q[:, dc, :],
                    start=(dc == 0), stop=(dc == DC - 1)),
                    inc=([(PE, 1)] if dc == DC - 1 else []))
            pe_stats = PE.n
            pe_stats_prev = pe_stats
            r, rb_ = rf3, rb3
            p.emit("vector", lambda v, r=r, pr=pr: v.tensor_scalar(
                r[:, 0, :], ps_s[pr][:], 1.0 / D, None, ALU.mult),
                wait=[(PE, pe_stats), (VE, None)], inc=[(VE, 1)])
            p.emit("vector", lambda v, r=r, pr=pr: v.tensor_scalar(
                r[:, 1, :], ps_q[pr][:], 1.0 / D, None, ALU.mult),
                wait=[(VE, None)], inc=[(VE, 1)])
            p.emit("vector", lambda v, r=r: v.tensor_tensor(
                r[:, 2, :], r[:, 0, :], r[:, 0, :], ALU.mult),
                wait=[(VE, None)], inc=[(VE, 1)])
            p.emit("vector", lambda v, r=r: v.tensor_tensor(
                r[:, 2, :], r[:, 1, :], r[:, 2, :], ALU.subtract),
                wait=[(VE, None)], inc=[(VE, 1)])
            p.emit("scalar", lambda s, r=r: s.activation(
                r[:, 2, :], r[:, 2, :], AF.Ln, bias=epscol[:]),
                wait=[VE, (SE, None)], inc=[(SE, 1)])
            p.emit("scalar", lambda s, r=r, rb_=rb_: s.activation(
                rb_[:, 0, :], r[:, 2, :], AF.Exp, scale=neghalf[:]),
                wait=[(SE, None)], inc=[(SE, 1)])
            rstd_se = SE.n
            p.emit("vector", lambda v, r=r, rb_=rb_: v.scalar_tensor_tensor(
                rb_[:, 1, :], r[:, 0, :], -1.0, rb_[:, 0, :],
                ALU.mult, ALU.mult),
                wait=[(SE, rstd_se), (VE, None)], inc=[(VE, 1)])
            rowm = VE.n
            hw = [sth2_done[ci - 2]] if ci >= 2 else []
            p.emit("vector", lambda v, rb_=rb_: v.tensor_tensor(
                t1b3[:], x1b[:],
                rb_[:, 0, :].unsqueeze(1).to_broadcast([128, DC, 512]),
                ALU.mult),
                wait=[(VE, rowm), (VE, None)], inc=[(VE, 1)])
            p.emit("vector", lambda v, rb_=rb_: v.tensor_tensor(
                t1b3[:], t1b3[:],
                rb_[:, 1, :].unsqueeze(1).to_broadcast([128, DC, 512]),
                ALU.add),
                wait=[(VE, None)], inc=[(VE, 1)])
            for dc in range(DC):
                p.emit("vector", lambda v, dc=dc, pr=pr:
                       v.tensor_scalar(
                           h2c[pr][:, dc, :], t1b3[:, dc, :],
                           modfm[:, 4, dc:dc + 1],
                           modfm[:, 3, dc:dc + 1],
                           ALU.mult, ALU.add),
                       wait=[(VE, None)] + (hw if dc == 0 else []),
                       inc=[(VE, 1)])
            dve_done[ci] = VE.n
            gps3_done[ci] = GP.n
            aff3_done[ci] = VE.n
            if ci + 2 < cfg.NTC3:
                emit_loads3(ci + 2)
            p.emit("sync", lambda g, pr=pr, sl=sl: g.dma_start(
                out=fmr(h2T)[:, :, sl], in_=h2c[pr][:]),
                wait=[(VE, aff3_done[ci])] +
                     ([sth2_done[ci - 2]] if ci >= 2 else []),
                inc=[(STH2[pr], 16)])
            sth2_done[ci] = (STH2[pr], STH2[pr].n)
            w_lg = [(SE, lg_se[ci - 1])] if ci >= 1 else []
            for dc in range(DC):
                p.emit("tensor", lambda t, dc=dc, pr=pr: t.matmul(
                    ps_lg[:], wrtb[:, dc, :], h2c[pr][:, dc, :],
                    start=(dc == 0), stop=(dc == DC - 1)),
                    wait=([(VE, aff3_done[ci]),
                           (LDC, P0["wrt_ld"])] + w_lg
                          if dc == 0 else []),
                    inc=([(PE, 1)] if dc == DC - 1 else []))
            p.emit("scalar", lambda s, sl=sl: s.copy(logT[:, sl], ps_lg[:]),
                   wait=[PE, (SE, None)], inc=[(SE, 1)])
            lg_se[ci] = SE.n
        es.close()
        return sth2_done, stx1_done, lg_se[cfg.NTC3 - 1]

    ctx_r = ExitStack()
    logT = ctx_r.enter_context(nc.sbuf_tensor("logT", [cfg.NRE, cfg.NQ], F32))
    sth2_done, stx1_done, lg_last = p3()
    ctx_w3.close()
    p.fence([(PE, None), (VE, None), (SE, None), (GP, None),
             sth2_done[cfg.NTC3 - 1], stx1_done[cfg.NTC3 - 1],
             LX[0], LX[1], LAT[0], LAT[1]] +
            ([sth2_done[cfg.NTC3 - 2], stx1_done[cfg.NTC3 - 2]]
             if cfg.NTC3 >= 2 else []))

    def router():
        es = ExitStack()
        sbl = lambda n, s_, d=F32: es.enter_context(nc.sbuf_tensor(n, s_, d))
        psl = lambda n, s_, d=F32: es.enter_context(nc.psum_tensor(n, s_, d))
        NT = cfg.NQ // 128
        NR = cfg.NRE
        gate_fm = sbl("r_gate_fm", [NR, cfg.NQ], BF16)
        lt = sbl("r_lt", [128, NT, NR])
        pex = sbl("r_pe", [128, NT, NR])
        red = sbl("r_red", [128, NT])
        m8 = sbl("r_m8", [128, 8])
        zap = sbl("r_zap", [128, NR])
        gtm = sbl("r_gtm", [128, NT, NR])
        ps_a = psl("r_ps_a", [128, 512])

        tr_se = []
        per_bank_lg = 512 // NR
        for t0 in range(0, NT, per_bank_lg):
            tiles = list(range(t0, min(t0 + per_bank_lg, NT)))
            w0 = [(SE, lg_last), (LDW_, None)]
            if tr_se:
                w0.append((SE, tr_se[-1]))
            for i, tt in enumerate(tiles):
                p.emit("tensor", lambda t, tt=tt, i=i: t.transpose(
                    ps_a[:, i * NR:(i + 1) * NR],
                    logT[:, tt * 128:(tt + 1) * 128],
                    ident[0:NR, 0:NR]),
                    wait=(w0 if i == 0 else []),
                    inc=([(PE, 1)] if i == len(tiles) - 1 else []))
            p.emit("scalar", lambda s, t0=t0, n=len(tiles): s.copy(
                lt[:, t0:t0 + n, :].rearrange("p a b -> p (a b)"),
                ps_a[:, 0:n * NR]), wait=[PE, (SE, None)], inc=[(SE, 1)])
            tr_se.append(SE.n)
        p.emit("scalar", lambda s: s.activation(
            pex[:].rearrange("p a b -> p (a b)"),
            lt[:].rearrange("p a b -> p (a b)"), AF.Exp),
            wait=[(SE, tr_se[-1])], inc=[(SE, 1)])
        exp_se = SE.n
        p.emit("vector", lambda v: v.reduce_sum(red[:], pex[:], axis=AX),
               wait=[(SE, exp_se), (VE, None)], inc=[(VE, 1)])
        p.emit("vector", lambda v: v.reciprocal(red[:], red[:]),
               wait=[(VE, None)], inc=[(VE, 1)])
        p.emit("vector", lambda v: v.tensor_tensor(
            pex[:], pex[:],
            red[:].unsqueeze(2).to_broadcast([128, NT, NR]), ALU.mult),
            wait=[(VE, None)], inc=[(VE, 1)])
        for tt in range(NT):
            p.emit("vector", lambda v, tt=tt: v.max(
                out=m8[:], in_=pex[:, tt, :]),
                wait=[(VE, None)], inc=[(VE, 1)])
            if cfg.TOPK < 8:
                p.emit("vector", lambda v: v.memset(
                    m8[:, cfg.TOPK:8], -1e30),
                    wait=[(VE, None)], inc=[(VE, 1)])
            p.emit("vector", lambda v, tt=tt: v.match_replace(
                out=zap[:], in_to_replace=m8[:],
                in_values=pex[:, tt, :], imm_value=-1.0),
                wait=[(VE, None)], inc=[(VE, 1)])
            p.emit("vector", lambda v, tt=tt: v.tensor_tensor(
                zap[:], zap[:], pex[:, tt, :], ALU.not_equal),
                wait=[(VE, None)], inc=[(VE, 1)])
            p.emit("vector", lambda v, tt=tt: v.tensor_tensor(
                gtm[:, tt, :], pex[:, tt, :], zap[:], ALU.mult),
                wait=[(VE, None)], inc=[(VE, 1)])
        gtm_ve = VE.n
        gt_se = []
        for t0 in range(0, NT, 4):
            tiles = list(range(t0, min(t0 + 4, NT)))
            w0 = [(VE, gtm_ve), (SE, tr_se[-1])]
            if gt_se:
                w0.append((SE, gt_se[-1]))
            for i, tt in enumerate(tiles):
                p.emit("tensor", lambda t, tt=tt, i=i: t.transpose(
                    ps_a[0:NR, i * 128:(i + 1) * 128],
                    gtm[:, tt, :], ident[:]),
                    wait=(w0 if i == 0 else []),
                    inc=([(PE, 1)] if i == len(tiles) - 1 else []))
            p.emit("scalar", lambda s, t0=t0, n=len(tiles): s.copy(
                gate_fm[:, t0 * 128:(t0 + n) * 128],
                ps_a[0:NR, 0:n * 128]), wait=[PE, (SE, None)], inc=[(SE, 1)])
            gt_se.append(SE.n)
        p.emit("sync", lambda g: g.dma_start(
            out=gate_hbm[:], in_=gate_fm[:]),
            wait=[(SE, gt_se[-1])], inc=[(LGR, 16)])
        gr_ld = (LGR, LGR.n)
        es.close()
        return gr_ld

    gate_ld = router()
    ctx_r.close()
    p.fence([(PE, None), (VE, None), (SE, None), (GP, None), LGR])

    def p4():
        es = ExitStack()
        sbl = lambda n, s_, d=F32: es.enter_context(nc.sbuf_tensor(n, s_, d))
        psl = lambda n, s_, d=F32: es.enter_context(nc.psum_tensor(n, s_, d))
        NP = len(cfg.passes)
        G = max(len(q) for q in cfg.passes)
        NSLOT = 2 * G - 1
        NTCH = cfg.NQ // 512
        DC2 = DC // 2
        wu = [sbl(f"f_wu{i}", [128, DC, E], BF16) for i in range(NSLOT)]
        wd = [sbl(f"f_wd{i}", [128, ECN, D], BF16) for i in range(NSLOT)]
        h2cb = [sbl(f"f_h2{i}", [128, DC, 512], BF16) for i in range(2)]
        x1cb = [sbl(f"f_x1{i}", [128, DC2, 512]) for i in range(2)]
        gbs = sbl("f_gb", [128, G, 512], BF16)
        grow = sbl("f_grow", [1, G * 512], BF16)
        actb = [sbl(f"f_act{i}", [128, ECN, 512], BF16) for i in range(G)]
        accin = [sbl(f"f_ai{i}", [128, DC2, 512], BF16) for i in range(2)]
        accout = [sbl(f"f_ao{i}", [128, DC2, 512], BF16) for i in range(2)]
        outc = [sbl("f_out0", [128, DC2, 512])]
        ps_dn = psl("f_ps_dn", [128, DC2 * 512])
        ps_up = psl("f_ps_up", [128, ECN * 512])

        wruf = wub_h.rearrange("(r c p) e -> r p c e", p=128, c=DC)
        wrdf = wdb_h.rearrange("(r c p) d -> r p c d", p=128, c=ECN)
        h2f = fmr(h2T)
        x1f = fmr(x1T)
        af_ = fmr(accT)
        of = fmr(outT)

        wload, grow_ld, ac_st, out_st = {}, {}, {}, {}
        gb_pe, gm_ve = {}, {}
        chunk_pe, pass_pe = {}, {}
        evac_mark = {}
        comb_done = {}
        dnh1_pe = {}

        slot_of, prev_of, slot_hist = {}, {}, {}
        slot_ctr = [0]
        deferred = {}

        def assign_slots(pi):
            for i in range(len(cfg.passes[pi])):
                s = slot_ctr[0]
                slot_ctr[0] = (s + 1) % NSLOT
                slot_of[(pi, i)] = s
                prev_of[(pi, i)] = slot_hist.get(s)
                slot_hist[s] = pi

        def emit_one_wload(pi, i):
            sl_ = slot_of[(pi, i)]
            pp = prev_of[(pi, i)]
            w_free = [(WCPA, None)] if pi < 2 else [(WCPB, None)]
            if pp is not None:
                w_free.append((PE, pass_pe[pp]))
            eid = cfg.passes[pi][i]
            p.emit("gpsimd", lambda g, sl_=sl_, s_=wruf[eid]: g.dma_start(
                out=wu[sl_][:], in_=s_), wait=w_free, inc=[(LFW[sl_], 16)])
            p.emit("gpsimd", lambda g, sl_=sl_, s_=wrdf[eid]: g.dma_start(
                out=wd[sl_][:], in_=s_), wait=[], inc=[(LFW[sl_], 16)])
            wload[(pi, i)] = (LFW[sl_], LFW[sl_].n)

        def emit_pass_loads(pi):
            for i in range(len(cfg.passes[pi])):
                pp = prev_of[(pi, i)]
                if pp is not None and pp not in pass_pe:
                    deferred.setdefault(pi, []).append(i)
                else:
                    emit_one_wload(pi, i)

        assign_slots(0)
        emit_pass_loads(0)
        if NP > 1:
            assign_slots(1)
            emit_pass_loads(1)

        for pi, plist in enumerate(cfg.passes):
            npl = len(plist)
            for i in deferred.pop(pi, []):
                emit_one_wload(pi, i)
            if pi >= 1 and pi + 1 <= NP - 1:
                assign_slots(pi + 1)
                emit_pass_loads(pi + 1)
            routed = [e for e in plist if e < cfg.NRE]
            nro = len(routed)
            r0 = routed[0] if nro else 0
            pending = None
            for ci in range(NTCH):
                pr = ci % 2
                sl = slice(ci * 512, (ci + 1) * 512)
                w_free = []
                if (pi, ci - 2) in chunk_pe:
                    w_free = [(PE, chunk_pe[(pi, ci - 2)])]
                elif pi > 0 and ci < 2:
                    w_free = [(PE, pass_pe[pi - 1])]
                p.emit("sync", lambda g, sl=sl, pr=pr: g.dma_start(
                    out=h2cb[pr][:], in_=h2f[:, :, sl]),
                    wait=w_free + [sth2_done[ci]], inc=[(LH2[pr], 16)])
                ldh_w = (LH2[pr], LH2[pr].n)
                ldx_w, ac_w = {}, {}

                def emit_half_loads(hf, ci=ci, sl=sl, ldx_w=ldx_w,
                                    ac_w=ac_w):
                    hidx = 2 * ci + hf
                    hb = hidx % 2
                    if pi == NP - 1:
                        xw = [stx1_done[ci]]
                        if (hidx - 2) in comb_done:
                            xw.append((VE, comb_done[hidx - 2]))
                        p.emit("sync", lambda g, sl=sl, hb=hb, hf=hf:
                               g.dma_start(
                                   out=x1cb[hb][:],
                                   in_=x1f[:, hf * DC2:(hf + 1) * DC2, sl]),
                               wait=xw, inc=[(LX1[hb], 16)])
                        ldx_w[hf] = (LX1[hb], LX1[hb].n)
                    if pi > 0:
                        aw_ = [ac_st[(pi - 1, hidx)]]
                        em = evac_mark.get((pi, hidx - 2),
                                           evac_mark.get(
                                               (pi - 1, 2 * NTCH - 2 + hidx)))
                        if em is not None:
                            aw_.append((SE if em[0] == "se" else VE, em[1]))
                        if pi == NP - 1 and (hidx - 2) in comb_done:
                            aw_.append((VE, comb_done[hidx - 2]))
                        p.emit("sync", lambda g, sl=sl, hb=hb, hf=hf:
                               g.dma_start(
                                   out=accin[hb][:],
                                   in_=af_[:, hf * DC2:(hf + 1) * DC2, sl]),
                               wait=aw_, inc=[(LAC[hb], 16)])
                        ac_w[hf] = (LAC[hb], LAC[hb].n)

                gb_bank_se = {}
                if nro:
                    gw = [gate_ld]
                    gl = grow_ld.get((pi, ci - 1),
                                     grow_ld.get((pi - 1, NTCH - 1)))
                    if gl is not None:
                        gw.append(gl)
                    gm = gb_pe.get((pi, ci - 1),
                                   gb_pe.get((pi - 1, NTCH - 1)))
                    if gm is not None:
                        gw.append((PE, gm))
                    p.emit("sync", lambda g, sl=sl, r0=r0, nro=nro:
                           g.dma_start(
                               out=grow[0:1, 0:nro * 512],
                               in_=gate_hbm[r0:r0 + nro, sl]),
                           wait=gw, inc=[(LGA[ci % 2], 16)])
                    grow_ld[(pi, ci)] = (LGA[ci % 2], LGA[ci % 2].n)
                emit_half_loads(0)
                if nro:
                    gmv = gm_ve.get((pi, ci - 1),
                                    gm_ve.get((pi - 1, NTCH - 1)))
                    first = True
                    for w0i in range(0, nro, ECN):
                        wave = list(range(w0i, min(w0i + ECN, nro)))
                        for k, j in enumerate(wave):
                            w_ = []
                            if first:
                                w_ = [grow_ld[(pi, ci)], (SE, None)]
                                if gmv is not None:
                                    w_.append((VE, gmv))
                                first = False
                            elif k == 0:
                                w_ = [(SE, None)]
                            p.emit("tensor",
                                   lambda t, k=k, r0=r0,
                                   eid=routed[j]:
                                   t.matmul(
                                       ps_up[:, k * 512:(k + 1) * 512],
                                       onesrow_bf[:],
                                       grow[0:1, (eid - r0) * 512:
                                            (eid - r0 + 1) * 512],
                                       start=True, stop=True),
                                   wait=w_, inc=[(PE, 1)])
                            p.emit("scalar", lambda s, w0i=w0i, k=k:
                                   s.copy(gbs[:, w0i + k, :],
                                          ps_up[:, k * 512:(k + 1) * 512]),
                                   wait=[PE, (SE, None)], inc=[(SE, 1)])
                            gb_bank_se[k] = SE.n
                    gb_pe[(pi, ci)] = PE.n

                up_pe, gelu_se, act_w, dnp = {}, {}, {}, {}

                def emit_up(i, ldh_w=ldh_w, pr=pr, gb_bank_se=gb_bank_se):
                    w0 = [wload[(pi, i)], ldh_w]
                    if i == 0:
                        if not gb_bank_se:
                            w0.append((SE, None))
                    else:
                        w0.append((SE, gelu_se[i - 1]))
                    for ec in range(ECN):
                        for dc in range(DC):
                            wec = []
                            if ec == 0 and dc == 0:
                                wec = list(w0)
                            if i == 0 and dc == 0 and ec in gb_bank_se:
                                wec.append((SE, gb_bank_se[ec]))
                            p.emit("tensor",
                                   lambda t, ec=ec, dc=dc, i=i, pr=pr,
                                   sl_=slot_of[(pi, i)]:
                                   t.matmul(
                                       ps_up[:, ec * 512:(ec + 1) * 512],
                                       wu[sl_][:, dc,
                                               ec * 128:(ec + 1) * 128],
                                       h2cb[pr][:, dc, :],
                                       start=(dc == 0), stop=(dc == DC - 1)),
                                   wait=wec,
                                   inc=([(PE, 1)]
                                        if ec == ECN - 1 and dc == DC - 1
                                        else []))
                    up_pe[i] = PE.n

                def emit_act(i, eid, ci=ci):
                    w = [(PE, up_pe[i]), (SE, None)]
                    prev = dnh1_pe.get((pi, ci - 1, i),
                                       dnh1_pe.get((pi - 1, NTCH - 1, i)))
                    if prev is not None:
                        w.append((PE, prev))
                    p.emit("scalar", lambda s, i=i: s.activation(
                        actb[i][:].rearrange("p a b -> p (a b)"),
                        ps_up[:], AF.Gelu),
                        wait=w, inc=[(SE, 1)])
                    gelu_se[i] = SE.n
                    if eid < cfg.NRE:
                        p.emit("vector", lambda v, i=i: v.tensor_tensor(
                            actb[i][:], actb[i][:],
                            gbs[:, i, :].unsqueeze(1).to_broadcast(
                                [128, ECN, 512]), ALU.mult),
                            wait=[(SE, gelu_se[i]), (VE, None)],
                            inc=[(VE, 1)])
                        act_w[i] = [(VE, VE.n)]
                        gm_ve[(pi, ci)] = VE.n
                    else:
                        act_w[i] = [(SE, gelu_se[i])]

                def emit_dn_half(hf, i, act_w=act_w, dnp=dnp, ci=ci,
                                 npl=npl):
                    for hcl in range(DC2):
                        hc = hf * DC2 + hcl
                        for ec in range(ECN):
                            w = []
                            if hcl == 0 and ec == 0:
                                w = list(act_w[i])
                                if i == 0:
                                    ph = 2 * ci + hf - 1
                                    em = evac_mark.get(
                                        (pi, ph),
                                        evac_mark.get((pi - 1,
                                                       2 * NTCH - 1)))
                                    if em is not None:
                                        w.append((SE if em[0] == "se"
                                                  else VE, em[1]))
                            mark = (hcl == DC2 - 1 and ec == ECN - 1
                                    and (hf == 1 or i == npl - 1))
                            p.emit("tensor",
                                   lambda t, hcl=hcl, hc=hc, ec=ec, i=i,
                                   npl=npl, sl_=slot_of[(pi, i)]:
                                   t.matmul(
                                       ps_dn[:, hcl * 512:(hcl + 1) * 512],
                                       wd[sl_][:, ec,
                                               hc * 128:(hc + 1) * 128],
                                       actb[i][:, ec, :],
                                       start=(i == 0 and ec == 0),
                                       stop=(i == npl - 1
                                             and ec == ECN - 1)),
                                   wait=w,
                                   inc=([(PE, 1)] if mark else []))
                            if mark:
                                if i == npl - 1:
                                    dnp[hf] = PE.n
                                if hf == 1:
                                    dnh1_pe[(pi, ci, i)] = PE.n

                def emit_evac(hf, dnp=dnp, ci=ci, ldx_w=ldx_w, ac_w=ac_w,
                              sl=sl):
                    hidx = 2 * ci + hf
                    hb = hidx % 2
                    stw = ac_st.get((pi, hidx - 2),
                                    ac_st.get((pi - 1, 2 * NTCH - 2 + hidx)))
                    if pi == 0:
                        w0 = [(PE, dnp[hf]), (SE, None)]
                        if stw is not None:
                            w0.append(stw)
                        p.emit("scalar", lambda s, hb=hb: s.copy(
                            accout[hb][:],
                            ps_dn[:].rearrange("p (a b) -> p a b", b=512)),
                            wait=w0, inc=[(SE, 1)])
                        evac_mark[(pi, hidx)] = ("se", SE.n)
                    elif pi < NP - 1:
                        w0 = [(PE, dnp[hf]), (VE, None), ac_w[hf]]
                        if stw is not None:
                            w0.append(stw)
                        p.emit("vector", lambda v, hb=hb: v.tensor_tensor(
                            accout[hb][:], accin[hb][:],
                            ps_dn[:].rearrange("p (a b) -> p a b", b=512),
                            ALU.add),
                            wait=w0, inc=[(VE, 1)])
                        evac_mark[(pi, hidx)] = ("ve", VE.n)
                    if pi < NP - 1:
                        ev = evac_mark[(pi, hidx)]
                        p.emit("scalar", lambda s, sl=sl, hb=hb, hf=hf:
                               s.dma_start(
                                   out=af_[:, hf * DC2:(hf + 1) * DC2, sl],
                                   in_=accout[hb][:]),
                               wait=[(SE if ev[0] == "se" else VE, ev[1])] +
                                    ([stw] if stw is not None else []),
                               inc=[(SAC[hb], 16)])
                        ac_st[(pi, hidx)] = (SAC[hb], SAC[hb].n)
                    else:
                        ow = out_st.get(hidx - 1)
                        for hcl in range(DC2):
                            hc = hf * DC2 + hcl
                            w0 = [(PE, dnp[hf]), (VE, None)]
                            if hcl == 0:
                                w0 += [ldx_w[hf], ac_w[hf],
                                       (VE, P0["mod_ve"])]
                                if ow is not None:
                                    w0.append(ow)
                            p.emit("vector",
                                   lambda v, hcl=hcl, hb=hb: v.tensor_tensor(
                                       outc[0][:, hcl, :],
                                       accin[hb][:, hcl, :],
                                       ps_dn[:, hcl * 512:(hcl + 1) * 512],
                                       ALU.add),
                                   wait=w0, inc=[(VE, 1)])
                            p.emit("vector",
                                   lambda v, hcl=hcl, hc=hc, hb=hb:
                                   v.scalar_tensor_tensor(
                                       outc[0][:, hcl, :],
                                       outc[0][:, hcl, :],
                                       modfm[:, 5, hc:hc + 1],
                                       x1cb[hb][:, hcl, :],
                                       ALU.mult, ALU.add),
                                   wait=[(VE, None)], inc=[(VE, 1)])
                        comb_done[hidx] = VE.n
                        evac_mark[(pi, hidx)] = ("ve", VE.n)
                        p.emit("scalar", lambda s, sl=sl, hb=hb, hf=hf:
                               s.dma_start(
                                   out=of[:, hf * DC2:(hf + 1) * DC2, sl],
                                   in_=outc[0][:]),
                               wait=[(VE, None)] +
                                    ([ow] if ow is not None else []),
                               inc=[(STO[hb], 16)])
                        out_st[hidx] = (STO[hb], STO[hb].n)

                emit_up(0)
                if pending is not None:
                    pending()
                    pending = None
                emit_half_loads(1)
                emit_act(0, plist[0])
                for i in range(1, npl):
                    emit_up(i)
                    emit_act(i, plist[i])
                    emit_dn_half(0, i - 1)
                emit_dn_half(0, npl - 1)
                emit_evac(0)

                def mk_pending(emit_dn_half=emit_dn_half,
                               emit_evac=emit_evac, npl=npl):
                    def run():
                        for i in range(npl):
                            emit_dn_half(1, i)
                        emit_evac(1)
                    return run

                pending = mk_pending()
                chunk_pe[(pi, ci)] = PE.n
            if pending is not None:
                pending()
                pending = None
            pass_pe[pi] = PE.n
        p.emit("sync", None, wait=[STO[0], STO[1]])
        es.close()

    p4()

    with ctx, nc.Block() as block:
        p.build(block)
    return nc


def _prep_consts(cfg, inputs):
    wqkv = np.asarray(inputs["L0_attn_w_qkv"], dtype=np.float32)
    D = cfg.D
    hhq, hhk = _householders(cfg)
    return {
        "wqkT": np.ascontiguousarray(wqkv[:, :2 * D].T),
        "wv": np.ascontiguousarray(wqkv[:, 2 * D:3 * D]),
        "wout": np.ascontiguousarray(
            np.asarray(inputs["L0_attn_w_out"], dtype=np.float32)),
        "hhq": np.ascontiguousarray(hhq.reshape(cfg.H * cfg.HD, cfg.HD)),
        "hhk": np.ascontiguousarray(hhk.reshape(cfg.H * cfg.HD, cfg.HD)),
        "hs": np.ascontiguousarray(
            np.asarray(inputs["L0_attn_head_scales"],
                       dtype=np.float32).reshape(1, cfg.H)),
        "wmod": np.ascontiguousarray(
            np.asarray(inputs["L0_wmod"], dtype=np.float32)),
        "wsu": np.ascontiguousarray(
            np.asarray(inputs["L0_ffn_shared_up"], dtype=np.float32)
            .transpose(0, 2, 1).reshape(cfg.NSE * D, cfg.E)),
        "wsd": np.ascontiguousarray(
            np.asarray(inputs["L0_ffn_shared_down"], dtype=np.float32)
            .transpose(0, 2, 1).reshape(cfg.NSE * cfg.E, D)),
        "wru": np.ascontiguousarray(
            np.asarray(inputs["L0_ffn_routed_up"], dtype=np.float32)
            .transpose(0, 2, 1).reshape(cfg.NRE * D, cfg.E)),
        "wrd": np.ascontiguousarray(
            np.asarray(inputs["L0_ffn_routed_down"], dtype=np.float32)
            .transpose(0, 2, 1).reshape(cfg.NRE * cfg.E, D)),
        "wrt": np.ascontiguousarray(
            np.asarray(inputs["L0_ffn_router_w"], dtype=np.float32).T),
        "identin": np.eye(128, dtype=np.float32),
    }


def _prep_core(cfg, inputs, core, consts):
    b, par = core // 2, core % 2
    x = np.asarray(inputs["L0_attn_x"], dtype=np.float32)[b]
    qr = _qrows(cfg, par)
    m = {
        "xT": np.ascontiguousarray(
            np.concatenate([x.T, x[qr].T], axis=1)),
        "cfm": np.ascontiguousarray(
            np.asarray(inputs["L0_c"], dtype=np.float32)[b]
            .reshape(cfg.DC, 128).T),
        "mask01": _mask01(par),
    }
    m.update(consts)
    return m


def kernel(**inputs):
    cfg = _Cfg()
    consts = _prep_consts(cfg, inputs)
    in_maps = [_prep_core(cfg, inputs, c, consts) for c in range(cfg.NCORES)]
    nc = _build(cfg)
    res = run_bass_kernel_spmd(nc, in_maps, core_ids=list(range(cfg.NCORES)))
    out = np.empty((cfg.NB, cfg.TKV, cfg.D), dtype=np.float32)
    for c in range(cfg.NCORES):
        b, par = c // 2, c % 2
        out[b, _qrows(cfg, par)] = np.asarray(res.results[c]["outT"]).T
    return out


# version 18
# speedup vs baseline: 1.0333x; 1.0015x over previous
import math
import numpy as np
import ml_dtypes
from contextlib import ExitStack

import concourse.bass as bass
import concourse.mybir as mybir
from concourse.bass_utils import run_bass_kernel_spmd

F32 = mybir.dt.float32
BF16 = mybir.dt.bfloat16
AF = mybir.ActivationFunctionType
ALU = mybir.AluOpType
AX = mybir.AxisListType.X

ENGINES = ("tensor", "vector", "scalar", "gpsimd", "sync")


class _Sem:
    def __init__(self, nc, name):
        self._cm = nc.semaphore(name)
        self.h = self._cm.__enter__()
        self.n = 0
        self.name = name


class _Prog:

    def __init__(self, nc):
        self.nc = nc
        self.ops = {e: [] for e in ENGINES}

    def sem(self, name):
        return _Sem(self.nc, name)

    def emit(self, engine, fn, wait=(), inc=()):
        wv = []
        for w in wait:
            if w is None:
                continue
            if isinstance(w, _Sem):
                wv.append((w, w.n))
            else:
                s, v = w
                wv.append((s, s.n if v is None else v))
        iv = []
        for s, k in inc:
            s.n += k
            iv.append((s, k))
        self.ops[engine].append((wv, fn, iv))

    def fence(self, marks):
        for e in ENGINES:
            self.emit(e, None, wait=marks)

    def build(self, block):
        def mk(engine):
            def body(eng):
                for wv, fn, iv in self.ops[engine]:
                    for s, v in wv:
                        if v > 0:
                            eng.wait_ge(s.h, v)
                    if fn is not None:
                        instr = fn(eng)
                        for s, k in iv:
                            instr = instr.then_inc(s.h, k)
            return body

        for engine in ENGINES:
            if self.ops[engine]:
                getattr(block, engine)(mk(engine))


class _Cfg:
    def __init__(self, D=1024, H=8, E=512, NRE=16, NSE=2, TOPK=4, KVT=64,
                 NB=4, eps=1e-5, G=4):
        self.D, self.H, self.E = D, H, E
        self.NRE, self.NSE, self.TOPK = NRE, NSE, TOPK
        self.NE = NRE + NSE
        self.KVT, self.NB = KVT, NB
        self.eps = eps
        self.HD = 128
        assert D == H * self.HD
        self.DC = D // 128
        self.ECN = E // 128
        self.TKV = KVT * 128
        self.NS = KVT // 2
        self.NQ = self.NS * 128
        self.NG = self.NS // 4
        assert self.NS % 4 == 0
        self.TTOT = self.TKV + self.NQ
        assert self.TTOT % 512 == 0
        self.NCH = self.TTOT // 512
        self.KVCH = self.TKV // 512
        self.QCH = self.NQ // 512
        self.NTC3 = self.NQ // 512
        self.NTC4 = self.NQ // 256
        self.NCORES = 2 * NB
        assert H % 2 == 0
        self.NPAIR = H // 2
        ids = list(range(self.NE))
        self.passes = [ids[i:i + G] for i in range(0, self.NE, G)]
        self.scale = 1.0 / math.sqrt(self.HD)


def _dirs(n):
    t = np.array([[(i >> 2 & 1) * 2 - 1, (i >> 1 & 1) * 2 - 1, (i & 1) * 2 - 1]
                  for i in range(8)], dtype=np.float32)
    return (t / np.sqrt(3.0, dtype=np.float32))[:n]


def _householders(cfg):
    H, HD, s = cfg.H, cfg.HD, cfg.scale
    d = _dirs(H)
    hhq = np.zeros((H, HD, HD), dtype=np.float32)
    hhk = np.zeros((H, HD, HD), dtype=np.float32)
    for h in range(H):
        u = np.zeros(HD, dtype=np.float64)
        u[:3] = d[h].astype(np.float64)
        v = u.copy()
        v[0] -= 1.0
        Hm = np.eye(HD) - 2.0 * np.outer(v, v) / (v @ v)
        hhq[h] = Hm.astype(np.float32)
        Hk = Hm.copy()
        Hk[:, 1:] *= s
        hhk[h] = Hk.astype(np.float32)
    return hhq, hhk


def _qrows(cfg, p):
    return np.concatenate(
        [np.arange(128 * (2 * i + p), 128 * (2 * i + p) + 128)
         for i in range(cfg.NS)])


def _mask01(p):
    k = np.arange(128)[:, None]
    q = np.arange(128)[None, :]
    tri = (k <= q).astype(np.float32)
    if p == 1:
        a, b = np.ones((128, 128), np.float32), tri
    else:
        a, b = tri, np.zeros((128, 128), np.float32)
    return np.ascontiguousarray(np.concatenate([a, b], axis=1))


def _active(cfg, g, kt):
    act = 0
    for j in range(4):
        if kt < 8 * g + 2 * j + 2:
            act += 1
    return act


def _build(cfg):
    nc = bass.Bass("TRN2", target_bir_lowering=False)
    D, DC, H, HD, E, ECN = cfg.D, cfg.DC, cfg.H, cfg.HD, cfg.E, cfg.ECN

    def din(name, shape, dt=F32):
        return nc.dram_tensor(name, shape, dt, kind="ExternalInput").ap()

    xT = din("xT", [D, cfg.TTOT])
    wqkT = din("wqkT", [2 * D, D])
    wv_d = din("wv", [D, D])
    wout_d = din("wout", [D, D])
    hhq_d = din("hhq", [H * HD, HD])
    hhk_d = din("hhk", [H * HD, HD])
    hs_d = din("hs", [1, H])
    wmod_d = din("wmod", [D, 6 * D])
    cfm_d = din("cfm", [128, DC])
    wsu_d = din("wsu", [cfg.NSE * D, E])
    wsd_d = din("wsd", [cfg.NSE * E, D])
    wru_d = din("wru", [cfg.NRE * D, E])
    wrd_d = din("wrd", [cfg.NRE * E, D])
    wrt_d = din("wrt", [D, cfg.NRE])
    mask_d = din("mask01", [128, 256])
    iden_d = din("identin", [128, 128])

    outT = nc.dram_tensor("outT", [D, cfg.NQ], F32, kind="ExternalOutput").ap()
    h1T = nc.dram_tensor("h1T", [D, cfg.TTOT], BF16, kind="Internal").ap()
    h2T = nc.dram_tensor("h2T", [D, cfg.NQ], BF16, kind="Internal").ap()
    x1T = nc.dram_tensor("x1T", [D, cfg.NQ], F32, kind="Internal").ap()
    atT = nc.dram_tensor("atT", [D, cfg.NQ], BF16, kind="Internal").ap()
    gate_hbm = nc.dram_tensor("gate_hbm", [cfg.NRE, cfg.NQ], BF16,
                              kind="Internal").ap()
    accT = nc.dram_tensor("accT", [D, cfg.NQ], BF16, kind="Internal").ap()
    NEX = cfg.NRE + cfg.NSE
    wub_h = nc.dram_tensor("wub", [NEX * D, E], BF16, kind="Internal").ap()
    wdb_h = nc.dram_tensor("wdb", [NEX * E, D], BF16, kind="Internal").ap()

    def fmr(ap_):
        return ap_.rearrange("(c p) t -> p c t", p=128)

    ctx = ExitStack()

    def sb(name, shape, dt=F32):
        return ctx.enter_context(nc.sbuf_tensor(name, shape, dt))

    p = _Prog(nc)
    LDW_ = p.sem("ldw")
    LDC = p.sem("ldc")
    PE = p.sem("pe")
    VE = p.sem("ve")
    SE = p.sem("se")
    GP = p.sem("gp")
    STH1 = [p.sem(f"sth1_{i}") for i in range(2)]
    STA = [p.sem(f"sta_{i}") for i in range(2)]
    STH2 = [p.sem(f"sth2_{i}") for i in range(2)]
    STX1 = [p.sem(f"stx1_{i}") for i in range(2)]
    STO = [p.sem(f"sto_{i}") for i in range(2)]
    LX = [p.sem(f"lx_{i}") for i in range(2)]
    LXB = [p.sem(f"lxb_{i}") for i in range(2)]
    LH1 = [p.sem(f"lh1_{i}") for i in range(2)]
    LAT = [p.sem(f"lat_{i}") for i in range(2)]
    LH2 = [p.sem(f"lh2_{i}") for i in range(2)]
    LX1 = [p.sem(f"lx1_{i}") for i in range(2)]
    LGR = p.sem("lgr")
    LFW = [p.sem(f"lfw_{i}") for i in range(8)]
    LGA = [p.sem(f"lga_{i}") for i in range(2)]
    LAC = [p.sem(f"lac_{i}") for i in range(2)]
    SAC = [p.sem(f"sac_{i}") for i in range(2)]

    onescol = sb("onescol", [128, 1], BF16)
    ones128 = sb("ones128", [128, 128], BF16)
    epscol = sb("epscol", [128, 1], F32)
    neghalf = sb("neghalf", [128, 1], F32)
    negone = sb("negone", [128, 1], F32)
    onesrow = sb("onesrow", [1, 128], F32)
    onesrow_bf = sb("onesrow_bf", [1, 128], BF16)
    one1 = sb("one1", [1, 1], F32)
    ident = sb("ident", [128, 128], F32)
    maskb = sb("maskb", [128, 256], BF16)
    modfm = sb("modfm", [128, 6, DC], F32)
    hsc = sb("hsc", [128, H], F32)
    ctx_w3 = ExitStack()
    sbw3 = lambda n, s_, d=F32: ctx_w3.enter_context(
        nc.sbuf_tensor(n, s_, d, side="right"))
    woutb = sbw3("woutb", [128, DC, D], BF16)
    wrtb = sbw3("wrtb", [128, DC, cfg.NRE], BF16)
    ctx_w2 = ExitStack()
    sbw2 = lambda n, s_, d=F32: ctx_w2.enter_context(
        nc.sbuf_tensor(n, s_, d, side="right"))
    wqrot = sbw2("wqrot", [128, H, DC, 128], BF16)
    wkrot = sbw2("wkrot", [128, H, DC, 128], BF16)
    wvb = sbw2("wvb", [128, DC, D], BF16)

    def p0():
        p.emit("vector", lambda v: v.memset(onescol[:], 1.0), inc=[(VE, 1)])
        p.emit("vector", lambda v: v.memset(ones128[:], 1.0), inc=[(VE, 1)])
        p.emit("vector", lambda v: v.memset(epscol[:], float(cfg.eps)),
               inc=[(VE, 1)])
        p.emit("vector", lambda v: v.memset(neghalf[:], -0.5), inc=[(VE, 1)])
        p.emit("vector", lambda v: v.memset(negone[:], -1.0), inc=[(VE, 1)])
        p.emit("vector", lambda v: v.memset(onesrow[:], 1.0), inc=[(VE, 1)])
        p.emit("vector", lambda v: v.memset(one1[:], 1.0), inc=[(VE, 1)])
        p.emit("vector", lambda v: v.memset(onesrow_bf[:], 1.0), inc=[(VE, 1)])
        const_ve = VE.n
        p.emit("sync", lambda g: g.dma_start(out=ident[:], in_=iden_d[:]),
               inc=[(LDW_, 16)])
        p.emit("gpsimd", lambda g: g.dma_start(out=maskb[:], in_=mask_d[:]),
               inc=[(LDC, 16)])
        p.emit("gpsimd", lambda g: g.dma_start(
            out=wrtb[:], in_=wrt_d.rearrange("(c p) r -> p c r", p=128)),
            inc=[(LDC, 16)])
        p.emit("gpsimd", lambda g: g.dma_start(out=wvb[:], in_=fmr(wv_d)),
               inc=[(LDC, 16)])
        p.emit("gpsimd", lambda g: g.dma_start(out=woutb[:], in_=fmr(wout_d)),
               inc=[(LDC, 16)])

        es = ExitStack()
        sbl = lambda n, s_, d=F32: es.enter_context(nc.sbuf_tensor(n, s_, d))
        psl = lambda n, s_, d=F32: es.enter_context(nc.psum_tensor(n, s_, d))
        cfm = sbl("p0_cfm", [128, DC])
        mod_row = sbl("p0_mod_row", [1, 6 * D])
        csg = sbl("p0_csg", [128, DC])
        scb = sbl("p0_scb", [128, DC], BF16)
        hs_sb = sbl("p0_hs", [1, H])
        nmodw = 6 if (6 * D // 512) % 6 == 0 else 1
        wmod_sb = [sbl(f"p0_wmod{i}", [128, DC, (6 * D) // nmodw])
                   for i in range(2 if nmodw > 1 else 1)]
        hhq_sb = sbl("p0_hhq", [128, H, HD])
        hhk_sb = sbl("p0_hhk", [128, H, HD])
        hhkb_sb = sbl("p0_hhkb", [128, H, HD], BF16)
        hhqb_sb = sbl("p0_hhqb", [128, H, HD], BF16)
        wqkT_sb = sbl("p0_wqkT", [128, 2 * H, D], BF16)
        gsc = sbl("p0_gsc", [128, DC])
        ps_mod = [psl(f"p0_ps_mod{i}", [1, 512]) for i in range(2)]
        ps_fm = psl("p0_ps_fm", [128, 64])
        ps_hs = psl("p0_ps_hs", [128, H])
        ps_rot = [psl(f"p0_ps_rot{i}", [128, 512]) for i in range(2)]

        p.emit("sync", lambda g: g.dma_start(out=cfm[:], in_=cfm_d[:]),
               inc=[(LDW_, 16)])
        p.emit("sync", lambda g: g.dma_start(out=hs_sb[:], in_=hs_d[:]),
               inc=[(LDW_, 16)])
        p.emit("sync", lambda g: g.dma_start(
            out=hhq_sb[:], in_=hhq_d.rearrange("(h p) c -> p h c", p=128)),
            inc=[(LDW_, 16)])
        p.emit("sync", lambda g: g.dma_start(
            out=hhk_sb[:], in_=hhk_d.rearrange("(h p) c -> p h c", p=128)),
            inc=[(LDW_, 16)])
        p.emit("gpsimd", lambda g: g.dma_start(
            out=wqkT_sb[:], in_=wqkT.rearrange("(c p) d -> p c d", p=128)),
            inc=[(LDC, 16)])
        cfm_ld = hs_ld = hh_ld = LDW_.n
        mask_ld = wrt_ld = wv_ld = wout_ld = wqk_ld = LDC.n
        WMD = p.sem("wmd")

        p.emit("scalar", lambda s: s.activation(csg[:], cfm[:], AF.Sigmoid),
               wait=[(LDW_, cfm_ld)], inc=[(SE, 1)])
        p.emit("vector", lambda v: v.tensor_mul(csg[:], csg[:], cfm[:]),
               wait=[SE, (VE, None)], inc=[(VE, 1)])
        p.emit("vector", lambda v: v.tensor_copy(scb[:], csg[:]),
               wait=[(VE, None)], inc=[(VE, 1)])
        scb_ve = VE.n

        p.emit("tensor", lambda t: t.matmul(ps_hs[:], onesrow[:], hs_sb[:],
                                            start=True, stop=True),
               wait=[(LDW_, hs_ld), (VE, const_ve)], inc=[(PE, 1)])
        p.emit("vector", lambda v: v.tensor_scalar(
            hsc[:], ps_hs[:], cfg.scale, None, ALU.add),
            wait=[PE, (VE, None)], inc=[(VE, 1)])
        hsc_ve = VE.n
        for h in range(H):
            p.emit("vector", lambda v, h=h: v.tensor_tensor(
                hhk_sb[:, h, 0:1], hhk_sb[:, h, 0:1], hsc[:, h:h + 1],
                ALU.mult),
                wait=[(VE, hsc_ve), (LDW_, hh_ld), (VE, None)], inc=[(VE, 1)])
        p.emit("vector", lambda v: v.tensor_copy(hhkb_sb[:], hhk_sb[:]),
               wait=[(VE, None)], inc=[(VE, 1)])
        p.emit("vector", lambda v: v.tensor_copy(hhqb_sb[:], hhq_sb[:]),
               wait=[(LDW_, hh_ld), (VE, None)], inc=[(VE, 1)])
        hhk_ve = VE.n

        nmodch = 6 * D // 512
        wavech = nmodch // nmodw
        evac_ve = {}
        wave_pe = {}
        wmodf = wmod_d.rearrange("(c p) f -> p c f", p=128)
        nbuf = 2 if nmodw > 1 else 1
        for wv_ in range(nmodw):
            wlo = wv_ * (6 * D // nmodw)
            wb = wv_ % nbuf
            p.emit("sync", lambda g, wlo=wlo, wb=wb: g.dma_start(
                out=wmod_sb[wb][:],
                in_=wmodf[:, :, wlo:wlo + (6 * D) // nmodw]),
                wait=([(PE, wave_pe[wv_ - nbuf])] if wv_ >= nbuf else []),
                inc=[(WMD, 16)])
            wmod_ld = WMD.n
            for fi in range(wavech):
                f = wv_ * wavech + fi
                par = f % 2
                w0 = [(VE, scb_ve), (WMD, wmod_ld)]
                if f >= 2:
                    w0.append((VE, evac_ve[f - 2]))
                for dc in range(DC):
                    p.emit("tensor", lambda t, dc=dc, fi=fi, par=par, wb=wb:
                           t.matmul(
                               ps_mod[par][:], csg[:, dc:dc + 1],
                               wmod_sb[wb][:, dc, fi * 512:(fi + 1) * 512],
                               start=(dc == 0), stop=(dc == DC - 1)),
                           wait=(w0 if dc == 0 else []),
                           inc=([(PE, 1)] if dc == DC - 1 else []))
                p.emit("vector", lambda v, f=f, par=par: v.tensor_copy(
                    mod_row[:, f * 512:(f + 1) * 512], ps_mod[par][:]),
                    wait=[PE, (VE, None)], inc=[(VE, 1)])
                evac_ve[f] = VE.n
            wave_pe[wv_] = PE.n
        modrow_ve = VE.n

        n6 = 6 * DC
        for j in range(n6):
            p.emit("tensor", lambda t, j=j: t.matmul(
                ps_fm[:, j:j + 1], mod_row[:, j * 128:(j + 1) * 128], one1[:],
                start=True, stop=True),
                wait=([(VE, modrow_ve), (VE, const_ve)] if j == 0 else []),
                inc=[(PE, 1)])
        p.emit("scalar", lambda s: s.copy(
            modfm[:].rearrange("p a b -> p (a b)"), ps_fm[:, 0:n6]),
            wait=[PE, (SE, None)], inc=[(SE, 1)])
        raw_se = SE.n

        for vi in (1, 4):
            p.emit("vector", lambda v, vi=vi: v.tensor_scalar_add(
                modfm[:, vi, :], modfm[:, vi, :], 1.0),
                wait=[(SE, raw_se), (VE, None)], inc=[(VE, 1)])
        for vi in (2, 5):
            p.emit("vector", lambda v, vi=vi: v.tensor_scalar(
                gsc[:], modfm[:, vi, :], 0.0, None, ALU.is_ge),
                wait=[(SE, raw_se), (VE, None)], inc=[(VE, 1)])
            p.emit("vector", lambda v: v.tensor_scalar(
                gsc[:], gsc[:], 2.0, -1.0, ALU.mult, ALU.add),
                wait=[(VE, None)], inc=[(VE, 1)])
            p.emit("vector", lambda v, vi=vi: v.tensor_tensor(
                modfm[:, vi, :], modfm[:, vi, :], gsc[:], ALU.mult),
                wait=[(VE, None)], inc=[(VE, 1)])
            p.emit("vector", lambda v, vi=vi: v.tensor_scalar_max(
                modfm[:, vi, :], modfm[:, vi, :], 0.25),
                wait=[(VE, None)], inc=[(VE, 1)])
            p.emit("vector", lambda v, vi=vi: v.tensor_tensor(
                modfm[:, vi, :], modfm[:, vi, :], gsc[:], ALU.mult),
                wait=[(VE, None)], inc=[(VE, 1)])
        mod_ve = VE.n

        evac_se = {}
        ri = 0
        for side in range(2):
            hh_sb_ = hhqb_sb if side == 0 else hhkb_sb
            dst = wqrot if side == 0 else wkrot
            for h in range(H):
                for d0 in range(0, DC, 4):
                    dcs = list(range(d0, min(d0 + 4, DC)))
                    par = ri % 2
                    w0 = [(VE, hhk_ve), (LDC, wqk_ld)]
                    if ri >= 2:
                        w0.append((SE, evac_se[ri - 2]))
                    crow = h if side == 0 else H + h
                    for i, dc in enumerate(dcs):
                        p.emit("tensor",
                               lambda t, i=i, dc=dc, crow=crow, hh_sb_=hh_sb_,
                               h=h, par=par:
                               t.matmul(ps_rot[par][:, i * 128:(i + 1) * 128],
                                        wqkT_sb[:, crow,
                                                dc * 128:(dc + 1) * 128],
                                        hh_sb_[:, h, :],
                                        start=True, stop=True),
                               wait=(w0 if i == 0 else []),
                               inc=([(PE, 1)] if i == len(dcs) - 1 else []))
                    p.emit("scalar",
                           lambda s, dst=dst, h=h, d0=d0, n=len(dcs), par=par:
                           s.copy(dst[:, h, d0:d0 + n, :],
                                  ps_rot[par][:, 0:n * 128]),
                           wait=[PE, (SE, None)], inc=[(SE, 1)])
                    evac_se[ri] = SE.n
                    ri += 1
        rot_se = SE.n
        es.close()
        return dict(mod_ve=mod_ve, rot_se=rot_se, mask_ld=mask_ld,
                    wrt_ld=wrt_ld, wv_ld=wv_ld, wout_ld=wout_ld,
                    const_ve=const_ve)

    P0 = p0()
    p.fence([(PE, None), (VE, None), (SE, None), (GP, None), LDW_, LDC])

    def ln_phase(prefix, src_of, n_ch, dst_of, ca_idx, sa_idx, st_sems,
                 src_wait_of=None):
        es = ExitStack()
        sbl = lambda n, s_, d=F32: es.enter_context(
            nc.sbuf_tensor(prefix + n, s_, d))
        psl = lambda n, s_, d=F32: es.enter_context(
            nc.psum_tensor(prefix + n, s_, d))
        xb = [sbl(f"xb{i}", [128, DC, 512], BF16) for i in range(2)]
        xq = [sbl(f"xq{i}", [128, DC, 512], BF16) for i in range(2)]
        h_c = [sbl(f"h{i}", [128, DC, 512], BF16) for i in range(2)]
        rf = [sbl(f"rf{i}", [128, 3, 512]) for i in range(2)]
        rb = [sbl(f"rb{i}", [128, 2, 512], BF16) for i in range(2)]
        t1b = sbl("t1b", [128, DC, 512], BF16)
        ps_s = [psl(f"ps_s{i}", [128, 512]) for i in range(2)]
        ps_q = [psl(f"ps_q{i}", [128, 512]) for i in range(2)]

        dve_done, st_done, aff_done = {}, {}, {}
        ld_marks = {}

        def emit_loads(ci):
            pr = ci % 2
            w_free = ([(VE, dve_done[ci - 2])]
                      if ci >= 2 else [])
            w_src = src_wait_of(ci) if src_wait_of else []
            p.emit("gpsimd", lambda g, ci=ci, pr=pr: g.dma_start(
                out=xb[pr][:], in_=src_of(ci)),
                wait=w_free + w_src, inc=[(LXB[pr], 16)])
            ld_marks[ci] = (LXB[pr], LXB[pr].n)

        emit_loads(0)
        if n_ch > 1:
            emit_loads(1)
        for ci in range(n_ch):
            pr = ci % 2
            ldc_w = ld_marks[ci]
            p.emit("scalar", lambda s, pr=pr: s.activation(
                xq[pr][:], xb[pr][:], AF.Square),
                wait=[ldc_w, (SE, None)], inc=[(SE, 1)])
            sq_se = SE.n
            pe_w = [ldc_w, (VE, P0["const_ve"])] + \
                ([(VE, dve_done[ci - 2])] if ci >= 2 else [])
            for dc in range(DC):
                p.emit("tensor", lambda t, dc=dc, pr=pr: t.matmul(
                    ps_s[pr][:], ones128[:], xb[pr][:, dc, :],
                    start=(dc == 0), stop=(dc == DC - 1)),
                    wait=(pe_w if dc == 0 else []))
            for dc in range(DC):
                p.emit("tensor", lambda t, dc=dc, pr=pr: t.matmul(
                    ps_q[pr][:], ones128[:], xq[pr][:, dc, :],
                    start=(dc == 0), stop=(dc == DC - 1)),
                    wait=([(SE, sq_se)] if dc == 0 else []),
                    inc=([(PE, 1)] if dc == DC - 1 else []))
            pe_stats = PE.n
            r, rb_ = rf[pr], rb[pr]
            rw = [(VE, dve_done[ci - 2])] if ci >= 2 else []
            p.emit("vector", lambda v, r=r, pr=pr: v.tensor_scalar(
                r[:, 0, :], ps_s[pr][:], 1.0 / D, None, ALU.mult),
                wait=[(PE, pe_stats), (VE, None)] + rw, inc=[(VE, 1)])
            p.emit("vector", lambda v, r=r, pr=pr: v.tensor_scalar(
                r[:, 1, :], ps_q[pr][:], 1.0 / D, None, ALU.mult),
                wait=[(VE, None)], inc=[(VE, 1)])
            p.emit("vector", lambda v, r=r: v.tensor_tensor(
                r[:, 2, :], r[:, 0, :], r[:, 0, :], ALU.mult),
                wait=[(VE, None)], inc=[(VE, 1)])
            p.emit("vector", lambda v, r=r: v.tensor_tensor(
                r[:, 2, :], r[:, 1, :], r[:, 2, :], ALU.subtract),
                wait=[(VE, None)], inc=[(VE, 1)])
            p.emit("scalar", lambda s, r=r: s.activation(
                r[:, 2, :], r[:, 2, :], AF.Ln, bias=epscol[:]),
                wait=[VE, (SE, None)], inc=[(SE, 1)])
            p.emit("scalar", lambda s, r=r, rb_=rb_: s.activation(
                rb_[:, 0, :], r[:, 2, :], AF.Exp, scale=neghalf[:]),
                wait=[(SE, None)], inc=[(SE, 1)])
            rstd_se = SE.n
            p.emit("vector", lambda v, r=r, rb_=rb_: v.scalar_tensor_tensor(
                rb_[:, 1, :], r[:, 0, :], -1.0, rb_[:, 0, :],
                ALU.mult, ALU.mult),
                wait=[(SE, rstd_se), (VE, None)], inc=[(VE, 1)])
            rowm = VE.n
            p.emit("vector", lambda v, pr=pr, rb_=rb_: v.tensor_tensor(
                t1b[:], xb[pr][:],
                rb_[:, 0, :].unsqueeze(1).to_broadcast([128, DC, 512]),
                ALU.mult),
                wait=[(VE, None), ldc_w, (VE, rowm), (VE, P0["mod_ve"])],
                inc=[(VE, 1)])
            p.emit("vector", lambda v, rb_=rb_: v.tensor_tensor(
                t1b[:], t1b[:],
                rb_[:, 1, :].unsqueeze(1).to_broadcast([128, DC, 512]),
                ALU.add),
                wait=[(VE, None)], inc=[(VE, 1)])
            for dc in range(DC):
                hw = [st_done[ci - 2]] if (dc == 0 and ci >= 2) else []
                p.emit("vector", lambda v, dc=dc, pr=pr:
                       v.tensor_scalar(
                           h_c[pr][:, dc, :], t1b[:, dc, :],
                           modfm[:, ca_idx, dc:dc + 1],
                           modfm[:, sa_idx, dc:dc + 1],
                           ALU.mult, ALU.add),
                       wait=[(VE, None)] + hw, inc=[(VE, 1)])
            dve_done[ci] = VE.n
            aff_done[ci] = VE.n
            if ci + 2 < n_ch:
                emit_loads(ci + 2)
            p.emit("sync", lambda g, ci=ci, pr=pr: g.dma_start(
                out=dst_of(ci), in_=h_c[pr][:]),
                wait=[(VE, aff_done[ci])] +
                     ([st_done[ci - 2]] if ci >= 2 else []),
                inc=[(st_sems[pr], 16)])
            st_done[ci] = (st_sems[pr], st_sems[pr].n)
        es.close()
        return st_done

    p1_st = ln_phase(
        "p1_",
        src_of=lambda ci: fmr(xT)[:, :, ci * 512:(ci + 1) * 512],
        n_ch=cfg.NCH,
        dst_of=lambda ci: fmr(h1T)[:, :, ci * 512:(ci + 1) * 512],
        ca_idx=1, sa_idx=0, st_sems=STH1)
    p.fence([(PE, None), (VE, None), (SE, None), (GP, None),
             p1_st[cfg.NCH - 1], p1_st[cfg.NCH - 2],
             LX[0], LX[1], LXB[0], LXB[1]])

    WCPA = p.sem("wcpa")
    WCPB = p.sem("wcpb")

    def emit_precast(first_wait):
        na = min(len(cfg.passes[0]) + (len(cfg.passes[1])
                                       if len(cfg.passes) > 1 else 0), NEX)
        for r in range(NEX):
            sem = WCPA if r < na else WCPB
            usrc = wru_d if r < cfg.NRE else wsu_d
            dsrc = wrd_d if r < cfg.NRE else wsd_d
            ui = r if r < cfg.NRE else r - cfg.NRE
            p.emit("gpsimd", lambda g, r=r, usrc=usrc, ui=ui: g.dma_start(
                out=wub_h[r * D:(r + 1) * D, :],
                in_=usrc[ui * D:(ui + 1) * D, :]),
                wait=(first_wait if r == 0 else []), inc=[(sem, 16)])
            p.emit("gpsimd", lambda g, r=r, dsrc=dsrc, ui=ui: g.dma_start(
                out=wdb_h[r * E:(r + 1) * E, :],
                in_=dsrc[ui * E:(ui + 1) * E, :]), inc=[(sem, 16)])

    def p2():
        es = ExitStack()
        sbl = lambda n, s_, d=F32: es.enter_context(nc.sbuf_tensor(n, s_, d))
        psl = lambda n, s_, d=F32: es.enter_context(nc.psum_tensor(n, s_, d))
        h1c = [sbl(f"a_h1c{i}", [128, DC, 512], BF16) for i in range(2)]
        Ksb = sbl("a_K", [128, 2, cfg.TKV], BF16)
        Vsb = sbl("a_V", [128, cfg.KVT, 2, 128], BF16)
        Qsb = sbl("a_Q", [128, 2, cfg.NQ], BF16)
        expb = [sbl(f"a_exp{i}", [128, 1024], BF16) for i in range(2)]
        recb = [sbl(f"a_recb{i}", [128, 512]) for i in range(2)]
        aoutg = [sbl(f"a_aout{i}", [128, 512], BF16) for i in range(2)]
        ps_p = [psl(f"a_ps_p{i}", [128, 512]) for i in range(2)]
        ps_t = [psl(f"a_ps_t{i}", [128, 1024]) for i in range(2)]
        ps_l = [psl(f"a_ps_l{i}", [128, 512]) for i in range(2)]
        h1f = fmr(h1T)

        proj_pe = {}
        aout_st = {}
        ag_st = {}
        gctr = 0
        ld2_marks = {}

        def emit_h1load(pair, ci, w_free):
            pr = ci % 2
            p.emit("sync", lambda g, ci=ci, pr=pr: g.dma_start(
                out=h1c[pr][:], in_=h1f[:, :, ci * 512:(ci + 1) * 512]),
                wait=w_free + [p1_st[ci]], inc=[(LH1[pr], 16)])
            ld2_marks[(pair, ci)] = (LH1[pr], LH1[pr].n)
        p.emit("vector", lambda v: v.memset(ps_t[0][:], 0.0),
               wait=[(VE, None)], inc=[(VE, 1)])
        p.emit("vector", lambda v: v.memset(ps_t[1][:], 0.0),
               wait=[(VE, None)], inc=[(VE, 1)])
        groupend_ve = VE.n
        pt_init_ve = VE.n
        prev_exp = {}
        ln_se = {}
        aout_ve = {}
        last_pair = None
        for pair in getattr(cfg, "pair_order", range(cfg.NPAIR)):
            h0 = 2 * pair
            evac_se = {}
            pi = 0
            for ci in range(cfg.KVCH + cfg.QCH):
                pr = ci % 2
                w_free = []
                if ci >= 2:
                    w_free = [(PE, proj_pe[(pair, ci - 2)])]
                elif last_pair is not None:
                    w_free = [(PE, proj_pe[(last_pair,
                                            cfg.KVCH + cfg.QCH - 2 + ci)])]
                if (pair, ci) not in ld2_marks:
                    emit_h1load(pair, ci, w_free)
                ldh_w = ld2_marks[(pair, ci)]
                is_kv = ci < cfg.KVCH

                def proj(lhs_of, rhs_of, out_of, n_acc, extra_w, out_w):
                    nonlocal pi
                    par = pi % 2
                    w0 = list(extra_w)
                    if pi >= 2:
                        w0.append((SE, evac_se[pi - 2]))
                    else:
                        w0.append((VE, max(aout_ve.values(), default=0)))
                    for a in range(n_acc):
                        p.emit("tensor",
                               lambda t, a=a, par=par, lhs_of=lhs_of,
                               rhs_of=rhs_of, out_w=out_w, n_acc=n_acc:
                               t.matmul(ps_p[par][:, 0:out_w],
                                        lhs_of(a), rhs_of(a),
                                        start=(a == 0), stop=(a == n_acc - 1)),
                               wait=(w0 if a == 0 else []),
                               inc=([(PE, 1)] if a == n_acc - 1 else []))
                    p.emit("scalar",
                           lambda s, par=par, out_of=out_of, out_w=out_w:
                           s.copy(out_of(), ps_p[par][:, 0:out_w]),
                           wait=[PE, (SE, None)], inc=[(SE, 1)])
                    evac_se[pi] = SE.n
                    pi += 1

                if is_kv:
                    for hh in range(2):
                        proj(lambda a, hh=hh, h0=h0: wkrot[:, h0 + hh, a, :],
                             lambda a, pr=pr: h1c[pr][:, a, :],
                             lambda hh=hh, ci=ci:
                             Ksb[:, hh, ci * 512:(ci + 1) * 512],
                             DC, [ldh_w, (SE, P0["rot_se"])], 512)
                    for ts_ in range(4):
                        proj(lambda a, pr=pr, ts_=ts_:
                             h1c[pr][:, a, ts_ * 128:(ts_ + 1) * 128],
                             lambda a, h0=h0:
                             wvb[:, a, h0 * 128:(h0 + 2) * 128],
                             lambda ci=ci, ts_=ts_:
                             Vsb[:, ci * 4 + ts_, :, :].rearrange(
                                 "p a b -> p (a b)"),
                             DC, [ldh_w, (LDC, P0["wv_ld"])], 256)
                else:
                    qi = ci - cfg.KVCH
                    for hh in range(2):
                        proj(lambda a, hh=hh, h0=h0: wqrot[:, h0 + hh, a, :],
                             lambda a, pr=pr: h1c[pr][:, a, :],
                             lambda hh=hh, qi=qi:
                             Qsb[:, hh, qi * 512:(qi + 1) * 512],
                             DC, [ldh_w, (SE, P0["rot_se"])], 512)
                proj_pe[(pair, ci)] = PE.n
            tailA = SE.n
            if pair == 0:
                emit_precast([(SE, tailA)])
            nxt = pair + 1
            if nxt < cfg.NPAIR:
                for cj in range(2):
                    emit_h1load(nxt, cj,
                                [(PE, proj_pe[(pair,
                                               cfg.KVCH + cfg.QCH - 2 + cj)])])

            for hh in range(2):
                h = h0 + hh
                for g in range(cfg.NG):
                    ktmax = 8 * g + 8
                    qlo = 4 * g * 128
                    gpar = gctr % 2
                    ps_o_g = ps_p[gpar]
                    ps_l_g = ps_l[gpar]
                    duos = [(k, min(k + 2, ktmax)) for k in range(0, ktmax, 2)]
                    nd = len(duos)
                    qk_pe, exp_se, msk_ve, av_pe = {}, {}, {}, {}
                    pe0 = prev_exp.copy()

                    def emit_qk(di):
                        k0, k1 = duos[di]
                        par = di % 2
                        for kt in range(k0, k1):
                            act = _active(cfg, g, kt)
                            j0 = 4 - act
                            col = (kt - k0) * 512
                            w = []
                            if kt == k0:
                                w = [(SE, tailA)]
                                if di >= 2:
                                    w.append((SE, exp_se[di - 2]))
                                elif (di % 2) in pe0:
                                    w.append((SE, pe0[di % 2]))
                                else:
                                    w.append((VE, pt_init_ve))
                            p.emit("tensor",
                                   lambda t, kt=kt, j0=j0, col=col, par=par,
                                   hh=hh, qlo=qlo, g=g:
                                   t.matmul(
                                       ps_t[par][:, col + j0 * 128:col + 512],
                                       Ksb[:, hh, kt * 128:(kt + 1) * 128],
                                       Qsb[:, hh, qlo + j0 * 128:qlo + 512],
                                       start=True, stop=True),
                                   wait=w,
                                   inc=([(PE, 1)] if kt == k1 - 1 else []))
                        qk_pe[di] = PE.n

                    def emit_exp(di):
                        k0, k1 = duos[di]
                        par = di % 2
                        wid = (k1 - k0) * 512
                        w = [(PE, qk_pe[di]), (SE, None)]
                        if di >= 2:
                            w.append((PE, av_pe[di - 2]))
                            if (di - 2) in msk_ve:
                                w.append((VE, msk_ve[di - 2]))
                        p.emit("scalar", lambda s, par=par, wid=wid:
                               s.activation(expb[par][:, 0:wid],
                                            ps_t[par][:, 0:wid], AF.Exp),
                               wait=w, inc=[(SE, 1)])
                        exp_se[di] = SE.n
                        mm = False
                        for kt in range(k0, k1):
                            if kt >= 8 * g:
                                jstar = (kt - 8 * g) // 2
                                mA = (kt - 8 * g) % 2 == 0
                                col = (kt - k0) * 512 + jstar * 128
                                mo = 0 if mA else 128
                                p.emit("vector",
                                       lambda v, par=par, col=col, mo=mo:
                                       v.tensor_tensor(
                                           expb[par][:, col:col + 128],
                                           expb[par][:, col:col + 128],
                                           maskb[:, mo:mo + 128], ALU.mult),
                                       wait=[(SE, exp_se[di]),
                                             (LDC, P0["mask_ld"]),
                                             (VE, None)],
                                       inc=[(VE, 1)])
                                mm = True
                        if mm:
                            msk_ve[di] = VE.n

                    def emit_av(di):
                        k0, k1 = duos[di]
                        par = di % 2
                        first = True
                        for kt in range(k0, k1):
                            act = _active(cfg, g, kt)
                            j0 = 4 - act
                            col = (kt - k0) * 512
                            w = []
                            if first:
                                w = [(SE, exp_se[di])]
                                if di in msk_ve:
                                    w.append((VE, msk_ve[di]))
                                if kt == 0:
                                    w.append((SE, ln_se.get(gpar, 0)))
                                    w.append((VE, aout_ve.get(gpar, 0)))
                            p.emit("tensor",
                                   lambda t, kt=kt, j0=j0, col=col, par=par,
                                   hh=hh, ktmax=ktmax, ps_o_g=ps_o_g:
                                   t.matmul(
                                       ps_o_g[:, j0 * 128:512],
                                       Vsb[:, kt, hh, :],
                                       expb[par][:, col + j0 * 128:col + 512],
                                       start=(kt == 0),
                                       stop=(kt == ktmax - 1)),
                                   wait=w)
                            p.emit("tensor",
                                   lambda t, kt=kt, j0=j0, col=col, par=par,
                                   ktmax=ktmax, ps_l_g=ps_l_g:
                                   t.matmul(
                                       ps_l_g[:, j0 * 128:512],
                                       ones128[:],
                                       expb[par][:, col + j0 * 128:col + 512],
                                       start=(kt == 0),
                                       stop=(kt == ktmax - 1)),
                                   inc=([(PE, 1)] if kt == k1 - 1 else []))
                            first = False
                        av_pe[di] = PE.n

                    for di in range(nd):
                        emit_qk(di)
                        emit_exp(di)
                        if di >= 1:
                            emit_av(di - 1)
                    emit_av(nd - 1)

                    p.emit("scalar", lambda s, ps_l_g=ps_l_g, gpar=gpar:
                           s.activation(recb[gpar][:], ps_l_g[:], AF.Ln),
                           wait=[(PE, av_pe[nd - 1]), (SE, None),
                                 (VE, aout_ve.get(gpar, 0))],
                           inc=[(SE, 1)])
                    ln_se[gpar] = SE.n
                    p.emit("scalar", lambda s, gpar=gpar:
                           s.activation(recb[gpar][:], recb[gpar][:],
                                        AF.Exp, scale=negone[:]),
                           wait=[(SE, None)], inc=[(SE, 1)])
                    rec_se = SE.n
                    gp = gctr % 2
                    aw = [ag_st[gctr - 2]] if gctr >= 2 else []
                    p.emit("vector", lambda v, gp=gp, gpar=gpar,
                           ps_o_g=ps_o_g:
                           v.tensor_tensor(
                               aoutg[gp][:], ps_o_g[:], recb[gpar][:],
                               ALU.mult),
                           wait=[(SE, rec_se), (VE, None)] + aw,
                           inc=[(VE, 1)])
                    aout_ve[gpar] = VE.n
                    groupend_ve = VE.n
                    prev_exp = {}
                    for dd in (nd - 2, nd - 1):
                        if dd >= 0:
                            prev_exp[dd % 2] = exp_se[dd]
                    p.emit("sync", lambda g_, h=h, g=g, gp=gp: g_.dma_start(
                        out=fmr(atT)[:, h, g * 512:(g + 1) * 512],
                        in_=aoutg[gp][:]),
                        wait=[(VE, groupend_ve)] + aw,
                        inc=[(STA[gp], 16)])
                    ag_st[gctr] = (STA[gp], STA[gp].n)
                    gctr += 1
            last_pair = pair
        es.close()
        return aout_st

    aout_st = p2()
    ctx_w2.close()
    p.fence([(PE, None), (VE, None), (SE, None), (GP, None),
             LH1[0], LH1[1], STA[0], STA[1], LGR])

    def p3():
        es = ExitStack()
        sbl = lambda n, s_, d=F32: es.enter_context(nc.sbuf_tensor(n, s_, d))
        psl = lambda n, s_, d=F32: es.enter_context(nc.psum_tensor(n, s_, d))
        xq = [sbl(f"o_xq{i}", [128, DC, 512]) for i in range(2)]
        ats = [sbl(f"o_at{i}", [128, H, 512], BF16) for i in range(2)]
        x1c = [sbl(f"o_x1{i}", [128, DC, 512]) for i in range(2)]
        x1b = sbl("o_x1b", [128, DC, 512], BF16)
        x1q = sbl("o_x1q", [128, DC, 512], BF16)
        h2c = [sbl(f"o_h2{i}", [128, DC, 512], BF16) for i in range(2)]
        rf3 = sbl("o_rf", [128, 3, 512])
        rb3 = sbl("o_rb", [128, 2, 512], BF16)
        t1b3 = sbl("o_t1b", [128, DC, 512], BF16)
        ps_o = [psl(f"o_ps_o{i}", [128, 512]) for i in range(2)]
        ps_s = [psl(f"o_ps_s{i}", [128, 512]) for i in range(2)]
        ps_q = [psl(f"o_ps_q{i}", [128, 512]) for i in range(2)]
        ps_lg = psl("o_ps_lg", [cfg.NRE, 512])

        atf = fmr(atT)
        xf = fmr(xT)
        dve_done, sth2_done, stx1_done, lg_se = {}, {}, {}, {}
        x1_ve = {}
        gps3_done = {}
        aff3_done = {}
        ld3_marks = {}
        pe_stats_prev = 0

        def emit_loads3(ci):
            pr = ci % 2
            sl = slice(ci * 512, (ci + 1) * 512)
            w_free = [(VE, dve_done[ci - 2]), (GP, gps3_done[ci - 2])] \
                if ci >= 2 else []
            p.emit("sync", lambda g, ci=ci, pr=pr: g.dma_start(
                out=xq[pr][:], in_=xf[:, :, cfg.TKV + ci * 512:
                                      cfg.TKV + (ci + 1) * 512]),
                wait=w_free, inc=[(LX[pr], 16)])
            p.emit("sync", lambda g, ci=ci, pr=pr, sl=sl: g.dma_start(
                out=ats[pr][:], in_=atf[:, :, sl]),
                wait=w_free + [STA[0], STA[1]],
                inc=[(LAT[pr], 16)])
            ld3_marks[ci] = ((LX[pr], LX[pr].n), (LAT[pr], LAT[pr].n))

        emit_loads3(0)
        if cfg.NTC3 > 1:
            emit_loads3(1)
        for ci in range(cfg.NTC3):
            pr = ci % 2
            sl = slice(ci * 512, (ci + 1) * 512)
            ldx_w, ldh_w = ld3_marks[ci]
            oproj_ve = {}
            for fc in range(DC):
                par2 = fc % 2
                w0 = [ldh_w, (LDC, P0["wout_ld"])]
                if fc >= 2:
                    w0.append((VE, oproj_ve[fc - 2]))
                elif ci >= 1:
                    w0.append((VE, x1_ve[ci - 1]))
                if fc == 0 and ci >= 2:
                    w0.append(stx1_done[ci - 2])
                for h in range(H):
                    p.emit("tensor", lambda t, h=h, fc=fc, pr=pr, par2=par2:
                           t.matmul(ps_o[par2][:],
                                    woutb[:, h, fc * 128:(fc + 1) * 128],
                                    ats[pr][:, h, :],
                                    start=(h == 0), stop=(h == H - 1)),
                           wait=(w0 if h == 0 else []),
                           inc=([(PE, 1)] if h == H - 1 else []))
                pe_o = PE.n
                p.emit("vector", lambda v, fc=fc, pr=pr, par2=par2:
                       v.scalar_tensor_tensor(
                           x1c[pr][:, fc, :], ps_o[par2][:],
                           modfm[:, 2, fc:fc + 1], xq[pr][:, fc, :],
                           ALU.mult, ALU.add),
                       wait=[(PE, pe_o), ldx_w, (VE, None),
                             (VE, P0["mod_ve"])],
                       inc=[(VE, 1)])
                oproj_ve[fc] = VE.n
            x1_ve[ci] = VE.n
            xw = [(PE, pe_stats_prev)] if ci >= 1 else []
            if ci >= 1:
                xw.append((VE, dve_done[ci - 1]))
            p.emit("scalar", lambda s, pr=pr: s.copy(x1b[:], x1c[pr][:]),
                   wait=[(VE, x1_ve[ci]), (SE, None)] + xw, inc=[(SE, 1)])
            p.emit("scalar", lambda s: s.activation(
                x1q[:], x1b[:], AF.Square), wait=[(SE, None)], inc=[(SE, 1)])
            sq_se = SE.n
            p.emit("sync", lambda g, pr=pr, sl=sl: g.dma_start(
                out=fmr(x1T)[:, :, sl], in_=x1c[pr][:]),
                wait=[(VE, x1_ve[ci])] +
                     ([stx1_done[ci - 2]] if ci >= 2 else []),
                inc=[(STX1[pr], 16)])
            stx1_done[ci] = (STX1[pr], STX1[pr].n)
            w_st = [(VE, dve_done[ci - 2]), (GP, gps3_done[ci - 2])] \
                if ci >= 2 else []
            for dc in range(DC):
                p.emit("tensor", lambda t, dc=dc, pr=pr: t.matmul(
                    ps_s[pr][:], ones128[:], x1b[:, dc, :],
                    start=(dc == 0), stop=(dc == DC - 1)),
                    wait=([(SE, sq_se)] + w_st if dc == 0 else []))
            for dc in range(DC):
                p.emit("tensor", lambda t, dc=dc, pr=pr: t.matmul(
                    ps_q[pr][:], ones128[:], x1q[:, dc, :],
                    start=(dc == 0), stop=(dc == DC - 1)),
                    inc=([(PE, 1)] if dc == DC - 1 else []))
            pe_stats = PE.n
            pe_stats_prev = pe_stats
            r, rb_ = rf3, rb3
            p.emit("vector", lambda v, r=r, pr=pr: v.tensor_scalar(
                r[:, 0, :], ps_s[pr][:], 1.0 / D, None, ALU.mult),
                wait=[(PE, pe_stats), (VE, None)], inc=[(VE, 1)])
            p.emit("vector", lambda v, r=r, pr=pr: v.tensor_scalar(
                r[:, 1, :], ps_q[pr][:], 1.0 / D, None, ALU.mult),
                wait=[(VE, None)], inc=[(VE, 1)])
            p.emit("vector", lambda v, r=r: v.tensor_tensor(
                r[:, 2, :], r[:, 0, :], r[:, 0, :], ALU.mult),
                wait=[(VE, None)], inc=[(VE, 1)])
            p.emit("vector", lambda v, r=r: v.tensor_tensor(
                r[:, 2, :], r[:, 1, :], r[:, 2, :], ALU.subtract),
                wait=[(VE, None)], inc=[(VE, 1)])
            p.emit("scalar", lambda s, r=r: s.activation(
                r[:, 2, :], r[:, 2, :], AF.Ln, bias=epscol[:]),
                wait=[VE, (SE, None)], inc=[(SE, 1)])
            p.emit("scalar", lambda s, r=r, rb_=rb_: s.activation(
                rb_[:, 0, :], r[:, 2, :], AF.Exp, scale=neghalf[:]),
                wait=[(SE, None)], inc=[(SE, 1)])
            rstd_se = SE.n
            p.emit("vector", lambda v, r=r, rb_=rb_: v.scalar_tensor_tensor(
                rb_[:, 1, :], r[:, 0, :], -1.0, rb_[:, 0, :],
                ALU.mult, ALU.mult),
                wait=[(SE, rstd_se), (VE, None)], inc=[(VE, 1)])
            rowm = VE.n
            hw = [sth2_done[ci - 2]] if ci >= 2 else []
            p.emit("vector", lambda v, rb_=rb_: v.tensor_tensor(
                t1b3[:], x1b[:],
                rb_[:, 0, :].unsqueeze(1).to_broadcast([128, DC, 512]),
                ALU.mult),
                wait=[(VE, rowm), (VE, None)], inc=[(VE, 1)])
            p.emit("vector", lambda v, rb_=rb_: v.tensor_tensor(
                t1b3[:], t1b3[:],
                rb_[:, 1, :].unsqueeze(1).to_broadcast([128, DC, 512]),
                ALU.add),
                wait=[(VE, None)], inc=[(VE, 1)])
            for dc in range(DC):
                p.emit("vector", lambda v, dc=dc, pr=pr:
                       v.tensor_scalar(
                           h2c[pr][:, dc, :], t1b3[:, dc, :],
                           modfm[:, 4, dc:dc + 1],
                           modfm[:, 3, dc:dc + 1],
                           ALU.mult, ALU.add),
                       wait=[(VE, None)] + (hw if dc == 0 else []),
                       inc=[(VE, 1)])
            dve_done[ci] = VE.n
            gps3_done[ci] = GP.n
            aff3_done[ci] = VE.n
            if ci + 2 < cfg.NTC3:
                emit_loads3(ci + 2)
            p.emit("sync", lambda g, pr=pr, sl=sl: g.dma_start(
                out=fmr(h2T)[:, :, sl], in_=h2c[pr][:]),
                wait=[(VE, aff3_done[ci])] +
                     ([sth2_done[ci - 2]] if ci >= 2 else []),
                inc=[(STH2[pr], 16)])
            sth2_done[ci] = (STH2[pr], STH2[pr].n)
            w_lg = [(SE, lg_se[ci - 1])] if ci >= 1 else []
            for dc in range(DC):
                p.emit("tensor", lambda t, dc=dc, pr=pr: t.matmul(
                    ps_lg[:], wrtb[:, dc, :], h2c[pr][:, dc, :],
                    start=(dc == 0), stop=(dc == DC - 1)),
                    wait=([(VE, aff3_done[ci]),
                           (LDC, P0["wrt_ld"])] + w_lg
                          if dc == 0 else []),
                    inc=([(PE, 1)] if dc == DC - 1 else []))
            p.emit("scalar", lambda s, sl=sl: s.copy(logT[:, sl], ps_lg[:]),
                   wait=[PE, (SE, None)], inc=[(SE, 1)])
            lg_se[ci] = SE.n
        es.close()
        return sth2_done, stx1_done, lg_se[cfg.NTC3 - 1]

    ctx_r = ExitStack()
    logT = ctx_r.enter_context(nc.sbuf_tensor("logT", [cfg.NRE, cfg.NQ], F32))
    sth2_done, stx1_done, lg_last = p3()
    ctx_w3.close()
    p.fence([(PE, None), (VE, None), (SE, None), (GP, None),
             sth2_done[cfg.NTC3 - 1], stx1_done[cfg.NTC3 - 1],
             LX[0], LX[1], LAT[0], LAT[1]] +
            ([sth2_done[cfg.NTC3 - 2], stx1_done[cfg.NTC3 - 2]]
             if cfg.NTC3 >= 2 else []))

    def router():
        es = ExitStack()
        sbl = lambda n, s_, d=F32: es.enter_context(nc.sbuf_tensor(n, s_, d))
        psl = lambda n, s_, d=F32: es.enter_context(nc.psum_tensor(n, s_, d))
        NT = cfg.NQ // 128
        NR = cfg.NRE
        gate_fm = sbl("r_gate_fm", [NR, cfg.NQ], BF16)
        lt = sbl("r_lt", [128, NT, NR])
        pex = sbl("r_pe", [128, NT, NR])
        red = sbl("r_red", [128, NT])
        m8 = sbl("r_m8", [128, 8])
        zap = sbl("r_zap", [128, NR])
        gtm = sbl("r_gtm", [128, NT, NR])
        ps_a = psl("r_ps_a", [128, 512])

        tr_se = []
        per_bank_lg = 512 // NR
        for t0 in range(0, NT, per_bank_lg):
            tiles = list(range(t0, min(t0 + per_bank_lg, NT)))
            w0 = [(SE, lg_last), (LDW_, None)]
            if tr_se:
                w0.append((SE, tr_se[-1]))
            for i, tt in enumerate(tiles):
                p.emit("tensor", lambda t, tt=tt, i=i: t.transpose(
                    ps_a[:, i * NR:(i + 1) * NR],
                    logT[:, tt * 128:(tt + 1) * 128],
                    ident[0:NR, 0:NR]),
                    wait=(w0 if i == 0 else []),
                    inc=([(PE, 1)] if i == len(tiles) - 1 else []))
            p.emit("scalar", lambda s, t0=t0, n=len(tiles): s.copy(
                lt[:, t0:t0 + n, :].rearrange("p a b -> p (a b)"),
                ps_a[:, 0:n * NR]), wait=[PE, (SE, None)], inc=[(SE, 1)])
            tr_se.append(SE.n)
        p.emit("scalar", lambda s: s.activation(
            pex[:].rearrange("p a b -> p (a b)"),
            lt[:].rearrange("p a b -> p (a b)"), AF.Exp),
            wait=[(SE, tr_se[-1])], inc=[(SE, 1)])
        exp_se = SE.n
        p.emit("vector", lambda v: v.reduce_sum(red[:], pex[:], axis=AX),
               wait=[(SE, exp_se), (VE, None)], inc=[(VE, 1)])
        p.emit("vector", lambda v: v.reciprocal(red[:], red[:]),
               wait=[(VE, None)], inc=[(VE, 1)])
        p.emit("vector", lambda v: v.tensor_tensor(
            pex[:], pex[:],
            red[:].unsqueeze(2).to_broadcast([128, NT, NR]), ALU.mult),
            wait=[(VE, None)], inc=[(VE, 1)])
        for tt in range(NT):
            p.emit("vector", lambda v, tt=tt: v.max(
                out=m8[:], in_=pex[:, tt, :]),
                wait=[(VE, None)], inc=[(VE, 1)])
            if cfg.TOPK < 8:
                p.emit("vector", lambda v: v.memset(
                    m8[:, cfg.TOPK:8], -1e30),
                    wait=[(VE, None)], inc=[(VE, 1)])
            p.emit("vector", lambda v, tt=tt: v.match_replace(
                out=zap[:], in_to_replace=m8[:],
                in_values=pex[:, tt, :], imm_value=-1.0),
                wait=[(VE, None)], inc=[(VE, 1)])
            p.emit("vector", lambda v, tt=tt: v.tensor_tensor(
                zap[:], zap[:], pex[:, tt, :], ALU.not_equal),
                wait=[(VE, None)], inc=[(VE, 1)])
            p.emit("vector", lambda v, tt=tt: v.tensor_tensor(
                gtm[:, tt, :], pex[:, tt, :], zap[:], ALU.mult),
                wait=[(VE, None)], inc=[(VE, 1)])
        gtm_ve = VE.n
        gt_se = []
        for t0 in range(0, NT, 4):
            tiles = list(range(t0, min(t0 + 4, NT)))
            w0 = [(VE, gtm_ve), (SE, tr_se[-1])]
            if gt_se:
                w0.append((SE, gt_se[-1]))
            for i, tt in enumerate(tiles):
                p.emit("tensor", lambda t, tt=tt, i=i: t.transpose(
                    ps_a[0:NR, i * 128:(i + 1) * 128],
                    gtm[:, tt, :], ident[:]),
                    wait=(w0 if i == 0 else []),
                    inc=([(PE, 1)] if i == len(tiles) - 1 else []))
            p.emit("scalar", lambda s, t0=t0, n=len(tiles): s.copy(
                gate_fm[:, t0 * 128:(t0 + n) * 128],
                ps_a[0:NR, 0:n * 128]), wait=[PE, (SE, None)], inc=[(SE, 1)])
            gt_se.append(SE.n)
        p.emit("sync", lambda g: g.dma_start(
            out=gate_hbm[:], in_=gate_fm[:]),
            wait=[(SE, gt_se[-1])], inc=[(LGR, 16)])
        gr_ld = (LGR, LGR.n)
        es.close()
        return gr_ld

    gate_ld = router()
    ctx_r.close()
    p.fence([(PE, None), (VE, None), (SE, None), (GP, None), LGR])

    def p4():
        es = ExitStack()
        sbl = lambda n, s_, d=F32: es.enter_context(nc.sbuf_tensor(n, s_, d))
        psl = lambda n, s_, d=F32: es.enter_context(nc.psum_tensor(n, s_, d))
        NP = len(cfg.passes)
        G = max(len(q) for q in cfg.passes)
        NSLOT = 2 * G - 1
        NTCH = cfg.NQ // 512
        DC2 = DC // 2
        wu = [sbl(f"f_wu{i}", [128, DC, E], BF16) for i in range(NSLOT)]
        wd = [sbl(f"f_wd{i}", [128, ECN, D], BF16) for i in range(NSLOT)]
        h2cb = [sbl(f"f_h2{i}", [128, DC, 512], BF16) for i in range(2)]
        x1cb = [sbl(f"f_x1{i}", [128, DC2, 512]) for i in range(2)]
        gbs = sbl("f_gb", [128, G, 512], BF16)
        grow = sbl("f_grow", [1, G * 512], BF16)
        actb = [sbl(f"f_act{i}", [128, ECN, 512], BF16) for i in range(G)]
        accin = [sbl(f"f_ai{i}", [128, DC2, 512], BF16) for i in range(2)]
        accout = [sbl(f"f_ao{i}", [128, DC2, 512], BF16) for i in range(2)]
        outc = [sbl("f_out0", [128, DC2, 512])]
        ps_dn = psl("f_ps_dn", [128, DC2 * 512])
        ps_up = psl("f_ps_up", [128, ECN * 512])

        wruf = wub_h.rearrange("(r c p) e -> r p c e", p=128, c=DC)
        wrdf = wdb_h.rearrange("(r c p) d -> r p c d", p=128, c=ECN)
        h2f = fmr(h2T)
        x1f = fmr(x1T)
        af_ = fmr(accT)
        of = fmr(outT)

        wload, grow_ld, ac_st, out_st = {}, {}, {}, {}
        gb_pe, gm_ve = {}, {}
        chunk_pe, pass_pe = {}, {}
        evac_mark = {}
        comb_done = {}
        dnh1_pe = {}

        slot_of, prev_of, slot_hist = {}, {}, {}
        slot_ctr = [0]
        deferred = {}

        def assign_slots(pi):
            for i in range(len(cfg.passes[pi])):
                s = slot_ctr[0]
                slot_ctr[0] = (s + 1) % NSLOT
                slot_of[(pi, i)] = s
                prev_of[(pi, i)] = slot_hist.get(s)
                slot_hist[s] = pi

        def emit_one_wload(pi, i):
            sl_ = slot_of[(pi, i)]
            pp = prev_of[(pi, i)]
            w_free = [(WCPA, None)] if pi < 2 else [(WCPB, None)]
            if pp is not None:
                w_free.append((PE, pass_pe[pp]))
            eid = cfg.passes[pi][i]
            p.emit("gpsimd", lambda g, sl_=sl_, s_=wruf[eid]: g.dma_start(
                out=wu[sl_][:], in_=s_), wait=w_free, inc=[(LFW[sl_], 16)])
            p.emit("gpsimd", lambda g, sl_=sl_, s_=wrdf[eid]: g.dma_start(
                out=wd[sl_][:], in_=s_), wait=[], inc=[(LFW[sl_], 16)])
            wload[(pi, i)] = (LFW[sl_], LFW[sl_].n)

        def emit_pass_loads(pi):
            for i in range(len(cfg.passes[pi])):
                pp = prev_of[(pi, i)]
                if pp is not None and pp not in pass_pe:
                    deferred.setdefault(pi, []).append(i)
                else:
                    emit_one_wload(pi, i)

        assign_slots(0)
        emit_pass_loads(0)
        if NP > 1:
            assign_slots(1)
            emit_pass_loads(1)

        for pi, plist in enumerate(cfg.passes):
            npl = len(plist)
            for i in deferred.pop(pi, []):
                emit_one_wload(pi, i)
            if pi >= 1 and pi + 1 <= NP - 1:
                assign_slots(pi + 1)
                emit_pass_loads(pi + 1)
            routed = [e for e in plist if e < cfg.NRE]
            nro = len(routed)
            r0 = routed[0] if nro else 0
            pending = None
            for ci in range(NTCH):
                pr = ci % 2
                sl = slice(ci * 512, (ci + 1) * 512)
                w_free = []
                if (pi, ci - 2) in chunk_pe:
                    w_free = [(PE, chunk_pe[(pi, ci - 2)])]
                elif pi > 0 and ci < 2:
                    w_free = [(PE, pass_pe[pi - 1])]
                p.emit("sync", lambda g, sl=sl, pr=pr: g.dma_start(
                    out=h2cb[pr][:], in_=h2f[:, :, sl]),
                    wait=w_free + [sth2_done[ci]], inc=[(LH2[pr], 16)])
                ldh_w = (LH2[pr], LH2[pr].n)
                ldx_w, ac_w = {}, {}

                def emit_half_loads(hf, ci=ci, sl=sl, ldx_w=ldx_w,
                                    ac_w=ac_w):
                    hidx = 2 * ci + hf
                    hb = hidx % 2
                    if pi == NP - 1:
                        xw = [stx1_done[ci]]
                        if (hidx - 2) in comb_done:
                            xw.append((VE, comb_done[hidx - 2]))
                        p.emit("sync", lambda g, sl=sl, hb=hb, hf=hf:
                               g.dma_start(
                                   out=x1cb[hb][:],
                                   in_=x1f[:, hf * DC2:(hf + 1) * DC2, sl]),
                               wait=xw, inc=[(LX1[hb], 16)])
                        ldx_w[hf] = (LX1[hb], LX1[hb].n)
                    if pi > 0:
                        aw_ = [ac_st[(pi - 1, hidx)]]
                        em = evac_mark.get((pi, hidx - 2),
                                           evac_mark.get(
                                               (pi - 1, 2 * NTCH - 2 + hidx)))
                        if em is not None:
                            aw_.append((SE if em[0] == "se" else VE, em[1]))
                        if pi == NP - 1 and (hidx - 2) in comb_done:
                            aw_.append((VE, comb_done[hidx - 2]))
                        p.emit("sync", lambda g, sl=sl, hb=hb, hf=hf:
                               g.dma_start(
                                   out=accin[hb][:],
                                   in_=af_[:, hf * DC2:(hf + 1) * DC2, sl]),
                               wait=aw_, inc=[(LAC[hb], 16)])
                        ac_w[hf] = (LAC[hb], LAC[hb].n)

                gb_bank_se = {}
                if nro:
                    gw = [gate_ld]
                    gl = grow_ld.get((pi, ci - 1),
                                     grow_ld.get((pi - 1, NTCH - 1)))
                    if gl is not None:
                        gw.append(gl)
                    gm = gb_pe.get((pi, ci - 1),
                                   gb_pe.get((pi - 1, NTCH - 1)))
                    if gm is not None:
                        gw.append((PE, gm))
                    p.emit("sync", lambda g, sl=sl, r0=r0, nro=nro:
                           g.dma_start(
                               out=grow[0:1, 0:nro * 512],
                               in_=gate_hbm[r0:r0 + nro, sl]),
                           wait=gw, inc=[(LGA[ci % 2], 16)])
                    grow_ld[(pi, ci)] = (LGA[ci % 2], LGA[ci % 2].n)
                emit_half_loads(0)
                if nro:
                    gmv = gm_ve.get((pi, ci - 1),
                                    gm_ve.get((pi - 1, NTCH - 1)))
                    first = True
                    for w0i in range(0, nro, ECN):
                        wave = list(range(w0i, min(w0i + ECN, nro)))
                        for k, j in enumerate(wave):
                            w_ = []
                            if first:
                                w_ = [grow_ld[(pi, ci)], (SE, None)]
                                if gmv is not None:
                                    w_.append((VE, gmv))
                                first = False
                            elif k == 0:
                                w_ = [(SE, None)]
                            p.emit("tensor",
                                   lambda t, k=k, r0=r0,
                                   eid=routed[j]:
                                   t.matmul(
                                       ps_up[:, k * 512:(k + 1) * 512],
                                       onesrow_bf[:],
                                       grow[0:1, (eid - r0) * 512:
                                            (eid - r0 + 1) * 512],
                                       start=True, stop=True),
                                   wait=w_, inc=[(PE, 1)])
                            p.emit("scalar", lambda s, w0i=w0i, k=k:
                                   s.copy(gbs[:, w0i + k, :],
                                          ps_up[:, k * 512:(k + 1) * 512]),
                                   wait=[PE, (SE, None)], inc=[(SE, 1)])
                            gb_bank_se[k] = SE.n
                    gb_pe[(pi, ci)] = PE.n

                up_pe, gelu_se, act_w, dnp = {}, {}, {}, {}

                def emit_up(i, ldh_w=ldh_w, pr=pr, gb_bank_se=gb_bank_se):
                    w0 = [wload[(pi, i)], ldh_w]
                    if i == 0:
                        if not gb_bank_se:
                            w0.append((SE, None))
                    else:
                        w0.append((SE, gelu_se[i - 1]))
                    for ec in range(ECN):
                        for dc in range(DC):
                            wec = []
                            if ec == 0 and dc == 0:
                                wec = list(w0)
                            if i == 0 and dc == 0 and ec in gb_bank_se:
                                wec.append((SE, gb_bank_se[ec]))
                            p.emit("tensor",
                                   lambda t, ec=ec, dc=dc, i=i, pr=pr,
                                   sl_=slot_of[(pi, i)]:
                                   t.matmul(
                                       ps_up[:, ec * 512:(ec + 1) * 512],
                                       wu[sl_][:, dc,
                                               ec * 128:(ec + 1) * 128],
                                       h2cb[pr][:, dc, :],
                                       start=(dc == 0), stop=(dc == DC - 1)),
                                   wait=wec,
                                   inc=([(PE, 1)]
                                        if ec == ECN - 1 and dc == DC - 1
                                        else []))
                    up_pe[i] = PE.n

                def emit_act(i, eid, ci=ci):
                    w = [(PE, up_pe[i]), (SE, None)]
                    prev = dnh1_pe.get((pi, ci - 1, i),
                                       dnh1_pe.get((pi - 1, NTCH - 1, i)))
                    if prev is not None:
                        w.append((PE, prev))
                    p.emit("scalar", lambda s, i=i: s.activation(
                        actb[i][:].rearrange("p a b -> p (a b)"),
                        ps_up[:], AF.Gelu),
                        wait=w, inc=[(SE, 1)])
                    gelu_se[i] = SE.n
                    if eid < cfg.NRE:
                        p.emit("vector", lambda v, i=i: v.tensor_tensor(
                            actb[i][:], actb[i][:],
                            gbs[:, i, :].unsqueeze(1).to_broadcast(
                                [128, ECN, 512]), ALU.mult),
                            wait=[(SE, gelu_se[i]), (VE, None)],
                            inc=[(VE, 1)])
                        act_w[i] = [(VE, VE.n)]
                        gm_ve[(pi, ci)] = VE.n
                    else:
                        act_w[i] = [(SE, gelu_se[i])]

                def emit_dn_half(hf, i, act_w=act_w, dnp=dnp, ci=ci,
                                 npl=npl):
                    for hcl in range(DC2):
                        hc = hf * DC2 + hcl
                        for ec in range(ECN):
                            w = []
                            if hcl == 0 and ec == 0:
                                w = list(act_w[i])
                                if i == 0:
                                    ph = 2 * ci + hf - 1
                                    em = evac_mark.get(
                                        (pi, ph),
                                        evac_mark.get((pi - 1,
                                                       2 * NTCH - 1)))
                                    if em is not None:
                                        w.append((SE if em[0] == "se"
                                                  else VE, em[1]))
                            mark = (hcl == DC2 - 1 and ec == ECN - 1
                                    and (hf == 1 or i == npl - 1))
                            p.emit("tensor",
                                   lambda t, hcl=hcl, hc=hc, ec=ec, i=i,
                                   npl=npl, sl_=slot_of[(pi, i)]:
                                   t.matmul(
                                       ps_dn[:, hcl * 512:(hcl + 1) * 512],
                                       wd[sl_][:, ec,
                                               hc * 128:(hc + 1) * 128],
                                       actb[i][:, ec, :],
                                       start=(i == 0 and ec == 0),
                                       stop=(i == npl - 1
                                             and ec == ECN - 1)),
                                   wait=w,
                                   inc=([(PE, 1)] if mark else []))
                            if mark:
                                if i == npl - 1:
                                    dnp[hf] = PE.n
                                if hf == 1:
                                    dnh1_pe[(pi, ci, i)] = PE.n

                def emit_evac(hf, dnp=dnp, ci=ci, ldx_w=ldx_w, ac_w=ac_w,
                              sl=sl):
                    hidx = 2 * ci + hf
                    hb = hidx % 2
                    stw = ac_st.get((pi, hidx - 2),
                                    ac_st.get((pi - 1, 2 * NTCH - 2 + hidx)))
                    if pi == 0:
                        w0 = [(PE, dnp[hf]), (SE, None)]
                        if stw is not None:
                            w0.append(stw)
                        p.emit("scalar", lambda s, hb=hb: s.copy(
                            accout[hb][:],
                            ps_dn[:].rearrange("p (a b) -> p a b", b=512)),
                            wait=w0, inc=[(SE, 1)])
                        evac_mark[(pi, hidx)] = ("se", SE.n)
                    elif pi < NP - 1:
                        w0 = [(PE, dnp[hf]), (VE, None), ac_w[hf]]
                        if stw is not None:
                            w0.append(stw)
                        p.emit("vector", lambda v, hb=hb: v.tensor_tensor(
                            accout[hb][:], accin[hb][:],
                            ps_dn[:].rearrange("p (a b) -> p a b", b=512),
                            ALU.add),
                            wait=w0, inc=[(VE, 1)])
                        evac_mark[(pi, hidx)] = ("ve", VE.n)
                    if pi < NP - 1:
                        ev = evac_mark[(pi, hidx)]
                        p.emit("scalar", lambda s, sl=sl, hb=hb, hf=hf:
                               s.dma_start(
                                   out=af_[:, hf * DC2:(hf + 1) * DC2, sl],
                                   in_=accout[hb][:]),
                               wait=[(SE if ev[0] == "se" else VE, ev[1])] +
                                    ([stw] if stw is not None else []),
                               inc=[(SAC[hb], 16)])
                        ac_st[(pi, hidx)] = (SAC[hb], SAC[hb].n)
                    else:
                        ow = out_st.get(hidx - 1)
                        for hcl in range(DC2):
                            hc = hf * DC2 + hcl
                            w0 = [(PE, dnp[hf]), (VE, None)]
                            if hcl == 0:
                                w0 += [ldx_w[hf], ac_w[hf],
                                       (VE, P0["mod_ve"])]
                                if ow is not None:
                                    w0.append(ow)
                            p.emit("vector",
                                   lambda v, hcl=hcl, hb=hb: v.tensor_tensor(
                                       outc[0][:, hcl, :],
                                       accin[hb][:, hcl, :],
                                       ps_dn[:, hcl * 512:(hcl + 1) * 512],
                                       ALU.add),
                                   wait=w0, inc=[(VE, 1)])
                            p.emit("vector",
                                   lambda v, hcl=hcl, hc=hc, hb=hb:
                                   v.scalar_tensor_tensor(
                                       outc[0][:, hcl, :],
                                       outc[0][:, hcl, :],
                                       modfm[:, 5, hc:hc + 1],
                                       x1cb[hb][:, hcl, :],
                                       ALU.mult, ALU.add),
                                   wait=[(VE, None)], inc=[(VE, 1)])
                        comb_done[hidx] = VE.n
                        evac_mark[(pi, hidx)] = ("ve", VE.n)
                        p.emit("scalar", lambda s, sl=sl, hb=hb, hf=hf:
                               s.dma_start(
                                   out=of[:, hf * DC2:(hf + 1) * DC2, sl],
                                   in_=outc[0][:]),
                               wait=[(VE, None)] +
                                    ([ow] if ow is not None else []),
                               inc=[(STO[hb], 16)])
                        out_st[hidx] = (STO[hb], STO[hb].n)

                emit_up(0)
                if pending is not None:
                    pending()
                    pending = None
                emit_half_loads(1)
                emit_act(0, plist[0])
                for i in range(1, npl):
                    emit_up(i)
                    emit_act(i, plist[i])
                    emit_dn_half(0, i - 1)
                emit_dn_half(0, npl - 1)
                emit_evac(0)

                def mk_pending(emit_dn_half=emit_dn_half,
                               emit_evac=emit_evac, npl=npl):
                    def run():
                        for i in range(npl):
                            emit_dn_half(1, i)
                        emit_evac(1)
                    return run

                pending = mk_pending()
                chunk_pe[(pi, ci)] = PE.n
            if pending is not None:
                pending()
                pending = None
            pass_pe[pi] = PE.n
        p.emit("sync", None, wait=[STO[0], STO[1]])
        es.close()

    p4()

    with ctx, nc.Block() as block:
        p.build(block)
    return nc


def _prep_consts(cfg, inputs):
    wqkv = np.asarray(inputs["L0_attn_w_qkv"], dtype=np.float32)
    D = cfg.D
    hhq, hhk = _householders(cfg)
    return {
        "wqkT": np.ascontiguousarray(wqkv[:, :2 * D].T),
        "wv": np.ascontiguousarray(wqkv[:, 2 * D:3 * D]),
        "wout": np.ascontiguousarray(
            np.asarray(inputs["L0_attn_w_out"], dtype=np.float32)),
        "hhq": np.ascontiguousarray(hhq.reshape(cfg.H * cfg.HD, cfg.HD)),
        "hhk": np.ascontiguousarray(hhk.reshape(cfg.H * cfg.HD, cfg.HD)),
        "hs": np.ascontiguousarray(
            np.asarray(inputs["L0_attn_head_scales"],
                       dtype=np.float32).reshape(1, cfg.H)),
        "wmod": np.ascontiguousarray(
            np.asarray(inputs["L0_wmod"], dtype=np.float32)),
        "wsu": np.ascontiguousarray(
            np.asarray(inputs["L0_ffn_shared_up"], dtype=np.float32)
            .transpose(0, 2, 1).reshape(cfg.NSE * D, cfg.E)),
        "wsd": np.ascontiguousarray(
            np.asarray(inputs["L0_ffn_shared_down"], dtype=np.float32)
            .transpose(0, 2, 1).reshape(cfg.NSE * cfg.E, D)),
        "wru": np.ascontiguousarray(
            np.asarray(inputs["L0_ffn_routed_up"], dtype=np.float32)
            .transpose(0, 2, 1).reshape(cfg.NRE * D, cfg.E)),
        "wrd": np.ascontiguousarray(
            np.asarray(inputs["L0_ffn_routed_down"], dtype=np.float32)
            .transpose(0, 2, 1).reshape(cfg.NRE * cfg.E, D)),
        "wrt": np.ascontiguousarray(
            np.asarray(inputs["L0_ffn_router_w"], dtype=np.float32).T),
        "identin": np.eye(128, dtype=np.float32),
    }


def _prep_core(cfg, inputs, core, consts):
    b, par = core // 2, core % 2
    x = np.asarray(inputs["L0_attn_x"], dtype=np.float32)[b]
    qr = _qrows(cfg, par)
    m = {
        "xT": np.ascontiguousarray(
            np.concatenate([x.T, x[qr].T], axis=1)),
        "cfm": np.ascontiguousarray(
            np.asarray(inputs["L0_c"], dtype=np.float32)[b]
            .reshape(cfg.DC, 128).T),
        "mask01": _mask01(par),
    }
    m.update(consts)
    return m


def kernel(**inputs):
    cfg = _Cfg()
    consts = _prep_consts(cfg, inputs)
    in_maps = [_prep_core(cfg, inputs, c, consts) for c in range(cfg.NCORES)]
    nc = _build(cfg)
    res = run_bass_kernel_spmd(nc, in_maps, core_ids=list(range(cfg.NCORES)))
    out = np.empty((cfg.NB, cfg.TKV, cfg.D), dtype=np.float32)
    for c in range(cfg.NCORES):
        b, par = c // 2, c % 2
        out[b, _qrows(cfg, par)] = np.asarray(res.results[c]["outT"]).T
    return out


# version 22
# speedup vs baseline: 1.0991x; 1.0637x over previous
import math
import numpy as np
import ml_dtypes
from contextlib import ExitStack

import concourse.bass as bass
import concourse.mybir as mybir
from concourse.bass_utils import run_bass_kernel_spmd

F32 = mybir.dt.float32
BF16 = mybir.dt.bfloat16
AF = mybir.ActivationFunctionType
ALU = mybir.AluOpType
AX = mybir.AxisListType.X

ENGINES = ("tensor", "vector", "scalar", "gpsimd", "sync")


class _Sem:
    def __init__(self, nc, name):
        self._cm = nc.semaphore(name)
        self.h = self._cm.__enter__()
        self.n = 0
        self.name = name


class _Prog:

    def __init__(self, nc):
        self.nc = nc
        self.ops = {e: [] for e in ENGINES}

    def sem(self, name):
        return _Sem(self.nc, name)

    def emit(self, engine, fn, wait=(), inc=()):
        wv = []
        for w in wait:
            if w is None:
                continue
            if isinstance(w, _Sem):
                wv.append((w, w.n))
            else:
                s, v = w
                wv.append((s, s.n if v is None else v))
        iv = []
        for s, k in inc:
            s.n += k
            iv.append((s, k))
        self.ops[engine].append((wv, fn, iv))

    def fence(self, marks):
        for e in ENGINES:
            self.emit(e, None, wait=marks)

    def build(self, block):
        def mk(engine):
            def body(eng):
                for wv, fn, iv in self.ops[engine]:
                    for s, v in wv:
                        if v > 0:
                            eng.wait_ge(s.h, v)
                    if fn is not None:
                        instr = fn(eng)
                        for s, k in iv:
                            instr = instr.then_inc(s.h, k)
            return body

        for engine in ENGINES:
            if self.ops[engine]:
                getattr(block, engine)(mk(engine))


class _Cfg:
    def __init__(self, D=1024, H=8, E=512, NRE=16, NSE=2, TOPK=4, KVT=64,
                 NB=4, eps=1e-5, G=4):
        self.D, self.H, self.E = D, H, E
        self.NRE, self.NSE, self.TOPK = NRE, NSE, TOPK
        self.NE = NRE + NSE
        self.KVT, self.NB = KVT, NB
        self.eps = eps
        self.HD = 128
        assert D == H * self.HD
        self.DC = D // 128
        self.ECN = E // 128
        self.TKV = KVT * 128
        self.NS = KVT // 2
        self.NQ = self.NS * 128
        self.NG = self.NS // 4
        assert self.NS % 4 == 0
        self.TTOT = self.TKV + self.NQ
        assert self.TTOT % 512 == 0
        self.NCH = self.TTOT // 512
        self.KVCH = self.TKV // 512
        self.QCH = self.NQ // 512
        self.NTC3 = self.NQ // 512
        self.NTC4 = self.NQ // 256
        self.NCORES = 2 * NB
        assert H % 2 == 0
        self.NPAIR = H // 2
        ids = list(range(self.NE))
        self.passes = [ids[i:i + G] for i in range(0, self.NE, G)]
        self.scale = 1.0 / math.sqrt(self.HD)


def _dirs(n):
    t = np.array([[(i >> 2 & 1) * 2 - 1, (i >> 1 & 1) * 2 - 1, (i & 1) * 2 - 1]
                  for i in range(8)], dtype=np.float32)
    return (t / np.sqrt(3.0, dtype=np.float32))[:n]


def _householders(cfg):
    H, HD, s = cfg.H, cfg.HD, cfg.scale
    d = _dirs(H)
    hhq = np.zeros((H, HD, HD), dtype=np.float32)
    hhk = np.zeros((H, HD, HD), dtype=np.float32)
    for h in range(H):
        u = np.zeros(HD, dtype=np.float64)
        u[:3] = d[h].astype(np.float64)
        v = u.copy()
        v[0] -= 1.0
        Hm = np.eye(HD) - 2.0 * np.outer(v, v) / (v @ v)
        hhq[h] = Hm.astype(np.float32)
        Hk = Hm.copy()
        Hk[:, 1:] *= s
        hhk[h] = Hk.astype(np.float32)
    return hhq, hhk


def _qrows(cfg, p):
    return np.concatenate(
        [np.arange(128 * (2 * i + p), 128 * (2 * i + p) + 128)
         for i in range(cfg.NS)])


def _mask01(p):
    k = np.arange(128)[:, None]
    q = np.arange(128)[None, :]
    tri = (k <= q).astype(np.float32)
    if p == 1:
        a, b = np.ones((128, 128), np.float32), tri
    else:
        a, b = tri, np.zeros((128, 128), np.float32)
    return np.ascontiguousarray(np.concatenate([a, b], axis=1))


def _active(cfg, g, kt):
    act = 0
    for j in range(4):
        if kt < 8 * g + 2 * j + 2:
            act += 1
    return act


def _build(cfg):
    nc = bass.Bass("TRN2", target_bir_lowering=False)
    D, DC, H, HD, E, ECN = cfg.D, cfg.DC, cfg.H, cfg.HD, cfg.E, cfg.ECN

    def din(name, shape, dt=F32):
        return nc.dram_tensor(name, shape, dt, kind="ExternalInput").ap()

    xT = din("xT", [D, cfg.TTOT])
    wqkT = din("wqkT", [2 * D, D])
    wv_d = din("wv", [D, D])
    wout_d = din("wout", [D, D])
    hhq_d = din("hhq", [H * HD, HD])
    hhk_d = din("hhk", [H * HD, HD])
    hs_d = din("hs", [1, H])
    wmod_d = din("wmod", [D, 6 * D])
    cfm_d = din("cfm", [128, DC])
    wsu_d = din("wsu", [cfg.NSE * D, E])
    wsd_d = din("wsd", [cfg.NSE * E, D])
    wru_d = din("wru", [cfg.NRE * D, E])
    wrd_d = din("wrd", [cfg.NRE * E, D])
    wrt_d = din("wrt", [D, cfg.NRE])
    mask_d = din("mask01", [128, 256])
    iden_d = din("identin", [128, 128])

    outT = nc.dram_tensor("outT", [D, cfg.NQ], F32, kind="ExternalOutput").ap()
    h1T = nc.dram_tensor("h1T", [D, cfg.TTOT], BF16, kind="Internal").ap()
    h2T = nc.dram_tensor("h2T", [D, cfg.NQ], BF16, kind="Internal").ap()
    x1T = nc.dram_tensor("x1T", [D, cfg.NQ], F32, kind="Internal").ap()
    atT = nc.dram_tensor("atT", [D, cfg.NQ], BF16, kind="Internal").ap()
    gate_hbm = nc.dram_tensor("gate_hbm", [cfg.NRE, cfg.NQ], BF16,
                              kind="Internal").ap()
    accT = nc.dram_tensor("accT", [D, cfg.NQ], BF16, kind="Internal").ap()
    NEX = cfg.NRE + cfg.NSE
    wub_h = nc.dram_tensor("wub", [NEX * D, E], BF16, kind="Internal").ap()
    wdb_h = nc.dram_tensor("wdb", [NEX * E, D], BF16, kind="Internal").ap()

    def fmr(ap_):
        return ap_.rearrange("(c p) t -> p c t", p=128)

    ctx = ExitStack()

    def sb(name, shape, dt=F32):
        return ctx.enter_context(nc.sbuf_tensor(name, shape, dt))

    p = _Prog(nc)
    LDW_ = p.sem("ldw")
    LDC = p.sem("ldc")
    PE = p.sem("pe")
    VE = p.sem("ve")
    SE = p.sem("se")
    GP = p.sem("gp")
    STH1 = [p.sem(f"sth1_{i}") for i in range(2)]
    STA = [p.sem(f"sta_{i}") for i in range(2)]
    STH2 = [p.sem(f"sth2_{i}") for i in range(2)]
    STX1 = [p.sem(f"stx1_{i}") for i in range(2)]
    STO = [p.sem(f"sto_{i}") for i in range(2)]
    LX = [p.sem(f"lx_{i}") for i in range(2)]
    LXB = [p.sem(f"lxb_{i}") for i in range(2)]
    LH1 = [p.sem(f"lh1_{i}") for i in range(2)]
    LAT = [p.sem(f"lat_{i}") for i in range(2)]
    LH2 = [p.sem(f"lh2_{i}") for i in range(2)]
    LX1 = [p.sem(f"lx1_{i}") for i in range(2)]
    LGR = p.sem("lgr")
    LFW = [p.sem(f"lfw_{i}") for i in range(8)]
    LGA = [p.sem(f"lga_{i}") for i in range(2)]
    LAC = [p.sem(f"lac_{i}") for i in range(2)]
    SAC = [p.sem(f"sac_{i}") for i in range(2)]

    onescol = sb("onescol", [128, 1], BF16)
    ones128 = sb("ones128", [128, 128], BF16)
    epscol = sb("epscol", [128, 1], F32)
    neghalf = sb("neghalf", [128, 1], F32)
    negone = sb("negone", [128, 1], F32)
    onesrow = sb("onesrow", [1, 128], F32)
    onesrow_bf = sb("onesrow_bf", [1, 128], BF16)
    one1 = sb("one1", [1, 1], F32)
    ident = sb("ident", [128, 128], F32)
    maskb = sb("maskb", [128, 256], BF16)
    modfm = sb("modfm", [128, 6, DC], F32)
    hsc = sb("hsc", [128, H], F32)
    ctx_w3 = ExitStack()
    sbw3 = lambda n, s_, d=F32: ctx_w3.enter_context(
        nc.sbuf_tensor(n, s_, d, side="right"))
    woutb = sbw3("woutb", [128, DC, D], BF16)
    wrtb = sbw3("wrtb", [128, DC, cfg.NRE], BF16)
    ctx_w2 = ExitStack()
    sbw2 = lambda n, s_, d=F32: ctx_w2.enter_context(
        nc.sbuf_tensor(n, s_, d, side="right"))
    wqrot = sbw2("wqrot", [128, H, DC, 128], BF16)
    wkrot = sbw2("wkrot", [128, H, DC, 128], BF16)
    wvb = sbw2("wvb", [128, DC, D], BF16)

    def p0():
        p.emit("vector", lambda v: v.memset(onescol[:], 1.0), inc=[(VE, 1)])
        p.emit("vector", lambda v: v.memset(ones128[:], 1.0), inc=[(VE, 1)])
        p.emit("vector", lambda v: v.memset(epscol[:], float(cfg.eps)),
               inc=[(VE, 1)])
        p.emit("vector", lambda v: v.memset(neghalf[:], -0.5), inc=[(VE, 1)])
        p.emit("vector", lambda v: v.memset(negone[:], -1.0), inc=[(VE, 1)])
        p.emit("vector", lambda v: v.memset(onesrow[:], 1.0), inc=[(VE, 1)])
        p.emit("vector", lambda v: v.memset(one1[:], 1.0), inc=[(VE, 1)])
        p.emit("vector", lambda v: v.memset(onesrow_bf[:], 1.0), inc=[(VE, 1)])
        const_ve = VE.n
        p.emit("sync", lambda g: g.dma_start(out=ident[:], in_=iden_d[:]),
               inc=[(LDW_, 16)])
        p.emit("gpsimd", lambda g: g.dma_start(out=maskb[:], in_=mask_d[:]),
               inc=[(LDC, 16)])
        p.emit("gpsimd", lambda g: g.dma_start(
            out=wrtb[:], in_=wrt_d.rearrange("(c p) r -> p c r", p=128)),
            inc=[(LDC, 16)])
        p.emit("gpsimd", lambda g: g.dma_start(out=wvb[:], in_=fmr(wv_d)),
               inc=[(LDC, 16)])
        p.emit("gpsimd", lambda g: g.dma_start(out=woutb[:], in_=fmr(wout_d)),
               inc=[(LDC, 16)])

        es = ExitStack()
        sbl = lambda n, s_, d=F32: es.enter_context(nc.sbuf_tensor(n, s_, d))
        psl = lambda n, s_, d=F32: es.enter_context(nc.psum_tensor(n, s_, d))
        cfm = sbl("p0_cfm", [128, DC])
        mod_row = sbl("p0_mod_row", [1, 6 * D])
        csg = sbl("p0_csg", [128, DC])
        scb = sbl("p0_scb", [128, DC], BF16)
        hs_sb = sbl("p0_hs", [1, H])
        nmodw = 6 if (6 * D // 512) % 6 == 0 else 1
        wmod_sb = [sbl(f"p0_wmod{i}", [128, DC, (6 * D) // nmodw])
                   for i in range(2 if nmodw > 1 else 1)]
        hhq_sb = sbl("p0_hhq", [128, H, HD])
        hhk_sb = sbl("p0_hhk", [128, H, HD])
        hhkb_sb = sbl("p0_hhkb", [128, H, HD], BF16)
        hhqb_sb = sbl("p0_hhqb", [128, H, HD], BF16)
        wqkT_sb = sbl("p0_wqkT", [128, 2 * H, D], BF16)
        gsc = sbl("p0_gsc", [128, DC])
        ps_mod = [psl(f"p0_ps_mod{i}", [1, 512]) for i in range(2)]
        ps_fm = psl("p0_ps_fm", [128, 64])
        ps_hs = psl("p0_ps_hs", [128, H])
        ps_rot = [psl(f"p0_ps_rot{i}", [128, 512]) for i in range(2)]

        p.emit("sync", lambda g: g.dma_start(out=cfm[:], in_=cfm_d[:]),
               inc=[(LDW_, 16)])
        p.emit("sync", lambda g: g.dma_start(out=hs_sb[:], in_=hs_d[:]),
               inc=[(LDW_, 16)])
        p.emit("sync", lambda g: g.dma_start(
            out=hhq_sb[:], in_=hhq_d.rearrange("(h p) c -> p h c", p=128)),
            inc=[(LDW_, 16)])
        p.emit("sync", lambda g: g.dma_start(
            out=hhk_sb[:], in_=hhk_d.rearrange("(h p) c -> p h c", p=128)),
            inc=[(LDW_, 16)])
        p.emit("gpsimd", lambda g: g.dma_start(
            out=wqkT_sb[:], in_=wqkT.rearrange("(c p) d -> p c d", p=128)),
            inc=[(LDC, 16)])
        cfm_ld = hs_ld = hh_ld = LDW_.n
        mask_ld = wrt_ld = wv_ld = wout_ld = wqk_ld = LDC.n
        WMD = p.sem("wmd")

        p.emit("scalar", lambda s: s.activation(csg[:], cfm[:], AF.Sigmoid),
               wait=[(LDW_, cfm_ld)], inc=[(SE, 1)])
        p.emit("vector", lambda v: v.tensor_mul(csg[:], csg[:], cfm[:]),
               wait=[SE, (VE, None)], inc=[(VE, 1)])
        p.emit("vector", lambda v: v.tensor_copy(scb[:], csg[:]),
               wait=[(VE, None)], inc=[(VE, 1)])
        scb_ve = VE.n

        p.emit("tensor", lambda t: t.matmul(ps_hs[:], onesrow[:], hs_sb[:],
                                            start=True, stop=True),
               wait=[(LDW_, hs_ld), (VE, const_ve)], inc=[(PE, 1)])
        p.emit("vector", lambda v: v.tensor_scalar(
            hsc[:], ps_hs[:], cfg.scale, None, ALU.add),
            wait=[PE, (VE, None)], inc=[(VE, 1)])
        hsc_ve = VE.n
        for h in range(H):
            p.emit("vector", lambda v, h=h: v.tensor_tensor(
                hhk_sb[:, h, 0:1], hhk_sb[:, h, 0:1], hsc[:, h:h + 1],
                ALU.mult),
                wait=[(VE, hsc_ve), (LDW_, hh_ld), (VE, None)], inc=[(VE, 1)])
        p.emit("vector", lambda v: v.tensor_copy(hhkb_sb[:], hhk_sb[:]),
               wait=[(VE, None)], inc=[(VE, 1)])
        p.emit("vector", lambda v: v.tensor_copy(hhqb_sb[:], hhq_sb[:]),
               wait=[(LDW_, hh_ld), (VE, None)], inc=[(VE, 1)])
        hhk_ve = VE.n

        nmodch = 6 * D // 512
        wavech = nmodch // nmodw
        evac_ve = {}
        wave_pe = {}
        wmodf = wmod_d.rearrange("(c p) f -> p c f", p=128)
        nbuf = 2 if nmodw > 1 else 1
        for wv_ in range(nmodw):
            wlo = wv_ * (6 * D // nmodw)
            wb = wv_ % nbuf
            p.emit("sync", lambda g, wlo=wlo, wb=wb: g.dma_start(
                out=wmod_sb[wb][:],
                in_=wmodf[:, :, wlo:wlo + (6 * D) // nmodw]),
                wait=([(PE, wave_pe[wv_ - nbuf])] if wv_ >= nbuf else []),
                inc=[(WMD, 16)])
            wmod_ld = WMD.n
            for fi in range(wavech):
                f = wv_ * wavech + fi
                par = f % 2
                w0 = [(VE, scb_ve), (WMD, wmod_ld)]
                if f >= 2:
                    w0.append((VE, evac_ve[f - 2]))
                for dc in range(DC):
                    p.emit("tensor", lambda t, dc=dc, fi=fi, par=par, wb=wb:
                           t.matmul(
                               ps_mod[par][:], csg[:, dc:dc + 1],
                               wmod_sb[wb][:, dc, fi * 512:(fi + 1) * 512],
                               start=(dc == 0), stop=(dc == DC - 1)),
                           wait=(w0 if dc == 0 else []),
                           inc=([(PE, 1)] if dc == DC - 1 else []))
                p.emit("vector", lambda v, f=f, par=par: v.tensor_copy(
                    mod_row[:, f * 512:(f + 1) * 512], ps_mod[par][:]),
                    wait=[PE, (VE, None)], inc=[(VE, 1)])
                evac_ve[f] = VE.n
            wave_pe[wv_] = PE.n
        modrow_ve = VE.n

        n6 = 6 * DC
        for j in range(n6):
            p.emit("tensor", lambda t, j=j: t.matmul(
                ps_fm[:, j:j + 1], mod_row[:, j * 128:(j + 1) * 128], one1[:],
                start=True, stop=True),
                wait=([(VE, modrow_ve), (VE, const_ve)] if j == 0 else []),
                inc=[(PE, 1)])
        p.emit("scalar", lambda s: s.copy(
            modfm[:].rearrange("p a b -> p (a b)"), ps_fm[:, 0:n6]),
            wait=[PE, (SE, None)], inc=[(SE, 1)])
        raw_se = SE.n

        for vi in (1, 4):
            p.emit("vector", lambda v, vi=vi: v.tensor_scalar_add(
                modfm[:, vi, :], modfm[:, vi, :], 1.0),
                wait=[(SE, raw_se), (VE, None)], inc=[(VE, 1)])
        for vi in (2, 5):
            p.emit("vector", lambda v, vi=vi: v.tensor_scalar(
                gsc[:], modfm[:, vi, :], 0.0, None, ALU.is_ge),
                wait=[(SE, raw_se), (VE, None)], inc=[(VE, 1)])
            p.emit("vector", lambda v: v.tensor_scalar(
                gsc[:], gsc[:], 2.0, -1.0, ALU.mult, ALU.add),
                wait=[(VE, None)], inc=[(VE, 1)])
            p.emit("vector", lambda v, vi=vi: v.tensor_tensor(
                modfm[:, vi, :], modfm[:, vi, :], gsc[:], ALU.mult),
                wait=[(VE, None)], inc=[(VE, 1)])
            p.emit("vector", lambda v, vi=vi: v.tensor_scalar_max(
                modfm[:, vi, :], modfm[:, vi, :], 0.25),
                wait=[(VE, None)], inc=[(VE, 1)])
            p.emit("vector", lambda v, vi=vi: v.tensor_tensor(
                modfm[:, vi, :], modfm[:, vi, :], gsc[:], ALU.mult),
                wait=[(VE, None)], inc=[(VE, 1)])
        mod_ve = VE.n

        evac_se = {}
        ri = 0
        for side in range(2):
            hh_sb_ = hhqb_sb if side == 0 else hhkb_sb
            dst = wqrot if side == 0 else wkrot
            for h in range(H):
                for d0 in range(0, DC, 4):
                    dcs = list(range(d0, min(d0 + 4, DC)))
                    par = ri % 2
                    w0 = [(VE, hhk_ve), (LDC, wqk_ld)]
                    if ri >= 2:
                        w0.append((SE, evac_se[ri - 2]))
                    crow = h if side == 0 else H + h
                    for i, dc in enumerate(dcs):
                        p.emit("tensor",
                               lambda t, i=i, dc=dc, crow=crow, hh_sb_=hh_sb_,
                               h=h, par=par:
                               t.matmul(ps_rot[par][:, i * 128:(i + 1) * 128],
                                        wqkT_sb[:, crow,
                                                dc * 128:(dc + 1) * 128],
                                        hh_sb_[:, h, :],
                                        start=True, stop=True),
                               wait=(w0 if i == 0 else []),
                               inc=([(PE, 1)] if i == len(dcs) - 1 else []))
                    p.emit("scalar",
                           lambda s, dst=dst, h=h, d0=d0, n=len(dcs), par=par:
                           s.copy(dst[:, h, d0:d0 + n, :],
                                  ps_rot[par][:, 0:n * 128]),
                           wait=[PE, (SE, None)], inc=[(SE, 1)])
                    evac_se[ri] = SE.n
                    ri += 1
        rot_se = SE.n
        es.close()
        return dict(mod_ve=mod_ve, rot_se=rot_se, mask_ld=mask_ld,
                    wrt_ld=wrt_ld, wv_ld=wv_ld, wout_ld=wout_ld,
                    const_ve=const_ve)

    P0 = p0()
    p.fence([(PE, None), (VE, None), (SE, None), (GP, None), LDW_, LDC])

    def ln_phase(prefix, src_of, n_ch, dst_of, ca_idx, sa_idx, st_sems,
                 src_wait_of=None):
        es = ExitStack()
        sbl = lambda n, s_, d=F32: es.enter_context(
            nc.sbuf_tensor(prefix + n, s_, d))
        psl = lambda n, s_, d=F32: es.enter_context(
            nc.psum_tensor(prefix + n, s_, d))
        xb = [sbl(f"xb{i}", [128, DC, 512], BF16) for i in range(2)]
        xq = [sbl(f"xq{i}", [128, DC, 512], BF16) for i in range(2)]
        h_c = [sbl(f"h{i}", [128, DC, 512], BF16) for i in range(2)]
        rf = [sbl(f"rf{i}", [128, 3, 512]) for i in range(2)]
        rb = [sbl(f"rb{i}", [128, 2, 512], BF16) for i in range(2)]
        t1b = sbl("t1b", [128, DC, 512], BF16)
        ps_s = [psl(f"ps_s{i}", [128, 512]) for i in range(2)]
        ps_q = [psl(f"ps_q{i}", [128, 512]) for i in range(2)]

        dve_done, st_done, aff_done = {}, {}, {}
        ld_marks = {}

        def emit_loads(ci):
            pr = ci % 2
            w_free = ([(VE, dve_done[ci - 2])]
                      if ci >= 2 else [])
            w_src = src_wait_of(ci) if src_wait_of else []
            p.emit("gpsimd", lambda g, ci=ci, pr=pr: g.dma_start(
                out=xb[pr][:], in_=src_of(ci)),
                wait=w_free + w_src, inc=[(LXB[pr], 16)])
            ld_marks[ci] = (LXB[pr], LXB[pr].n)

        emit_loads(0)
        if n_ch > 1:
            emit_loads(1)
        for ci in range(n_ch):
            pr = ci % 2
            ldc_w = ld_marks[ci]
            p.emit("scalar", lambda s, pr=pr: s.activation(
                xq[pr][:], xb[pr][:], AF.Square),
                wait=[ldc_w, (SE, None)], inc=[(SE, 1)])
            sq_se = SE.n
            pe_w = [ldc_w, (VE, P0["const_ve"])] + \
                ([(VE, dve_done[ci - 2])] if ci >= 2 else [])
            for dc in range(DC):
                p.emit("tensor", lambda t, dc=dc, pr=pr: t.matmul(
                    ps_s[pr][:], ones128[:], xb[pr][:, dc, :],
                    start=(dc == 0), stop=(dc == DC - 1)),
                    wait=(pe_w if dc == 0 else []))
            for dc in range(DC):
                p.emit("tensor", lambda t, dc=dc, pr=pr: t.matmul(
                    ps_q[pr][:], ones128[:], xq[pr][:, dc, :],
                    start=(dc == 0), stop=(dc == DC - 1)),
                    wait=([(SE, sq_se)] if dc == 0 else []),
                    inc=([(PE, 1)] if dc == DC - 1 else []))
            pe_stats = PE.n
            r, rb_ = rf[pr], rb[pr]
            rw = [(VE, dve_done[ci - 2])] if ci >= 2 else []
            p.emit("vector", lambda v, r=r, pr=pr: v.tensor_scalar(
                r[:, 0, :], ps_s[pr][:], 1.0 / D, None, ALU.mult),
                wait=[(PE, pe_stats), (VE, None)] + rw, inc=[(VE, 1)])
            p.emit("vector", lambda v, r=r, pr=pr: v.tensor_scalar(
                r[:, 1, :], ps_q[pr][:], 1.0 / D, None, ALU.mult),
                wait=[(VE, None)], inc=[(VE, 1)])
            p.emit("vector", lambda v, r=r: v.tensor_tensor(
                r[:, 2, :], r[:, 0, :], r[:, 0, :], ALU.mult),
                wait=[(VE, None)], inc=[(VE, 1)])
            p.emit("vector", lambda v, r=r: v.tensor_tensor(
                r[:, 2, :], r[:, 1, :], r[:, 2, :], ALU.subtract),
                wait=[(VE, None)], inc=[(VE, 1)])
            p.emit("scalar", lambda s, r=r: s.activation(
                r[:, 2, :], r[:, 2, :], AF.Ln, bias=epscol[:]),
                wait=[VE, (SE, None)], inc=[(SE, 1)])
            p.emit("scalar", lambda s, r=r, rb_=rb_: s.activation(
                rb_[:, 0, :], r[:, 2, :], AF.Exp, scale=neghalf[:]),
                wait=[(SE, None)], inc=[(SE, 1)])
            rstd_se = SE.n
            p.emit("vector", lambda v, r=r, rb_=rb_: v.scalar_tensor_tensor(
                rb_[:, 1, :], r[:, 0, :], -1.0, rb_[:, 0, :],
                ALU.mult, ALU.mult),
                wait=[(SE, rstd_se), (VE, None)], inc=[(VE, 1)])
            rowm = VE.n
            p.emit("vector", lambda v, pr=pr, rb_=rb_: v.tensor_tensor(
                t1b[:], xb[pr][:],
                rb_[:, 0, :].unsqueeze(1).to_broadcast([128, DC, 512]),
                ALU.mult),
                wait=[(VE, None), ldc_w, (VE, rowm), (VE, P0["mod_ve"])],
                inc=[(VE, 1)])
            p.emit("vector", lambda v, rb_=rb_: v.tensor_tensor(
                t1b[:], t1b[:],
                rb_[:, 1, :].unsqueeze(1).to_broadcast([128, DC, 512]),
                ALU.add),
                wait=[(VE, None)], inc=[(VE, 1)])
            for dc in range(DC):
                hw = [st_done[ci - 2]] if (dc == 0 and ci >= 2) else []
                p.emit("vector", lambda v, dc=dc, pr=pr:
                       v.tensor_scalar(
                           h_c[pr][:, dc, :], t1b[:, dc, :],
                           modfm[:, ca_idx, dc:dc + 1],
                           modfm[:, sa_idx, dc:dc + 1],
                           ALU.mult, ALU.add),
                       wait=[(VE, None)] + hw, inc=[(VE, 1)])
            dve_done[ci] = VE.n
            aff_done[ci] = VE.n
            if ci + 2 < n_ch:
                emit_loads(ci + 2)
            p.emit("sync", lambda g, ci=ci, pr=pr: g.dma_start(
                out=dst_of(ci), in_=h_c[pr][:]),
                wait=[(VE, aff_done[ci])] +
                     ([st_done[ci - 2]] if ci >= 2 else []),
                inc=[(st_sems[pr], 16)])
            st_done[ci] = (st_sems[pr], st_sems[pr].n)
        es.close()
        return st_done

    p1_st = ln_phase(
        "p1_",
        src_of=lambda ci: fmr(xT)[:, :, ci * 512:(ci + 1) * 512],
        n_ch=cfg.NCH,
        dst_of=lambda ci: fmr(h1T)[:, :, ci * 512:(ci + 1) * 512],
        ca_idx=1, sa_idx=0, st_sems=STH1)
    p.fence([(PE, None), (VE, None), (SE, None), (GP, None),
             p1_st[cfg.NCH - 1], p1_st[cfg.NCH - 2],
             LX[0], LX[1], LXB[0], LXB[1]])

    WCPA = p.sem("wcpa")
    WCPB = p.sem("wcpb")

    def emit_precast(first_wait):
        na = min(len(cfg.passes[0]) + (len(cfg.passes[1])
                                       if len(cfg.passes) > 1 else 0), NEX)
        for r in range(NEX):
            sem = WCPA if r < na else WCPB
            usrc = wru_d if r < cfg.NRE else wsu_d
            dsrc = wrd_d if r < cfg.NRE else wsd_d
            ui = r if r < cfg.NRE else r - cfg.NRE
            p.emit("gpsimd", lambda g, r=r, usrc=usrc, ui=ui: g.dma_start(
                out=wub_h[r * D:(r + 1) * D, :],
                in_=usrc[ui * D:(ui + 1) * D, :]),
                wait=(first_wait if r == 0 else []), inc=[(sem, 16)])
            p.emit("gpsimd", lambda g, r=r, dsrc=dsrc, ui=ui: g.dma_start(
                out=wdb_h[r * E:(r + 1) * E, :],
                in_=dsrc[ui * E:(ui + 1) * E, :]), inc=[(sem, 16)])

    def p2():
        es = ExitStack()
        sbl = lambda n, s_, d=F32: es.enter_context(nc.sbuf_tensor(n, s_, d))
        psl = lambda n, s_, d=F32: es.enter_context(nc.psum_tensor(n, s_, d))
        h1c = [sbl(f"a_h1c{i}", [128, DC, 512], BF16) for i in range(2)]
        Ksb = sbl("a_K", [128, 2, cfg.TKV], BF16)
        Vsb = sbl("a_V", [128, cfg.KVT, 2, 128], BF16)
        Qsb = sbl("a_Q", [128, 2, cfg.NQ], BF16)
        expb = [sbl(f"a_exp{i}", [128, 1024], BF16) for i in range(2)]
        recb = [sbl(f"a_recb{i}", [128, 512]) for i in range(2)]
        aoutg = [sbl(f"a_aout{i}", [128, 512], BF16) for i in range(2)]
        ps_p = [psl(f"a_ps_p{i}", [128, 512]) for i in range(2)]
        ps_t = [psl(f"a_ps_t{i}", [128, 1024]) for i in range(2)]
        ps_l = [psl(f"a_ps_l{i}", [128, 512]) for i in range(2)]
        h1f = fmr(h1T)

        proj_pe = {}
        aout_st = {}
        ag_st = {}
        gctr = 0
        ld2_marks = {}

        def emit_h1load(pair, ci, w_free):
            pr = ci % 2
            p.emit("sync", lambda g, ci=ci, pr=pr: g.dma_start(
                out=h1c[pr][:], in_=h1f[:, :, ci * 512:(ci + 1) * 512]),
                wait=w_free + [p1_st[ci]], inc=[(LH1[pr], 16)])
            ld2_marks[(pair, ci)] = (LH1[pr], LH1[pr].n)
        p.emit("vector", lambda v: v.memset(ps_t[0][:], 0.0),
               wait=[(VE, None)], inc=[(VE, 1)])
        p.emit("vector", lambda v: v.memset(ps_t[1][:], 0.0),
               wait=[(VE, None)], inc=[(VE, 1)])
        groupend_ve = VE.n
        pt_init_ve = VE.n
        prev_exp = {}
        ln_se = {}
        aout_ve = {}
        last_pair = None
        for pair in getattr(cfg, "pair_order", range(cfg.NPAIR)):
            h0 = 2 * pair
            evac_se = {}
            pi = 0
            for ci in range(cfg.KVCH + cfg.QCH):
                pr = ci % 2
                w_free = []
                if ci >= 2:
                    w_free = [(PE, proj_pe[(pair, ci - 2)])]
                elif last_pair is not None:
                    w_free = [(PE, proj_pe[(last_pair,
                                            cfg.KVCH + cfg.QCH - 2 + ci)])]
                if (pair, ci) not in ld2_marks:
                    emit_h1load(pair, ci, w_free)
                ldh_w = ld2_marks[(pair, ci)]
                is_kv = ci < cfg.KVCH

                def proj(lhs_of, rhs_of, out_of, n_acc, extra_w, out_w):
                    nonlocal pi
                    par = pi % 2
                    w0 = list(extra_w)
                    if pi >= 2:
                        w0.append((SE, evac_se[pi - 2]))
                    else:
                        w0.append((VE, max(aout_ve.values(), default=0)))
                    for a in range(n_acc):
                        p.emit("tensor",
                               lambda t, a=a, par=par, lhs_of=lhs_of,
                               rhs_of=rhs_of, out_w=out_w, n_acc=n_acc:
                               t.matmul(ps_p[par][:, 0:out_w],
                                        lhs_of(a), rhs_of(a),
                                        start=(a == 0), stop=(a == n_acc - 1)),
                               wait=(w0 if a == 0 else []),
                               inc=([(PE, 1)] if a == n_acc - 1 else []))
                    p.emit("scalar",
                           lambda s, par=par, out_of=out_of, out_w=out_w:
                           s.copy(out_of(), ps_p[par][:, 0:out_w]),
                           wait=[PE, (SE, None)], inc=[(SE, 1)])
                    evac_se[pi] = SE.n
                    pi += 1

                if is_kv:
                    for hh in range(2):
                        proj(lambda a, hh=hh, h0=h0: wkrot[:, h0 + hh, a, :],
                             lambda a, pr=pr: h1c[pr][:, a, :],
                             lambda hh=hh, ci=ci:
                             Ksb[:, hh, ci * 512:(ci + 1) * 512],
                             DC, [ldh_w, (SE, P0["rot_se"])], 512)
                    for ts_ in range(4):
                        proj(lambda a, pr=pr, ts_=ts_:
                             h1c[pr][:, a, ts_ * 128:(ts_ + 1) * 128],
                             lambda a, h0=h0:
                             wvb[:, a, h0 * 128:(h0 + 2) * 128],
                             lambda ci=ci, ts_=ts_:
                             Vsb[:, ci * 4 + ts_, :, :].rearrange(
                                 "p a b -> p (a b)"),
                             DC, [ldh_w, (LDC, P0["wv_ld"])], 256)
                else:
                    qi = ci - cfg.KVCH
                    for hh in range(2):
                        proj(lambda a, hh=hh, h0=h0: wqrot[:, h0 + hh, a, :],
                             lambda a, pr=pr: h1c[pr][:, a, :],
                             lambda hh=hh, qi=qi:
                             Qsb[:, hh, qi * 512:(qi + 1) * 512],
                             DC, [ldh_w, (SE, P0["rot_se"])], 512)
                proj_pe[(pair, ci)] = PE.n
            tailA = SE.n
            if pair == 0:
                emit_precast([(SE, tailA)])
            nxt = pair + 1
            if nxt < cfg.NPAIR:
                for cj in range(2):
                    emit_h1load(nxt, cj,
                                [(PE, proj_pe[(pair,
                                               cfg.KVCH + cfg.QCH - 2 + cj)])])

            for hh in range(2):
                h = h0 + hh
                for g in range(cfg.NG):
                    ktmax = 8 * g + 8
                    qlo = 4 * g * 128
                    gpar = gctr % 2
                    ps_o_g = ps_p[gpar]
                    ps_l_g = ps_l[gpar]
                    duos = [(k, min(k + 2, ktmax)) for k in range(0, ktmax, 2)]
                    nd = len(duos)
                    qk_pe, exp_se, msk_ve, av_pe = {}, {}, {}, {}
                    pe0 = prev_exp.copy()

                    def emit_qk(di):
                        k0, k1 = duos[di]
                        par = di % 2
                        for kt in range(k0, k1):
                            act = _active(cfg, g, kt)
                            j0 = 4 - act
                            col = (kt - k0) * 512
                            w = []
                            if kt == k0:
                                w = [(SE, tailA)]
                                if di >= 2:
                                    w.append((SE, exp_se[di - 2]))
                                elif (di % 2) in pe0:
                                    w.append((SE, pe0[di % 2]))
                                else:
                                    w.append((VE, pt_init_ve))
                            p.emit("tensor",
                                   lambda t, kt=kt, j0=j0, col=col, par=par,
                                   hh=hh, qlo=qlo, g=g:
                                   t.matmul(
                                       ps_t[par][:, col + j0 * 128:col + 512],
                                       Ksb[:, hh, kt * 128:(kt + 1) * 128],
                                       Qsb[:, hh, qlo + j0 * 128:qlo + 512],
                                       start=True, stop=True),
                                   wait=w,
                                   inc=([(PE, 1)] if kt == k1 - 1 else []))
                        qk_pe[di] = PE.n

                    def emit_exp(di):
                        k0, k1 = duos[di]
                        par = di % 2
                        wid = (k1 - k0) * 512
                        w = [(PE, qk_pe[di]), (SE, None)]
                        if di >= 2:
                            w.append((PE, av_pe[di - 2]))
                            if (di - 2) in msk_ve:
                                w.append((VE, msk_ve[di - 2]))
                        p.emit("scalar", lambda s, par=par, wid=wid:
                               s.activation(expb[par][:, 0:wid],
                                            ps_t[par][:, 0:wid], AF.Exp),
                               wait=w, inc=[(SE, 1)])
                        exp_se[di] = SE.n
                        mm = False
                        for kt in range(k0, k1):
                            if kt >= 8 * g:
                                jstar = (kt - 8 * g) // 2
                                mA = (kt - 8 * g) % 2 == 0
                                col = (kt - k0) * 512 + jstar * 128
                                mo = 0 if mA else 128
                                p.emit("vector",
                                       lambda v, par=par, col=col, mo=mo:
                                       v.tensor_tensor(
                                           expb[par][:, col:col + 128],
                                           expb[par][:, col:col + 128],
                                           maskb[:, mo:mo + 128], ALU.mult),
                                       wait=[(SE, exp_se[di]),
                                             (LDC, P0["mask_ld"]),
                                             (VE, None)],
                                       inc=[(VE, 1)])
                                mm = True
                        if mm:
                            msk_ve[di] = VE.n

                    def emit_av(di):
                        k0, k1 = duos[di]
                        par = di % 2
                        first = True
                        for kt in range(k0, k1):
                            act = _active(cfg, g, kt)
                            j0 = 4 - act
                            col = (kt - k0) * 512
                            w = []
                            if first:
                                w = [(SE, exp_se[di])]
                                if di in msk_ve:
                                    w.append((VE, msk_ve[di]))
                                if kt == 0:
                                    w.append((SE, ln_se.get(gpar, 0)))
                                    w.append((VE, aout_ve.get(gpar, 0)))
                            p.emit("tensor",
                                   lambda t, kt=kt, j0=j0, col=col, par=par,
                                   hh=hh, ktmax=ktmax, ps_o_g=ps_o_g:
                                   t.matmul(
                                       ps_o_g[:, j0 * 128:512],
                                       Vsb[:, kt, hh, :],
                                       expb[par][:, col + j0 * 128:col + 512],
                                       start=(kt == 0),
                                       stop=(kt == ktmax - 1)),
                                   wait=w)
                            p.emit("tensor",
                                   lambda t, kt=kt, j0=j0, col=col, par=par,
                                   ktmax=ktmax, ps_l_g=ps_l_g:
                                   t.matmul(
                                       ps_l_g[:, j0 * 128:512],
                                       ones128[:],
                                       expb[par][:, col + j0 * 128:col + 512],
                                       start=(kt == 0),
                                       stop=(kt == ktmax - 1)),
                                   inc=([(PE, 1)] if kt == k1 - 1 else []))
                            first = False
                        av_pe[di] = PE.n

                    for di in range(nd):
                        emit_qk(di)
                        emit_exp(di)
                        if di >= 1:
                            emit_av(di - 1)
                    emit_av(nd - 1)

                    p.emit("scalar", lambda s, ps_l_g=ps_l_g, gpar=gpar:
                           s.activation(recb[gpar][:], ps_l_g[:], AF.Ln),
                           wait=[(PE, av_pe[nd - 1]), (SE, None),
                                 (VE, aout_ve.get(gpar, 0))],
                           inc=[(SE, 1)])
                    ln_se[gpar] = SE.n
                    p.emit("scalar", lambda s, gpar=gpar:
                           s.activation(recb[gpar][:], recb[gpar][:],
                                        AF.Exp, scale=negone[:]),
                           wait=[(SE, None)], inc=[(SE, 1)])
                    rec_se = SE.n
                    gp = gctr % 2
                    aw = [ag_st[gctr - 2]] if gctr >= 2 else []
                    p.emit("vector", lambda v, gp=gp, gpar=gpar,
                           ps_o_g=ps_o_g:
                           v.tensor_tensor(
                               aoutg[gp][:], ps_o_g[:], recb[gpar][:],
                               ALU.mult),
                           wait=[(SE, rec_se), (VE, None)] + aw,
                           inc=[(VE, 1)])
                    aout_ve[gpar] = VE.n
                    groupend_ve = VE.n
                    prev_exp = {}
                    for dd in (nd - 2, nd - 1):
                        if dd >= 0:
                            prev_exp[dd % 2] = exp_se[dd]
                    p.emit("sync", lambda g_, h=h, g=g, gp=gp: g_.dma_start(
                        out=fmr(atT)[:, h, g * 512:(g + 1) * 512],
                        in_=aoutg[gp][:]),
                        wait=[(VE, groupend_ve)] + aw,
                        inc=[(STA[gp], 16)])
                    ag_st[gctr] = (STA[gp], STA[gp].n)
                    gctr += 1
            last_pair = pair
        es.close()
        return aout_st

    aout_st = p2()
    ctx_w2.close()
    p.fence([(PE, None), (VE, None), (SE, None), (GP, None),
             LH1[0], LH1[1], STA[0], STA[1], LGR])

    def p3():
        es = ExitStack()
        sbl = lambda n, s_, d=F32: es.enter_context(nc.sbuf_tensor(n, s_, d))
        psl = lambda n, s_, d=F32: es.enter_context(nc.psum_tensor(n, s_, d))
        xq = [sbl(f"o_xq{i}", [128, DC, 512]) for i in range(2)]
        ats = [sbl(f"o_at{i}", [128, H, 512], BF16) for i in range(2)]
        x1c = [sbl(f"o_x1{i}", [128, DC, 512]) for i in range(2)]
        x1b = sbl("o_x1b", [128, DC, 512], BF16)
        x1q = sbl("o_x1q", [128, DC, 512], BF16)
        h2c = [sbl(f"o_h2{i}", [128, DC, 512], BF16) for i in range(2)]
        rf3 = sbl("o_rf", [128, 3, 512])
        rb3 = sbl("o_rb", [128, 2, 512], BF16)
        t1b3 = sbl("o_t1b", [128, DC, 512], BF16)
        ps_o = [psl(f"o_ps_o{i}", [128, 512]) for i in range(2)]
        ps_s = [psl(f"o_ps_s{i}", [128, 512]) for i in range(2)]
        ps_q = [psl(f"o_ps_q{i}", [128, 512]) for i in range(2)]
        ps_lg = psl("o_ps_lg", [cfg.NRE, 512])

        atf = fmr(atT)
        xf = fmr(xT)
        dve_done, sth2_done, stx1_done, lg_se = {}, {}, {}, {}
        x1_ve = {}
        gps3_done = {}
        aff3_done = {}
        ld3_marks = {}
        pe_stats_prev = 0

        def emit_loads3(ci):
            pr = ci % 2
            sl = slice(ci * 512, (ci + 1) * 512)
            w_free = [(VE, dve_done[ci - 2]), (GP, gps3_done[ci - 2])] \
                if ci >= 2 else []
            p.emit("sync", lambda g, ci=ci, pr=pr: g.dma_start(
                out=xq[pr][:], in_=xf[:, :, cfg.TKV + ci * 512:
                                      cfg.TKV + (ci + 1) * 512]),
                wait=w_free, inc=[(LX[pr], 16)])
            p.emit("sync", lambda g, ci=ci, pr=pr, sl=sl: g.dma_start(
                out=ats[pr][:], in_=atf[:, :, sl]),
                wait=w_free + [STA[0], STA[1]],
                inc=[(LAT[pr], 16)])
            ld3_marks[ci] = ((LX[pr], LX[pr].n), (LAT[pr], LAT[pr].n))

        emit_loads3(0)
        if cfg.NTC3 > 1:
            emit_loads3(1)
        for ci in range(cfg.NTC3):
            pr = ci % 2
            sl = slice(ci * 512, (ci + 1) * 512)
            ldx_w, ldh_w = ld3_marks[ci]
            oproj_ve = {}
            for fc in range(DC):
                par2 = fc % 2
                w0 = [ldh_w, (LDC, P0["wout_ld"])]
                if fc >= 2:
                    w0.append((VE, oproj_ve[fc - 2]))
                elif ci >= 1:
                    w0.append((VE, x1_ve[ci - 1]))
                if fc == 0 and ci >= 2:
                    w0.append(stx1_done[ci - 2])
                for h in range(H):
                    p.emit("tensor", lambda t, h=h, fc=fc, pr=pr, par2=par2:
                           t.matmul(ps_o[par2][:],
                                    woutb[:, h, fc * 128:(fc + 1) * 128],
                                    ats[pr][:, h, :],
                                    start=(h == 0), stop=(h == H - 1)),
                           wait=(w0 if h == 0 else []),
                           inc=([(PE, 1)] if h == H - 1 else []))
                pe_o = PE.n
                p.emit("vector", lambda v, fc=fc, pr=pr, par2=par2:
                       v.scalar_tensor_tensor(
                           x1c[pr][:, fc, :], ps_o[par2][:],
                           modfm[:, 2, fc:fc + 1], xq[pr][:, fc, :],
                           ALU.mult, ALU.add),
                       wait=[(PE, pe_o), ldx_w, (VE, None),
                             (VE, P0["mod_ve"])],
                       inc=[(VE, 1)])
                oproj_ve[fc] = VE.n
            x1_ve[ci] = VE.n
            xw = [(PE, pe_stats_prev)] if ci >= 1 else []
            if ci >= 1:
                xw.append((VE, dve_done[ci - 1]))
            p.emit("scalar", lambda s, pr=pr: s.copy(x1b[:], x1c[pr][:]),
                   wait=[(VE, x1_ve[ci]), (SE, None)] + xw, inc=[(SE, 1)])
            p.emit("scalar", lambda s: s.activation(
                x1q[:], x1b[:], AF.Square), wait=[(SE, None)], inc=[(SE, 1)])
            sq_se = SE.n
            p.emit("sync", lambda g, pr=pr, sl=sl: g.dma_start(
                out=fmr(x1T)[:, :, sl], in_=x1c[pr][:]),
                wait=[(VE, x1_ve[ci])] +
                     ([stx1_done[ci - 2]] if ci >= 2 else []),
                inc=[(STX1[pr], 16)])
            stx1_done[ci] = (STX1[pr], STX1[pr].n)
            w_st = [(VE, dve_done[ci - 2]), (GP, gps3_done[ci - 2])] \
                if ci >= 2 else []
            for dc in range(DC):
                p.emit("tensor", lambda t, dc=dc, pr=pr: t.matmul(
                    ps_s[pr][:], ones128[:], x1b[:, dc, :],
                    start=(dc == 0), stop=(dc == DC - 1)),
                    wait=([(SE, sq_se)] + w_st if dc == 0 else []))
            for dc in range(DC):
                p.emit("tensor", lambda t, dc=dc, pr=pr: t.matmul(
                    ps_q[pr][:], ones128[:], x1q[:, dc, :],
                    start=(dc == 0), stop=(dc == DC - 1)),
                    inc=([(PE, 1)] if dc == DC - 1 else []))
            pe_stats = PE.n
            pe_stats_prev = pe_stats
            r, rb_ = rf3, rb3
            p.emit("vector", lambda v, r=r, pr=pr: v.tensor_scalar(
                r[:, 0, :], ps_s[pr][:], 1.0 / D, None, ALU.mult),
                wait=[(PE, pe_stats), (VE, None)], inc=[(VE, 1)])
            p.emit("vector", lambda v, r=r, pr=pr: v.tensor_scalar(
                r[:, 1, :], ps_q[pr][:], 1.0 / D, None, ALU.mult),
                wait=[(VE, None)], inc=[(VE, 1)])
            p.emit("vector", lambda v, r=r: v.tensor_tensor(
                r[:, 2, :], r[:, 0, :], r[:, 0, :], ALU.mult),
                wait=[(VE, None)], inc=[(VE, 1)])
            p.emit("vector", lambda v, r=r: v.tensor_tensor(
                r[:, 2, :], r[:, 1, :], r[:, 2, :], ALU.subtract),
                wait=[(VE, None)], inc=[(VE, 1)])
            p.emit("scalar", lambda s, r=r: s.activation(
                r[:, 2, :], r[:, 2, :], AF.Ln, bias=epscol[:]),
                wait=[VE, (SE, None)], inc=[(SE, 1)])
            p.emit("scalar", lambda s, r=r, rb_=rb_: s.activation(
                rb_[:, 0, :], r[:, 2, :], AF.Exp, scale=neghalf[:]),
                wait=[(SE, None)], inc=[(SE, 1)])
            rstd_se = SE.n
            p.emit("vector", lambda v, r=r, rb_=rb_: v.scalar_tensor_tensor(
                rb_[:, 1, :], r[:, 0, :], -1.0, rb_[:, 0, :],
                ALU.mult, ALU.mult),
                wait=[(SE, rstd_se), (VE, None)], inc=[(VE, 1)])
            rowm = VE.n
            hw = [sth2_done[ci - 2]] if ci >= 2 else []
            p.emit("vector", lambda v, rb_=rb_: v.tensor_tensor(
                t1b3[:], x1b[:],
                rb_[:, 0, :].unsqueeze(1).to_broadcast([128, DC, 512]),
                ALU.mult),
                wait=[(VE, rowm), (VE, None)], inc=[(VE, 1)])
            p.emit("vector", lambda v, rb_=rb_: v.tensor_tensor(
                t1b3[:], t1b3[:],
                rb_[:, 1, :].unsqueeze(1).to_broadcast([128, DC, 512]),
                ALU.add),
                wait=[(VE, None)], inc=[(VE, 1)])
            for dc in range(DC):
                p.emit("vector", lambda v, dc=dc, pr=pr:
                       v.tensor_scalar(
                           h2c[pr][:, dc, :], t1b3[:, dc, :],
                           modfm[:, 4, dc:dc + 1],
                           modfm[:, 3, dc:dc + 1],
                           ALU.mult, ALU.add),
                       wait=[(VE, None)] + (hw if dc == 0 else []),
                       inc=[(VE, 1)])
            dve_done[ci] = VE.n
            gps3_done[ci] = GP.n
            aff3_done[ci] = VE.n
            if ci + 2 < cfg.NTC3:
                emit_loads3(ci + 2)
            p.emit("sync", lambda g, pr=pr, sl=sl: g.dma_start(
                out=fmr(h2T)[:, :, sl], in_=h2c[pr][:]),
                wait=[(VE, aff3_done[ci])] +
                     ([sth2_done[ci - 2]] if ci >= 2 else []),
                inc=[(STH2[pr], 16)])
            sth2_done[ci] = (STH2[pr], STH2[pr].n)
            w_lg = [(SE, lg_se[ci - 1])] if ci >= 1 else []
            for dc in range(DC):
                p.emit("tensor", lambda t, dc=dc, pr=pr: t.matmul(
                    ps_lg[:], wrtb[:, dc, :], h2c[pr][:, dc, :],
                    start=(dc == 0), stop=(dc == DC - 1)),
                    wait=([(VE, aff3_done[ci]),
                           (LDC, P0["wrt_ld"])] + w_lg
                          if dc == 0 else []),
                    inc=([(PE, 1)] if dc == DC - 1 else []))
            p.emit("scalar", lambda s, sl=sl: s.copy(logT[:, sl], ps_lg[:]),
                   wait=[PE, (SE, None)], inc=[(SE, 1)])
            lg_se[ci] = SE.n
        es.close()
        return sth2_done, stx1_done, lg_se[cfg.NTC3 - 1]

    ctx_r = ExitStack()
    logT = ctx_r.enter_context(nc.sbuf_tensor("logT", [cfg.NRE, cfg.NQ], F32))
    sth2_done, stx1_done, lg_last = p3()
    ctx_w3.close()
    p.fence([(PE, None), (VE, None), (SE, None), (GP, None),
             sth2_done[cfg.NTC3 - 1], stx1_done[cfg.NTC3 - 1],
             LX[0], LX[1], LAT[0], LAT[1]] +
            ([sth2_done[cfg.NTC3 - 2], stx1_done[cfg.NTC3 - 2]]
             if cfg.NTC3 >= 2 else []))

    def router():
        es = ExitStack()
        sbl = lambda n, s_, d=F32: es.enter_context(nc.sbuf_tensor(n, s_, d))
        psl = lambda n, s_, d=F32: es.enter_context(nc.psum_tensor(n, s_, d))
        NT = cfg.NQ // 128
        NR = cfg.NRE
        gate_fm = sbl("r_gate_fm", [NR, cfg.NQ], BF16)
        lt = sbl("r_lt", [128, NT, NR])
        pex = sbl("r_pe", [128, NT, NR])
        red = sbl("r_red", [128, NT])
        m8 = sbl("r_m8", [128, 8])
        zap = sbl("r_zap", [128, NR])
        gtm = sbl("r_gtm", [128, NT, NR])
        ps_a = psl("r_ps_a", [128, 512])

        tr_se = []
        per_bank_lg = 512 // NR
        for t0 in range(0, NT, per_bank_lg):
            tiles = list(range(t0, min(t0 + per_bank_lg, NT)))
            w0 = [(SE, lg_last), (LDW_, None)]
            if tr_se:
                w0.append((SE, tr_se[-1]))
            for i, tt in enumerate(tiles):
                p.emit("tensor", lambda t, tt=tt, i=i: t.transpose(
                    ps_a[:, i * NR:(i + 1) * NR],
                    logT[:, tt * 128:(tt + 1) * 128],
                    ident[0:NR, 0:NR]),
                    wait=(w0 if i == 0 else []),
                    inc=([(PE, 1)] if i == len(tiles) - 1 else []))
            p.emit("scalar", lambda s, t0=t0, n=len(tiles): s.copy(
                lt[:, t0:t0 + n, :].rearrange("p a b -> p (a b)"),
                ps_a[:, 0:n * NR]), wait=[PE, (SE, None)], inc=[(SE, 1)])
            tr_se.append(SE.n)
        p.emit("scalar", lambda s: s.activation(
            pex[:].rearrange("p a b -> p (a b)"),
            lt[:].rearrange("p a b -> p (a b)"), AF.Exp),
            wait=[(SE, tr_se[-1])], inc=[(SE, 1)])
        exp_se = SE.n
        p.emit("vector", lambda v: v.reduce_sum(red[:], pex[:], axis=AX),
               wait=[(SE, exp_se), (VE, None)], inc=[(VE, 1)])
        p.emit("vector", lambda v: v.reciprocal(red[:], red[:]),
               wait=[(VE, None)], inc=[(VE, 1)])
        p.emit("vector", lambda v: v.tensor_tensor(
            pex[:], pex[:],
            red[:].unsqueeze(2).to_broadcast([128, NT, NR]), ALU.mult),
            wait=[(VE, None)], inc=[(VE, 1)])
        for tt in range(NT):
            p.emit("vector", lambda v, tt=tt: v.max(
                out=m8[:], in_=pex[:, tt, :]),
                wait=[(VE, None)], inc=[(VE, 1)])
            if cfg.TOPK < 8:
                p.emit("vector", lambda v: v.memset(
                    m8[:, cfg.TOPK:8], -1e30),
                    wait=[(VE, None)], inc=[(VE, 1)])
            p.emit("vector", lambda v, tt=tt: v.match_replace(
                out=zap[:], in_to_replace=m8[:],
                in_values=pex[:, tt, :], imm_value=-1.0),
                wait=[(VE, None)], inc=[(VE, 1)])
            p.emit("vector", lambda v, tt=tt: v.tensor_tensor(
                zap[:], zap[:], pex[:, tt, :], ALU.not_equal),
                wait=[(VE, None)], inc=[(VE, 1)])
            p.emit("vector", lambda v, tt=tt: v.tensor_tensor(
                gtm[:, tt, :], pex[:, tt, :], zap[:], ALU.mult),
                wait=[(VE, None)], inc=[(VE, 1)])
        gtm_ve = VE.n
        gt_se = []
        for t0 in range(0, NT, 4):
            tiles = list(range(t0, min(t0 + 4, NT)))
            w0 = [(VE, gtm_ve), (SE, tr_se[-1])]
            if gt_se:
                w0.append((SE, gt_se[-1]))
            for i, tt in enumerate(tiles):
                p.emit("tensor", lambda t, tt=tt, i=i: t.transpose(
                    ps_a[0:NR, i * 128:(i + 1) * 128],
                    gtm[:, tt, :], ident[:]),
                    wait=(w0 if i == 0 else []),
                    inc=([(PE, 1)] if i == len(tiles) - 1 else []))
            p.emit("scalar", lambda s, t0=t0, n=len(tiles): s.copy(
                gate_fm[:, t0 * 128:(t0 + n) * 128],
                ps_a[0:NR, 0:n * 128]), wait=[PE, (SE, None)], inc=[(SE, 1)])
            gt_se.append(SE.n)
        p.emit("sync", lambda g: g.dma_start(
            out=gate_hbm[:], in_=gate_fm[:]),
            wait=[(SE, gt_se[-1])], inc=[(LGR, 16)])
        gr_ld = (LGR, LGR.n)
        es.close()
        return gr_ld

    gate_ld = router()
    ctx_r.close()
    p.fence([(PE, None), (VE, None), (SE, None), (GP, None), LGR])

    def p4():
        es = ExitStack()
        sbl = lambda n, s_, d=F32: es.enter_context(nc.sbuf_tensor(n, s_, d))
        psl = lambda n, s_, d=F32: es.enter_context(nc.psum_tensor(n, s_, d))
        NP = len(cfg.passes)
        G = max(len(q) for q in cfg.passes)
        NSLOT = 2 * G - 1
        NTCH = cfg.NQ // 512
        DC2 = DC // 2
        wu = [sbl(f"f_wu{i}", [128, DC, E], BF16) for i in range(NSLOT)]
        wd = [sbl(f"f_wd{i}", [128, ECN, D], BF16) for i in range(NSLOT)]
        h2cb = [sbl(f"f_h2{i}", [128, DC, 512], BF16) for i in range(2)]
        x1cb = [sbl(f"f_x1{i}", [128, DC2, 512]) for i in range(2)]
        gbs = sbl("f_gb", [128, G, 512], BF16)
        grow = sbl("f_grow", [1, G * 512], BF16)
        actb = [sbl(f"f_act{i}", [128, ECN, 512], BF16) for i in range(G)]
        accin = [sbl(f"f_ai{i}", [128, DC2, 512], BF16) for i in range(2)]
        accout = [sbl(f"f_ao{i}", [128, DC2, 512], BF16) for i in range(2)]
        outc = [sbl(f"f_out{i}", [128, DC2, 512]) for i in range(2)]
        ps_dn = psl("f_ps_dn", [128, DC2 * 512])
        ps_up = psl("f_ps_up", [128, ECN * 512])

        wruf = wub_h.rearrange("(r c p) e -> r p c e", p=128, c=DC)
        wrdf = wdb_h.rearrange("(r c p) d -> r p c d", p=128, c=ECN)
        h2f = fmr(h2T)
        x1f = fmr(x1T)
        af_ = fmr(accT)
        of = fmr(outT)

        wload, grow_ld, ac_st, out_st = {}, {}, {}, {}
        gb_pe, gm_ve = {}, {}
        chunk_pe, pass_pe = {}, {}
        evac_mark = {}
        comb_done = {}
        dnh1_pe = {}

        slot_of, prev_of, slot_hist = {}, {}, {}
        slot_ctr = [0]
        deferred = {}

        def assign_slots(pi):
            for i in range(len(cfg.passes[pi])):
                s = slot_ctr[0]
                slot_ctr[0] = (s + 1) % NSLOT
                slot_of[(pi, i)] = s
                prev_of[(pi, i)] = slot_hist.get(s)
                slot_hist[s] = pi

        def emit_one_wload(pi, i):
            sl_ = slot_of[(pi, i)]
            pp = prev_of[(pi, i)]
            w_free = [(WCPA, None)] if pi < 2 else [(WCPB, None)]
            if pp is not None:
                w_free.append((PE, pass_pe[pp]))
            eid = cfg.passes[pi][i]
            p.emit("gpsimd", lambda g, sl_=sl_, s_=wruf[eid]: g.dma_start(
                out=wu[sl_][:], in_=s_), wait=w_free, inc=[(LFW[sl_], 16)])
            p.emit("gpsimd", lambda g, sl_=sl_, s_=wrdf[eid]: g.dma_start(
                out=wd[sl_][:], in_=s_), wait=[], inc=[(LFW[sl_], 16)])
            wload[(pi, i)] = (LFW[sl_], LFW[sl_].n)

        def emit_pass_loads(pi):
            for i in range(len(cfg.passes[pi])):
                pp = prev_of[(pi, i)]
                if pp is not None and pp not in pass_pe:
                    deferred.setdefault(pi, []).append(i)
                else:
                    emit_one_wload(pi, i)

        assign_slots(0)
        emit_pass_loads(0)
        if NP > 1:
            assign_slots(1)
            emit_pass_loads(1)

        for pi, plist in enumerate(cfg.passes):
            npl = len(plist)
            for i in deferred.pop(pi, []):
                emit_one_wload(pi, i)
            if pi >= 1 and pi + 1 <= NP - 1:
                assign_slots(pi + 1)
                emit_pass_loads(pi + 1)
            routed = [e for e in plist if e < cfg.NRE]
            nro = len(routed)
            r0 = routed[0] if nro else 0
            pending = None
            for ci in range(NTCH):
                pr = ci % 2
                sl = slice(ci * 512, (ci + 1) * 512)
                w_free = []
                if (pi, ci - 2) in chunk_pe:
                    w_free = [(PE, chunk_pe[(pi, ci - 2)])]
                elif pi > 0 and ci < 2:
                    w_free = [(PE, pass_pe[pi - 1])]
                p.emit("sync", lambda g, sl=sl, pr=pr: g.dma_start(
                    out=h2cb[pr][:], in_=h2f[:, :, sl]),
                    wait=w_free + [sth2_done[ci]], inc=[(LH2[pr], 16)])
                ldh_w = (LH2[pr], LH2[pr].n)
                ldx_w, ac_w = {}, {}

                def emit_half_loads(hf, ci=ci, sl=sl, ldx_w=ldx_w,
                                    ac_w=ac_w):
                    hidx = 2 * ci + hf
                    hb = hidx % 2
                    if pi == NP - 1:
                        xw = [stx1_done[ci]]
                        if (hidx - 2) in comb_done:
                            xw.append((VE, comb_done[hidx - 2]))
                        p.emit("sync", lambda g, sl=sl, hb=hb, hf=hf:
                               g.dma_start(
                                   out=x1cb[hb][:],
                                   in_=x1f[:, hf * DC2:(hf + 1) * DC2, sl]),
                               wait=xw, inc=[(LX1[hb], 16)])
                        ldx_w[hf] = (LX1[hb], LX1[hb].n)
                    if pi > 0:
                        aw_ = [ac_st[(pi - 1, hidx)]]
                        em = evac_mark.get((pi, hidx - 2),
                                           evac_mark.get(
                                               (pi - 1, 2 * NTCH - 2 + hidx)))
                        if em is not None:
                            aw_.append((SE if em[0] == "se" else VE, em[1]))
                        if pi == NP - 1 and (hidx - 2) in comb_done:
                            aw_.append((VE, comb_done[hidx - 2]))
                        p.emit("sync", lambda g, sl=sl, hb=hb, hf=hf:
                               g.dma_start(
                                   out=accin[hb][:],
                                   in_=af_[:, hf * DC2:(hf + 1) * DC2, sl]),
                               wait=aw_, inc=[(LAC[hb], 16)])
                        ac_w[hf] = (LAC[hb], LAC[hb].n)

                gb_bank_se = {}
                if nro:
                    gw = [gate_ld]
                    gl = grow_ld.get((pi, ci - 1),
                                     grow_ld.get((pi - 1, NTCH - 1)))
                    if gl is not None:
                        gw.append(gl)
                    gm = gb_pe.get((pi, ci - 1),
                                   gb_pe.get((pi - 1, NTCH - 1)))
                    if gm is not None:
                        gw.append((PE, gm))
                    p.emit("sync", lambda g, sl=sl, r0=r0, nro=nro:
                           g.dma_start(
                               out=grow[0:1, 0:nro * 512],
                               in_=gate_hbm[r0:r0 + nro, sl]),
                           wait=gw, inc=[(LGA[ci % 2], 16)])
                    grow_ld[(pi, ci)] = (LGA[ci % 2], LGA[ci % 2].n)
                emit_half_loads(0)
                if nro:
                    gmv = gm_ve.get((pi, ci - 1),
                                    gm_ve.get((pi - 1, NTCH - 1)))
                    first = True
                    for w0i in range(0, nro, ECN):
                        wave = list(range(w0i, min(w0i + ECN, nro)))
                        for k, j in enumerate(wave):
                            w_ = []
                            if first:
                                w_ = [grow_ld[(pi, ci)], (SE, None)]
                                if gmv is not None:
                                    w_.append((VE, gmv))
                                first = False
                            elif k == 0:
                                w_ = [(SE, None)]
                            p.emit("tensor",
                                   lambda t, k=k, r0=r0,
                                   eid=routed[j]:
                                   t.matmul(
                                       ps_up[:, k * 512:(k + 1) * 512],
                                       onesrow_bf[:],
                                       grow[0:1, (eid - r0) * 512:
                                            (eid - r0 + 1) * 512],
                                       start=True, stop=True),
                                   wait=w_, inc=[(PE, 1)])
                            p.emit("scalar", lambda s, w0i=w0i, k=k:
                                   s.copy(gbs[:, w0i + k, :],
                                          ps_up[:, k * 512:(k + 1) * 512]),
                                   wait=[PE, (SE, None)], inc=[(SE, 1)])
                            gb_bank_se[k] = SE.n
                    gb_pe[(pi, ci)] = PE.n

                up_pe, gelu_se, act_w, dnp = {}, {}, {}, {}

                def emit_up(i, ldh_w=ldh_w, pr=pr, gb_bank_se=gb_bank_se):
                    w0 = [wload[(pi, i)], ldh_w]
                    if i == 0:
                        if not gb_bank_se:
                            w0.append((SE, None))
                    else:
                        w0.append((SE, gelu_se[i - 1]))
                    for ec in range(ECN):
                        for dc in range(DC):
                            wec = []
                            if ec == 0 and dc == 0:
                                wec = list(w0)
                            if i == 0 and dc == 0 and ec in gb_bank_se:
                                wec.append((SE, gb_bank_se[ec]))
                            p.emit("tensor",
                                   lambda t, ec=ec, dc=dc, i=i, pr=pr,
                                   sl_=slot_of[(pi, i)]:
                                   t.matmul(
                                       ps_up[:, ec * 512:(ec + 1) * 512],
                                       wu[sl_][:, dc,
                                               ec * 128:(ec + 1) * 128],
                                       h2cb[pr][:, dc, :],
                                       start=(dc == 0), stop=(dc == DC - 1)),
                                   wait=wec,
                                   inc=([(PE, 1)]
                                        if ec == ECN - 1 and dc == DC - 1
                                        else []))
                    up_pe[i] = PE.n

                def emit_act(i, eid, ci=ci):
                    w = [(PE, up_pe[i]), (SE, None)]
                    prev = dnh1_pe.get((pi, ci - 1, i),
                                       dnh1_pe.get((pi - 1, NTCH - 1, i)))
                    if prev is not None:
                        w.append((PE, prev))
                    p.emit("scalar", lambda s, i=i: s.activation(
                        actb[i][:].rearrange("p a b -> p (a b)"),
                        ps_up[:], AF.Gelu),
                        wait=w, inc=[(SE, 1)])
                    gelu_se[i] = SE.n
                    if eid < cfg.NRE:
                        p.emit("vector", lambda v, i=i: v.tensor_tensor(
                            actb[i][:], actb[i][:],
                            gbs[:, i, :].unsqueeze(1).to_broadcast(
                                [128, ECN, 512]), ALU.mult),
                            wait=[(SE, gelu_se[i]), (VE, None)],
                            inc=[(VE, 1)])
                        act_w[i] = [(VE, VE.n)]
                        gm_ve[(pi, ci)] = VE.n
                    else:
                        act_w[i] = [(SE, gelu_se[i])]

                def emit_dn_half(hf, i, act_w=act_w, dnp=dnp, ci=ci,
                                 npl=npl):
                    for hcl in range(DC2):
                        hc = hf * DC2 + hcl
                        for ec in range(ECN):
                            w = []
                            if hcl == 0 and ec == 0:
                                w = list(act_w[i])
                                if i == 0:
                                    ph = 2 * ci + hf - 1
                                    em = evac_mark.get(
                                        (pi, ph),
                                        evac_mark.get((pi - 1,
                                                       2 * NTCH - 1)))
                                    if em is not None:
                                        w.append((SE if em[0] == "se"
                                                  else VE, em[1]))
                            mark = (hcl == DC2 - 1 and ec == ECN - 1
                                    and (hf == 1 or i == npl - 1))
                            p.emit("tensor",
                                   lambda t, hcl=hcl, hc=hc, ec=ec, i=i,
                                   npl=npl, sl_=slot_of[(pi, i)]:
                                   t.matmul(
                                       ps_dn[:, hcl * 512:(hcl + 1) * 512],
                                       wd[sl_][:, ec,
                                               hc * 128:(hc + 1) * 128],
                                       actb[i][:, ec, :],
                                       start=(i == 0 and ec == 0),
                                       stop=(i == npl - 1
                                             and ec == ECN - 1)),
                                   wait=w,
                                   inc=([(PE, 1)] if mark else []))
                            if mark:
                                if i == npl - 1:
                                    dnp[hf] = PE.n
                                if hf == 1:
                                    dnh1_pe[(pi, ci, i)] = PE.n

                def emit_evac(hf, dnp=dnp, ci=ci, ldx_w=ldx_w, ac_w=ac_w,
                              sl=sl):
                    hidx = 2 * ci + hf
                    hb = hidx % 2
                    stw = ac_st.get((pi, hidx - 2),
                                    ac_st.get((pi - 1, 2 * NTCH - 2 + hidx)))
                    if pi == 0:
                        w0 = [(PE, dnp[hf]), (SE, None)]
                        if stw is not None:
                            w0.append(stw)
                        p.emit("scalar", lambda s, hb=hb: s.copy(
                            accout[hb][:],
                            ps_dn[:].rearrange("p (a b) -> p a b", b=512)),
                            wait=w0, inc=[(SE, 1)])
                        evac_mark[(pi, hidx)] = ("se", SE.n)
                    elif pi < NP - 1:
                        w0 = [(PE, dnp[hf]), (VE, None), ac_w[hf]]
                        if stw is not None:
                            w0.append(stw)
                        p.emit("vector", lambda v, hb=hb: v.tensor_tensor(
                            accout[hb][:], accin[hb][:],
                            ps_dn[:].rearrange("p (a b) -> p a b", b=512),
                            ALU.add),
                            wait=w0, inc=[(VE, 1)])
                        evac_mark[(pi, hidx)] = ("ve", VE.n)
                    if pi < NP - 1:
                        ev = evac_mark[(pi, hidx)]
                        p.emit("gpsimd", lambda g, sl=sl, hb=hb, hf=hf:
                               g.dma_start(
                                   out=af_[:, hf * DC2:(hf + 1) * DC2, sl],
                                   in_=accout[hb][:]),
                               wait=[(SE if ev[0] == "se" else VE, ev[1])] +
                                    ([stw] if stw is not None else []),
                               inc=[(SAC[hb], 16)])
                        ac_st[(pi, hidx)] = (SAC[hb], SAC[hb].n)
                    else:
                        ow = out_st.get(hidx - 2)
                        for hcl in range(DC2):
                            hc = hf * DC2 + hcl
                            w0 = [(PE, dnp[hf]), (VE, None)]
                            if hcl == 0:
                                w0 += [ldx_w[hf], ac_w[hf],
                                       (VE, P0["mod_ve"])]
                                if ow is not None:
                                    w0.append(ow)
                            p.emit("vector",
                                   lambda v, hcl=hcl, hb=hb: v.tensor_tensor(
                                       outc[hb][:, hcl, :],
                                       accin[hb][:, hcl, :],
                                       ps_dn[:, hcl * 512:(hcl + 1) * 512],
                                       ALU.add),
                                   wait=w0, inc=[(VE, 1)])
                            p.emit("vector",
                                   lambda v, hcl=hcl, hc=hc, hb=hb:
                                   v.scalar_tensor_tensor(
                                       outc[hb][:, hcl, :],
                                       outc[hb][:, hcl, :],
                                       modfm[:, 5, hc:hc + 1],
                                       x1cb[hb][:, hcl, :],
                                       ALU.mult, ALU.add),
                                   wait=[(VE, None)], inc=[(VE, 1)])
                        comb_done[hidx] = VE.n
                        evac_mark[(pi, hidx)] = ("ve", VE.n)
                        p.emit("gpsimd", lambda g, sl=sl, hb=hb, hf=hf:
                               g.dma_start(
                                   out=of[:, hf * DC2:(hf + 1) * DC2, sl],
                                   in_=outc[hb][:]),
                               wait=[(VE, None)] +
                                    ([ow] if ow is not None else []),
                               inc=[(STO[hb], 16)])
                        out_st[hidx] = (STO[hb], STO[hb].n)

                emit_up(0)
                if pending is not None:
                    pending()
                    pending = None
                emit_half_loads(1)
                emit_act(0, plist[0])
                for i in range(1, npl):
                    emit_up(i)
                    emit_act(i, plist[i])
                    emit_dn_half(0, i - 1)
                emit_dn_half(0, npl - 1)
                emit_evac(0)

                def mk_pending(emit_dn_half=emit_dn_half,
                               emit_evac=emit_evac, npl=npl):
                    def run():
                        for i in range(npl):
                            emit_dn_half(1, i)
                        emit_evac(1)
                    return run

                pending = mk_pending()
                chunk_pe[(pi, ci)] = PE.n
            if pending is not None:
                pending()
                pending = None
            pass_pe[pi] = PE.n
        p.emit("sync", None, wait=[STO[0], STO[1]])
        es.close()

    p4()

    with ctx, nc.Block() as block:
        p.build(block)
    return nc


def _prep_consts(cfg, inputs):
    wqkv = np.asarray(inputs["L0_attn_w_qkv"], dtype=np.float32)
    D = cfg.D
    hhq, hhk = _householders(cfg)
    return {
        "wqkT": np.ascontiguousarray(wqkv[:, :2 * D].T),
        "wv": np.ascontiguousarray(wqkv[:, 2 * D:3 * D]),
        "wout": np.ascontiguousarray(
            np.asarray(inputs["L0_attn_w_out"], dtype=np.float32)),
        "hhq": np.ascontiguousarray(hhq.reshape(cfg.H * cfg.HD, cfg.HD)),
        "hhk": np.ascontiguousarray(hhk.reshape(cfg.H * cfg.HD, cfg.HD)),
        "hs": np.ascontiguousarray(
            np.asarray(inputs["L0_attn_head_scales"],
                       dtype=np.float32).reshape(1, cfg.H)),
        "wmod": np.ascontiguousarray(
            np.asarray(inputs["L0_wmod"], dtype=np.float32)),
        "wsu": np.ascontiguousarray(
            np.asarray(inputs["L0_ffn_shared_up"], dtype=np.float32)
            .transpose(0, 2, 1).reshape(cfg.NSE * D, cfg.E)),
        "wsd": np.ascontiguousarray(
            np.asarray(inputs["L0_ffn_shared_down"], dtype=np.float32)
            .transpose(0, 2, 1).reshape(cfg.NSE * cfg.E, D)),
        "wru": np.ascontiguousarray(
            np.asarray(inputs["L0_ffn_routed_up"], dtype=np.float32)
            .transpose(0, 2, 1).reshape(cfg.NRE * D, cfg.E)),
        "wrd": np.ascontiguousarray(
            np.asarray(inputs["L0_ffn_routed_down"], dtype=np.float32)
            .transpose(0, 2, 1).reshape(cfg.NRE * cfg.E, D)),
        "wrt": np.ascontiguousarray(
            np.asarray(inputs["L0_ffn_router_w"], dtype=np.float32).T),
        "identin": np.eye(128, dtype=np.float32),
    }


def _prep_core(cfg, inputs, core, consts):
    b, par = core // 2, core % 2
    x = np.asarray(inputs["L0_attn_x"], dtype=np.float32)[b]
    qr = _qrows(cfg, par)
    m = {
        "xT": np.ascontiguousarray(
            np.concatenate([x.T, x[qr].T], axis=1)),
        "cfm": np.ascontiguousarray(
            np.asarray(inputs["L0_c"], dtype=np.float32)[b]
            .reshape(cfg.DC, 128).T),
        "mask01": _mask01(par),
    }
    m.update(consts)
    return m


def kernel(**inputs):
    cfg = _Cfg()
    consts = _prep_consts(cfg, inputs)
    in_maps = [_prep_core(cfg, inputs, c, consts) for c in range(cfg.NCORES)]
    nc = _build(cfg)
    res = run_bass_kernel_spmd(nc, in_maps, core_ids=list(range(cfg.NCORES)))
    out = np.empty((cfg.NB, cfg.TKV, cfg.D), dtype=np.float32)
    for c in range(cfg.NCORES):
        b, par = c // 2, c % 2
        out[b, _qrows(cfg, par)] = np.asarray(res.results[c]["outT"]).T
    return out
